# version 09
# baseline (speedup 1.0000x reference)
import sys
sys.path.insert(0, "/opt/trn_rl_repo")
import contextlib
import numpy as np
import ml_dtypes
import concourse.bass as bass
import concourse.mybir as mybir
from concourse.bass_utils import run_bass_kernel_spmd

F32 = mybir.dt.float32
BF16 = mybir.dt.bfloat16
AF = mybir.ActivationFunctionType
ALU = mybir.AluOpType
AX = mybir.AxisListType

D = 1024
NH = 16
HD = 64
DFF0 = 4096
DFF1 = 2816
NC8 = D // 128
NI = DFF1 // 128
EXP_C = 16.0
NEG = -1e10


def default_cfg():
    return dict(S=2048, nseq=4, layers=(0, 1), dumps=(), skip_attn=False, skip_ffn=False)


class Ctr:
    def __init__(self, es, nc, name):
        self.sem = es.enter_context(nc.semaphore(name))
        self.n = 0

    def inc(self, ins, k=1):
        ins.then_inc(self.sem, k)
        self.n += k


class FakeIns:
    __slots__ = ("incs",)

    def __init__(self):
        self.incs = []

    def then_inc(self, sem, v):
        self.incs.append((sem, v))
        return self


class Rec:

    def __init__(self, stream):
        self._stream = stream

    def __getattr__(self, name):
        st = self._stream

        def call(*a, **k):
            fi = FakeIns()
            st.append((name, a, k, fi))
            return fi
        return call


class Prog:
    def __init__(self, cfg):
        self.cfg = cfg
        self.S = cfg["S"]
        self.NQT = self.S // 512
        self.NKC = self.S // 128
        self.nc = bass.Bass("TRN2", target_bir_lowering=False)
        self.es = contextlib.ExitStack()
        self.dump_names = []

    def sb(self, name, shape, dt):
        return self.es.enter_context(self.nc.sbuf_tensor(name, shape, dt))

    def bank(self, i, lo=0, hi=512):
        return self.psum[:, i * 512 + lo:i * 512 + hi]

    def build(self):
        cfg, S, nc = self.cfg, self.S, self.nc
        di = lambda n, s, dt=F32: nc.dram_tensor(n, list(s), dt, kind="ExternalInput").ap()
        dt_ = lambda n, s, dt: nc.dram_tensor(n, list(s), dt, kind="Internal").ap()
        self.x_in = di("x_t", (cfg["nseq"], D, S))
        self.out = nc.dram_tensor("out_t", [cfg["nseq"], D, S], F32, kind="ExternalOutput").ap()
        self.msk = di("msk", (4, 128, 512), BF16)
        self.wqT = di("L0_wqT", (D, D))
        self.wkT = di("L0_wkT", (D, D))
        self.wvT = di("L0_wvT", (D, D))
        self.woT = di("L0_woT", (D, D))
        self.w1n = di("L0_w1", (DFF0, D))
        self.w2n = di("L0_w2", (D, DFF0))
        self.b1h = di("L0_b1h", (128, DFF0 // 128))
        self.b2h = di("L0_b2h", (128, NC8))
        self.wqkv = di("L1_wqkv", (NH, D, 192))
        self.wout = di("L1_wout", (NH, HD, D))
        self.wg = di("L1_wg", (DFF1, D))
        self.wu = di("L1_wu", (DFF1, D))
        self.wd = di("L1_wd", (D, DFF1))

        self.v_scr = dt_("v_scr", (S, D), BF16)
        self.t_scr = dt_("t_scr", (NH, S, HD), BF16)
        self.bw1T_scr = dt_("bw1T_scr", (D, DFF0), BF16)
        self.bw2T_scr = dt_("bw2T_scr", (DFF0, D), BF16)
        self.wgT_scr = dt_("wgT_scr", (D, DFF1), BF16)
        self.wuT_scr = dt_("wuT_scr", (D, DFF1), BF16)
        self.wdT_scr = dt_("wdT_scr", (DFF1, D), BF16)
        self.row_scr = dt_("row_scr", (1024,), F32)

        self.x = self.sb("x", [128, NC8, S], F32)
        self.xn = self.sb("xn", [128, NC8, S], BF16)
        self.arena = self.sb("arena", [128, 20480], BF16)
        self.kpair = self.sb("kpair", [128, S], BF16)
        self.qsmall = self.sb("qsmall", [128, 512], BF16)
        self.vpair = self.sb("vpair", [128, self.NKC, 132], BF16)
        self.wbufA = self.sb("wbufA", [128, NC8, 1024], BF16)
        self.wbufB = self.sb("wbufB", [128, 32, 128], BF16)
        self.mskt = self.sb("mskt", [128, 4, 512], BF16)
        self.consts = self.sb("consts", [128, 4], F32)
        self.ones1 = self.sb("ones1", [128, 1], BF16)
        self.onesK = self.sb("onesK", [1, 128], BF16)
        self.eA = self.sb("eA", [1, 128], BF16)
        self.eB = self.sb("eB", [1, 128], BF16)
        self.rowb = self.sb("rowb", [1, 2 * max(S, 2048)], BF16)
        self.st_sb = self.sb("st_sb", [1, 2 * S], BF16)
        self.dAB = self.sb("dAB", [1, 1024], BF16)
        self.wsum_c = self.sb("wsum_c", [128, NC8], F32)
        self.scrA = self.sb("scrA", [128, 512], F32)
        self.scrB = self.sb("scrB", [128, 512], F32)
        self.qsc = self.sb("qsc", [128, 4, 1024], BF16)
        self.qsf = self.sb("qsf", [128, 4, 64], F32)
        self.h_stage = self.sb("h_stage", [128, 512], BF16)
        self.tstage = self.sb("tstage", [128, 2, 64], BF16)
        self.b1_sb = self.sb("b1_sb", [128, 32], F32)
        self.b2_sb = self.sb("b2_sb", [128, NC8], F32)
        self.psum = self.es.enter_context(self.nc.psum_tensor("psum", [128, 4096], F32))

        A = self.arena
        self.attn_fm = A[:, 0:NC8 * S].rearrange("p (c q) -> p c q", c=NC8)
        off = NC8 * S
        self.ptA_g = A[:, off:off + 2048]
        self.ptB_g = A[:, off + 2048:off + 4096]
        self.ptA_f = A[:, 0:self.NKC * 512].rearrange("p (k r) -> p k r", r=512)
        self.ptB_f = A[:, self.NKC * 512:2 * self.NKC * 512].rearrange("p (k r) -> p k r", r=512)
        self.h_l0 = A[:, 0:32 * 512].rearrange("p (k r) -> p k r", r=512)
        self.xq = A[:, 0:NC8 * 512].rearrange("p (k r) -> p k r", r=512)
        self.hq = A[:, NC8 * 512:(NC8 + NI) * 512].rearrange("p (k r) -> p k r", r=512)
        self.prep_bf = A[:, 0:4096]
        self.y_sb = self.qsc[:, 0:2, :].rearrange("p a b -> p (a b)")

        es = self.es
        nc_ = self.nc
        self.cT = Ctr(es, nc_, "cT")
        self.cV = Ctr(es, nc_, "cV")
        self.cA = Ctr(es, nc_, "cA")
        self.cG = Ctr(es, nc_, "cG")
        self.cS = Ctr(es, nc_, "cS")
        self.cW = [Ctr(es, nc_, "cW0"), Ctr(es, nc_, "cW1")]
        self.cWG = [Ctr(es, nc_, "cWG0"), Ctr(es, nc_, "cWG1")]
        self.all_ctrs = [self.cT, self.cV, self.cA, self.cG, self.cS] + self.cW + self.cWG

        self.streams = {k: [] for k in ("tensor", "vector", "scalar", "gpsimd", "sync")}
        self.emit_all()
        with es:
            with nc.Block() as blk:
                for kind in ("tensor", "vector", "scalar", "gpsimd", "sync"):
                    getattr(blk, kind)(self._replayer(kind))
        return nc

    def _replayer(self, kind):
        stream = self.streams[kind]

        def run(eng):
            seen = {}
            for name, a, k, fi in stream:
                if name == "wait_ge":
                    sem, v = a
                    key = id(sem)
                    if seen.get(key, -1) >= v:
                        continue
                    seen[key] = v
                    eng.wait_ge(sem, v)
                    continue
                ins = getattr(eng, name)(*a, **k)
                for sem, v in fi.incs:
                    ins.then_inc(sem, v)
        return run

    def snap(self):
        return [(c, c.n) for c in self.all_ctrs]

    def body(self, kind, fn, dma_inc=None, waits=None):
        ctr = {"tensor": self.cT, "vector": self.cV, "scalar": self.cA,
               "gpsimd": self.cG, "sync": self.cS}[kind]
        is_dma = kind in ("gpsimd", "sync")
        if waits is None:
            waits = self.snap()
        waits = [(c, n) for (c, n) in waits if n > 0]
        eng = Rec(self.streams[kind])
        for c, n in waits:
            eng.wait_ge(c.sem, n)
        last = fn(eng)
        if is_dma:
            assert last is None, "DMA bodies must inc per-DMA via dS/dG"
        else:
            ctr.inc(last, 1)

    def dS(self, eng, out, in_, transpose=False):
        if transpose:
            ins = eng.dma_start_transpose(out=out, in_=in_)
        else:
            ins = eng.dma_start(out=out, in_=in_)
        self.cS.inc(ins, 16)
        return ins

    def dG(self, eng, out, in_):
        ins = eng.dma_start(out=out, in_=in_)
        self.cG.inc(ins, 16)
        return ins

    QUANT_EDGES = False

    def ve(self, eng, ins):
        self.cV.inc(ins)
        eng.wait_ge(self.cV.sem, self.cV.n)

    def vq(self, eng, ins):
        if self.QUANT_EDGES:
            self.ve(eng, ins)

    def ae(self, eng, ins):
        self.cA.inc(ins)
        eng.wait_ge(self.cA.sem, self.cA.n)

    def dump(self, name, ap, shape, dt=F32):
        if name not in self.cfg["dumps"]:
            return
        o = self.nc.dram_tensor("dump_" + name, list(shape), dt, kind="ExternalOutput").ap()
        self.dump_names.append("dump_" + name)
        self.body("sync", lambda eng: self.dS(eng, o, ap) and None)

    def drain(self, gens, n):
        for _ in range(n):
            for g in list(gens):
                try:
                    next(g)
                    break
                except StopIteration:
                    gens.remove(g)

    def drain_all(self, gens):
        for g in list(gens):
            for _ in g:
                pass
            gens.remove(g)

    def emit_all(self):
        cfg = self.cfg
        self.ph_init()
        if 1 in cfg["layers"]:
            self.ph_prep_wsum()
        self._g0 = [self.gen_prep_l0()] if 0 in cfg["layers"] else []
        self._g1 = [self.gen_prep_l1()] if 1 in cfg["layers"] else []
        for s in range(cfg["nseq"]):
            self.ph_load_x(s)
            for L in cfg["layers"]:
                self.ph_layernorm()
                self.dump(f"xn{L}_s{s}", self.xn[:].rearrange("p c q -> p (c q)"),
                          (128, NC8 * self.S), BF16)
                if not cfg.get("skip_attn"):
                    self.ph_project_v(L)
                    self.ph_attention(L, drains=(self._g0 if L == 0 else self._g1) if s == 0 else None)
                if L == 0:
                    self.drain_all(self._g0)
                    if not cfg.get("skip_attn"):
                        self.dump(f"afm_s{s}", self.attn_fm[:].rearrange("p c q -> p (c q)"),
                                  (128, NC8 * self.S), BF16)
                        self.ph_l0_oproj()
                    if not cfg.get("skip_ffn"):
                        self.ph_l0_ffn()
                else:
                    self.drain_all(self._g1)
                    if not cfg.get("skip_attn"):
                        self.ph_l1_scatter()
                    if not cfg.get("skip_ffn"):
                        self.ph_l1_ffn()
                self.dump(f"x{L}_s{s}", self.x[:].rearrange("p c q -> p (c q)"),
                          (128, NC8 * self.S), F32)
            self.ph_store_x(s)
        self.ph_final()

    def ph_init(self):
        def fv(eng):
            eng.memset(self.consts[:, 0:1], -EXP_C)
            eng.memset(self.consts[:, 1:2], 0.0)
            eng.memset(self.consts[:, 2:3], 1e-5)
            eng.memset(self.ones1[:], 1.0)
            eng.memset(self.onesK[:], 1.0)
            eng.memset(self.eA[:, 0:64], 1.0)
            eng.memset(self.eA[:, 64:128], 0.0)
            eng.memset(self.eB[:, 0:64], 0.0)
            eng.memset(self.eB[:, 64:128], 1.0)
            eng.memset(self.vpair[:, :, 64:65].rearrange("p a b -> p (a b)"), 1.0)
            return eng.memset(self.vpair[:, :, 129:130].rearrange("p a b -> p (a b)"), 1.0)
        self.body("vector", fv)

        def fs(eng):
            self.dS(eng, self.mskt[:], self.msk.rearrange("m p r -> p m r"))
            self.dS(eng, self.b1_sb[:], self.b1h)
            self.dS(eng, self.b2_sb[:], self.b2h)
        self.body("sync", fs)

    def gen_prep_l0(self):
        wflat = self.wbufA[:].rearrange("p c v -> p (c v)")
        pin = wflat[:, 0:2048].rearrange("p (s w) -> p s w", s=2)
        sg = wflat[:, 2048:6144].rearrange("p (s w) -> p s w", s=4)
        alf = self.qsf[:, 0, 0:4]
        snapP = self.snap()
        first = True
        for wn, scr, R, F in [(self.w1n, self.bw1T_scr, DFF0, D),
                              (self.w2n, self.bw2T_scr, D, DFF0)]:
            nchunk = R // 128
            ncb = F // 1024
            wv = wn.rearrange("(c p) f -> p c f", p=128)
            scrTv = scr.rearrange("(b p) f -> p b f", p=128)
            sPrev = getattr(self, "_g0_sprev", 0)
            for c in range(nchunk):
                for b in range(ncb):
                    cw = self.cWG[b % 2]
                    wts = list(snapP) if first else [(self.cA, self.cA.n), (self.cV, self.cV.n)]
                    first = False
                    def fdma(eng, c=c, b=b, cw=cw, wv=wv):
                        cw.inc(eng.dma_start(out=pin[:, b % 2, :],
                                             in_=wv[:, c, b * 1024:(b + 1) * 1024]), 16)
                    self.body("gpsimd", fdma, waits=wts)
                    wD = cw.n
                    def fred(eng, b=b, F=F):
                        ins = eng.tensor_reduce(out=alf[:, 2:3], in_=pin[:, b % 2, :],
                                                axis=AX.X, op=ALU.add, apply_absolute_value=True)
                        self.ve(eng, ins)
                        if b == 0:
                            return eng.tensor_scalar(out=alf[:, 0:1], in0=alf[:, 2:3],
                                                     scalar1=1.0 / F, scalar2=None, op0=ALU.mult)
                        return eng.scalar_tensor_tensor(out=alf[:, 0:1], in0=alf[:, 2:3],
                                                        scalar=1.0 / F, in1=alf[:, 0:1],
                                                        op0=ALU.mult, op1=ALU.add)
                    self.body("vector", fred, waits=[(cw, wD), (self.cV, self.cV.n)])
                    self.body("scalar", lambda eng, b=b: eng.activation(
                        out=sg[:, b, :], in_=pin[:, b % 2, :], func=AF.Sign,
                        bias=self.consts[:, 1:2], scale=1.0),
                        waits=[(cw, wD), (self.cA, self.cA.n), (self.cS, sPrev)])
                aSg = self.cA.n
                def fmul(eng, ncb=ncb):
                    last = None
                    for b in range(ncb):
                        last = eng.tensor_scalar(out=sg[:, b, :], in0=sg[:, b, :],
                                                 scalar1=alf[:, 0:1], scalar2=None, op0=ALU.mult)
                        self.ve(eng, last)
                    return eng.nop()
                self.body("vector", fmul, waits=[(self.cA, aSg), (self.cV, self.cV.n)])
                vMul = self.cV.n
                def ftr(eng, c=c, ncb=ncb, scrTv=scrTv):
                    for b in range(ncb):
                        for j in range(8):
                            self.dS(eng, self.qsc[:, 3, j * 128:(j + 1) * 128],
                                    sg[:, b, j * 128:(j + 1) * 128], transpose=True)
                        eng.wait_ge(self.cS.sem, self.cS.n)
                        self.dS(eng, scrTv[:, b * 8:b * 8 + 8, c * 128:(c + 1) * 128],
                                self.qsc[:, 3, :].rearrange("p (j i) -> p j i", i=128))
                        eng.wait_ge(self.cS.sem, self.cS.n)
                self.body("sync", ftr, waits=[(self.cV, vMul), (self.cS, self.cS.n)])
                sPrev = self.cS.n
                self._g0_sprev = sPrev
                yield

    def gen_prep_l1(self):
        pin = self.wbufA[:].rearrange("p c v -> p (c v)")[:, 4096:6144].rearrange("p (s w) -> p s w", s=2)
        for w, scrT, R, C in [(self.wg, self.wgT_scr, DFF1, D),
                              (self.wu, self.wuT_scr, DFF1, D),
                              (self.wd, self.wdT_scr, D, DFF1)]:
            nchunk = R // 128
            wv = w.rearrange("(c p) f -> p c f", p=128)
            scrTv = scrT.rearrange("(b p) f -> p b f", p=128)
            cbs = []
            o = 0
            while o < C:
                wdt = min(1024, C - o)
                cbs.append((o, wdt))
                o += wdt
            blocks = [(c, co, wdt) for c in range(nchunk) for (co, wdt) in cbs]
            snapP = self.snap()
            vQ = {}
            for i, (c, co, wdt) in enumerate(blocks):
                cw = self.cWG[i % 2]
                wts = list(snapP) if i < 2 else [(self.cV, vQ[i - 2])]
                self.body("gpsimd", lambda eng, c=c, co=co, wdt=wdt, i=i, cw=cw, wv=wv: cw.inc(
                    eng.dma_start(out=pin[:, i % 2, 0:wdt], in_=wv[:, c, co:co + wdt]), 16) and None,
                    waits=wts, dma_inc=0)
                wD = cw.n
                self._prep_quant_wait = (cw, wD)
                self.emit_quant([pin[:, i % 2, 0:wdt]], None, wdt, twist=False)
                vQ[i] = self.cV.n
                npc = wdt // 128
                def ftr(eng, npc=npc):
                    for j in range(npc):
                        self.dS(eng, self.qsc[:, 3, j * 128:(j + 1) * 128],
                                self.qsc[:, 1, j * 128:(j + 1) * 128], transpose=True)
                self.body("sync", ftr)
                self.body("sync", lambda eng, c=c, co=co, npc=npc, scrTv=scrTv: self.dS(
                    eng, scrTv[:, co // 128:co // 128 + npc, c * 128:(c + 1) * 128],
                    self.qsc[:, 3, 0:npc * 128].rearrange("p (j i) -> p j i", i=128)) and None)
                yield

    def ph_prep_wsum(self):
        wv = self.wout.rearrange("n h d -> (n h) d").rearrange("(c p) d -> p c d", p=128)
        for c in range(NC8):
            self.body("gpsimd", lambda eng, c=c, wv=wv: self.dG(
                eng, self.prep_bf[:, 0:1024], wv[:, c, :]) and None)
            def ft(eng, c=c):
                eng.matmul(self.bank(0)[0:1, :], self.ones1[:], self.prep_bf[:, 0:512],
                           start=(c == 0), stop=(c == NC8 - 1))
                return eng.matmul(self.bank(1)[0:1, :], self.ones1[:], self.prep_bf[:, 512:1024],
                                  start=(c == 0), stop=(c == NC8 - 1))
            self.body("tensor", ft)
        def fev(eng):
            eng.copy(out=self.scrA[0:1, :], in_=self.bank(0)[0:1, :])
            return eng.copy(out=self.scrB[0:1, :], in_=self.bank(1)[0:1, :])
        self.body("scalar", fev)
        def fst(eng):
            eng.dma_start(out=self.row_scr[0:512], in_=self.scrA[0:1, :].rearrange("a b -> (a b)"))
            eng.dma_start(out=self.row_scr[512:1024], in_=self.scrB[0:1, :].rearrange("a b -> (a b)"))
            return eng.dma_start(out=self.wsum_c[:], in_=self.row_scr.rearrange("(c p) -> p c", p=128))
        def fst(eng):
            self.dS(eng, self.row_scr[0:512].unsqueeze(0), self.scrA[0:1, :])
            self.dS(eng, self.row_scr[512:1024].unsqueeze(0), self.scrB[0:1, :])
            eng.wait_ge(self.cS.sem, self.cS.n)
            ins = eng.dma_start(out=self.wsum_c[:], in_=self.row_scr.rearrange("(c p) -> p c", p=128),
                                allow_slow_non_contiguous=True)
            self.cS.inc(ins, 16)
        self.body("sync", fst)

    def ph_load_x(self, s):
        self.body("sync", lambda eng, s=s: self.dS(
            eng, self.x[:], self.x_in[s].rearrange("(c p) q -> p c q", p=128)) and None)

    def ph_layernorm(self):
        S, NQT = self.S, self.NQT
        inv = 1.0 / D
        for qt in range(NQT):
            b0, b1 = (0, 1) if qt % 2 == 0 else (2, 3)
            for dc in range(NC8):
                o = 512 * (dc % 2)
                def fa(eng, qt=qt, dc=dc, o=o):
                    eng.copy(out=self.qsc[:, 0, o:o + 512], in_=self.x[:, dc, qt * 512:(qt + 1) * 512])
                    return eng.activation(out=self.qsc[:, 1, o:o + 512],
                                          in_=self.x[:, dc, qt * 512:(qt + 1) * 512],
                                          func=AF.Square, bias=self.consts[:, 1:2], scale=1.0)
                self.body("scalar", fa)
                def ft(eng, dc=dc, o=o, b0=b0, b1=b1):
                    eng.matmul(self.bank(b0)[0:1, :], self.ones1[:], self.qsc[:, 0, o:o + 512],
                               start=(dc == 0), stop=(dc == NC8 - 1))
                    return eng.matmul(self.bank(b1)[0:1, :], self.ones1[:], self.qsc[:, 1, o:o + 512],
                                      start=(dc == 0), stop=(dc == NC8 - 1))
                self.body("tensor", ft)
            def fe(eng, qt=qt, b0=b0, b1=b1):
                eng.mul(out=self.st_sb[0:1, qt * 512:(qt + 1) * 512],
                        in_=self.bank(b0)[0:1, :], mul=inv)
                return eng.mul(out=self.st_sb[0:1, S + qt * 512:S + (qt + 1) * 512],
                               in_=self.bank(b1)[0:1, :], mul=inv)
            self.body("scalar", fe)
        def fv(eng):
            ins = eng.tensor_tensor(out=self.rowb[0:1, S:2 * S], in0=self.st_sb[0:1, 0:S],
                                    in1=self.st_sb[0:1, 0:S], op=ALU.mult)
            self.ve(eng, ins)
            return eng.tensor_tensor(out=self.st_sb[0:1, S:2 * S], in0=self.st_sb[0:1, S:2 * S],
                                     in1=self.rowb[0:1, S:2 * S], op=ALU.subtract)
        self.body("vector", fv)
        def fs(eng):
            ins = eng.activation(out=self.st_sb[0:1, S:2 * S], in_=self.st_sb[0:1, S:2 * S],
                                 func=AF.Ln, bias=self.consts[0:1, 2:3], scale=1.0)
            self.ae(eng, ins)
            return eng.activation(out=self.rowb[0:1, 0:S], in_=self.st_sb[0:1, S:2 * S],
                                  func=AF.Exp, bias=self.consts[0:1, 1:2], scale=-0.5)
        self.body("scalar", fs)
        self.body("vector", lambda eng: eng.scalar_tensor_tensor(
            out=self.rowb[0:1, S:2 * S], in0=self.st_sb[0:1, 0:S], scalar=-1.0,
            in1=self.rowb[0:1, 0:S], op0=ALU.mult, op1=ALU.mult))
        for qt in range(NQT):
            b0, b1 = (0, 1) if qt % 2 == 0 else (2, 3)
            def fbc(eng, qt=qt, b0=b0, b1=b1):
                eng.matmul(self.bank(b0), self.onesK[:], self.rowb[0:1, qt * 512:(qt + 1) * 512],
                           start=True, stop=True)
                return eng.matmul(self.bank(b1), self.onesK[:],
                                  self.rowb[0:1, S + qt * 512:S + (qt + 1) * 512],
                                  start=True, stop=True)
            self.body("tensor", fbc)
            def fap(eng, qt=qt, b0=b0, b1=b1):
                last = None
                for dc in range(NC8):
                    ins = eng.tensor_tensor(out=self.scrA[:],
                                            in0=self.x[:, dc, qt * 512:(qt + 1) * 512],
                                            in1=self.bank(b0), op=ALU.mult)
                    self.ve(eng, ins)
                    last = eng.tensor_tensor(out=self.xn[:, dc, qt * 512:(qt + 1) * 512],
                                             in0=self.scrA[:], in1=self.bank(b1), op=ALU.add)
                    if dc < NC8 - 1:
                        self.ve(eng, last)
                return last
            self.body("vector", fap)

    def ph_project_v(self, L):
        S = self.S
        if L == 0:
            self.body("gpsimd", lambda eng: self.dG(
                eng, self.wbufA[:], self.wvT.rearrange("(c p) v -> p c v", p=128)) and None)
        else:
            def fwv(eng):
                for c in range(NC8):
                    self.dG(eng, self.wbufA[:, c, :].rearrange("p (n h) -> p n h", n=NH),
                            self.wqkv[:, :, 128:192].rearrange("n (c p) h -> p c n h", p=128)[:, c, :, :])
            self.body("gpsimd", fwv)
        snapV = self.snap()
        aE = {}
        sD = {}
        for qc in range(S // 128):
            b0, b1 = (0, 1) if qc % 2 == 0 else (2, 3)
            wts = list(snapV) if qc < 2 else [(self.cA, aE[qc - 2])]
            def ft(eng, qc=qc, b0=b0, b1=b1):
                last = None
                for half, b in ((0, b0), (1, b1)):
                    for dc in range(NC8):
                        last = eng.matmul(self.bank(b), self.xn[:, dc, qc * 128:(qc + 1) * 128],
                                          self.wbufA[:, dc, half * 512:(half + 1) * 512],
                                          start=(dc == 0), stop=(dc == NC8 - 1))
                return last
            self.body("tensor", ft, waits=wts)
            tT = self.cT.n
            o = 1024 * (qc % 2)
            wts = [(self.cT, tT)] + ([(self.cS, sD[qc - 2])] if qc >= 2 else [])
            def fe(eng, b0=b0, b1=b1, o=o):
                eng.copy(out=self.qsc[:, 0, :].rearrange("p a -> p a")[:, 0:0] if False else
                         self.qsc[:, 0 + 2 * (o // 1024), :][:, 0:512], in_=self.bank(b0))
                return eng.copy(out=self.qsc[:, 0 + 2 * (o // 1024), :][:, 512:1024], in_=self.bank(b1))
            self.body("scalar", fe, waits=wts)
            aE[qc] = self.cA.n
            self.body("sync", lambda eng, qc=qc, o=o: self.dS(
                eng, self.v_scr[qc * 128:(qc + 1) * 128, :],
                self.qsc[:, 0 + 2 * (o // 1024), :]) and None,
                waits=[(self.cA, aE[qc]), (self.cS, self.cS.n)])
            sD[qc] = self.cS.n

    def ph_attention(self, L, drains=None):
        S, NQT = self.S, self.NQT
        self._at = dict(fv=self.cV.n, fa2=self.cA.n)
        for hp in range(NH // 2):
            if L == 0:
                def fwl(eng, hp=hp):
                    self.dG(eng, self.wbufB[:, 0:8, :],
                            self.wqT.rearrange("(c p) k -> p c k", p=128)[:, :, hp * 128:(hp + 1) * 128])
                    self.dG(eng, self.wbufB[:, 8:16, :],
                            self.wkT.rearrange("(c p) k -> p c k", p=128)[:, :, hp * 128:(hp + 1) * 128])
            else:
                def fwl(eng, hp=hp):
                    for e in range(2):
                        self.dG(eng, self.wbufB[:, 0:8, e * 64:(e + 1) * 64],
                                self.wqkv[2 * hp + e, :, 0:64].rearrange("(c p) h -> p c h", p=128))
                        self.dG(eng, self.wbufB[:, 8:16, e * 64:(e + 1) * 64],
                                self.wqkv[2 * hp + e, :, 64:128].rearrange("(c p) h -> p c h", p=128))
            self.body("gpsimd", fwl)
            def fvl(eng, hp=hp):
                self.dS(eng, self.vpair[:, :, 0:64],
                        self.v_scr[:, hp * 128:hp * 128 + 64].rearrange("(k p) v -> p k v", p=128))
                self.dS(eng, self.vpair[:, :, 65:129],
                        self.v_scr[:, hp * 128 + 64:hp * 128 + 128].rearrange("(k p) v -> p k v", p=128))
            self.body("sync", fvl)
            cG_h = self.cG.n
            cS_h = self.cS.n
            snapH = self.snap()
            kev = {}
            for qt in range(NQT):
                bk = 4 if qt % 2 == 0 else 5
                if qt < 2:
                    wts = [(self.cG, cG_h), (self.cA, self._at["fa2"]), (self.cV, self._at["fv"])]
                else:
                    wts = [(self.cA, kev[qt - 2])]
                def ftk(eng, qt=qt, bk=bk):
                    last = None
                    for dc in range(NC8):
                        last = eng.matmul(self.bank(bk), self.wbufB[:, 8 + dc, :],
                                          self.xn[:, dc, qt * 512:(qt + 1) * 512],
                                          start=(dc == 0), stop=(dc == NC8 - 1))
                    return last
                self.body("tensor", ftk, waits=wts)
                tK = self.cT.n
                self.body("scalar", lambda eng, qt=qt, bk=bk: eng.copy(
                    out=self.kpair[:, qt * 512:(qt + 1) * 512], in_=self.bank(bk)),
                    waits=[(self.cT, tK), (self.cA, self.cA.n)])
                kev[qt] = self.cA.n
            for qt in range(NQT):
                nkk = 4 * (qt + 1)
                def ftq(eng, qt=qt):
                    last = None
                    for dc in range(NC8):
                        last = eng.matmul(self.bank(6), self.wbufB[:, dc, :],
                                          self.xn[:, dc, qt * 512:(qt + 1) * 512],
                                          start=(dc == 0), stop=(dc == NC8 - 1))
                    return last
                self.body("tensor", ftq,
                          waits=[(self.cG, cG_h), (self.cA, self._at["fa2"]), (self.cA, kev[qt])])
                tQ = self.cT.n
                self.body("scalar", lambda eng: eng.mul(out=self.qsmall[:], in_=self.bank(6), mul=0.125),
                          waits=[(self.cT, tQ), (self.cA, self.cA.n)])
                self._at["qev"] = self.cA.n
                self._at["cS_h"] = cS_h
                if L == 0:
                    self.attn_l0_tile(hp, qt, nkk)
                else:
                    self.attn_l1_tile(hp, qt, nkk)
                if drains:
                    self.drain(drains, 10 if L == 0 else 2)

    def _emit_sgroup(self, eng, g, qt, base=0):
        last = None
        for j in range(2):
            kc = 2 * g + j
            last = eng.matmul(self.bank(base + j), self.kpair[0:64, kc * 128:(kc + 1) * 128],
                              self.qsmall[0:64, :], start=True, stop=True)
            last = eng.matmul(self.bank(base + 2 + j), self.kpair[64:128, kc * 128:(kc + 1) * 128],
                              self.qsmall[64:128, :], start=True, stop=True)
        return last

    def _group_has_mask(self, g, qt):
        return any(0 <= (2 * g + j) * 128 - qt * 512 < 512 for j in range(2))

    def _emit_masks(self, eng, g, qt, base=0):
        last = None
        for j in range(2):
            kc = 2 * g + j
            o = kc * 128 - qt * 512
            if 0 <= o < 512:
                for hb in (base + j, base + 2 + j):
                    last = eng.tensor_tensor(out=self.bank(hb), in0=self.bank(hb),
                                             in1=self.mskt[:, o // 128, :], op=ALU.add)
        return last

    def attn_l0_tile(self, hp, qt, nkk):
        ng = nkk // 2
        snap0 = self.snap()
        tS = []
        base_V = self.cV.n
        base_A = self.cA.n
        hasm = [self._group_has_mask(g, qt) for g in range(ng)]
        masked = [g for g in range(ng) if hasm[g]]
        vM = {}
        for i, g in enumerate(masked):
            vM[g] = base_V + i + 1
        aE = [base_A + g + 1 for g in range(ng)]

        ftw = [(self.cA, self._at["qev"]), (self.cS, self._at["cS_h"]),
               (self.cV, self._at["fv"])]

        def emit_av(eng, g):
            last = None
            sl = 1024 * (g % 2)
            for j in range(2):
                kc = 2 * g + j
                first = (kc == 0)
                lastk = (kc == nkk - 1)
                eng.matmul(self.bank(4)[0:64, :], self.vpair[:, kc, 0:64],
                           self.ptA_g[:, sl + j * 512:sl + (j + 1) * 512],
                           start=first, stop=lastk, tile_position=(0, 0))
                eng.matmul(self.bank(4)[64:128, :], self.vpair[:, kc, 65:129],
                           self.ptB_g[:, sl + j * 512:sl + (j + 1) * 512],
                           start=first, stop=lastk, tile_position=(0, 64))
                eng.matmul(self.bank(5)[0:1, :], self.ones1[:],
                           self.ptA_g[:, sl + j * 512:sl + (j + 1) * 512],
                           start=first, stop=lastk)
                last = eng.matmul(self.bank(6)[0:1, :], self.ones1[:],
                                  self.ptB_g[:, sl + j * 512:sl + (j + 1) * 512],
                                  start=first, stop=lastk)
            return last

        def ft(eng):
            last = self._emit_sgroup(eng, 0, qt)
            self.cT.inc(last)
            tS.append(self.cT.n)
            for g in range(1, ng):
                eng.wait_ge(self.cA.sem, aE[g - 1])
                last = self._emit_sgroup(eng, g, qt)
                self.cT.inc(last)
                tS.append(self.cT.n)
                last = emit_av(eng, g - 1)
                if g < ng - 1:
                    self.cT.inc(last)
            eng.wait_ge(self.cA.sem, aE[ng - 1])
            return emit_av(eng, ng - 1)
        self.body("tensor", ft, waits=ftw)

        def fv(eng):
            last = None
            for g in masked:
                eng.wait_ge(self.cT.sem, tS[g])
                last = self._emit_masks(eng, g, qt)
                if g != masked[-1]:
                    self.cV.inc(last)
            return last if last is not None else eng.nop()
        self.body("vector", fv, waits=snap0)
        assert self.cV.n == base_V + max(len(masked), 1), (self.cV.n, base_V, masked)

        def fa(eng):
            last = None
            for g in range(ng):
                eng.wait_ge(self.cT.sem, tS[g])
                if hasm[g]:
                    eng.wait_ge(self.cV.sem, vM[g])
                sl = 1024 * (g % 2)
                eng.activation(out=self.ptA_g[:, sl:sl + 1024], in_=self.psum[:, 0:1024],
                               func=AF.Exp, bias=self.consts[:, 0:1], scale=1.0)
                last = eng.activation(out=self.ptB_g[:, sl:sl + 1024], in_=self.psum[:, 1024:2048],
                                      func=AF.Exp, bias=self.consts[:, 0:1], scale=1.0)
                if g < ng - 1:
                    self.cA.inc(last)
            return last
        self.body("scalar", fa, waits=snap0)
        assert self.cA.n == aE[-1], (self.cA.n, aE)

        def fa2(eng):
            eng.copy(out=self.dAB[0:1, 0:512], in_=self.bank(5)[0:1, :])
            return eng.copy(out=self.dAB[0:1, 512:1024], in_=self.bank(6)[0:1, :])
        self.body("scalar", fa2)
        self._at["fa2"] = self.cA.n
        def ft2(eng):
            eng.matmul(self.bank(7), self.eA[:], self.dAB[0:1, 0:512], start=True, stop=False)
            return eng.matmul(self.bank(7), self.eB[:], self.dAB[0:1, 512:1024], start=False, stop=True)
        self.body("tensor", ft2)
        def fa3(eng):
            ins = eng.activation(out=self.scrA[:], in_=self.bank(7), func=AF.Ln,
                                 bias=self.consts[:, 1:2], scale=1.0)
            self.ae(eng, ins)
            return eng.activation(out=self.scrB[:], in_=self.scrA[:], func=AF.Exp,
                                  bias=self.consts[:, 1:2], scale=-1.0)
        self.body("scalar", fa3)
        self.body("vector", lambda eng: eng.tensor_tensor(
            out=self.attn_fm[:, hp, qt * 512:(qt + 1) * 512], in0=self.bank(4),
            in1=self.scrB[:], op=ALU.mult))
        self._at["fv"] = self.cV.n

    def attn_l1_tile(self, hp, qt, nkk):
        ng = nkk // 2
        snap0 = self.snap()
        tS = []
        base_V = self.cV.n
        base_A = self.cA.n
        hasm = [self._group_has_mask(g, qt) for g in range(ng)]
        masked = [g for g in range(ng) if hasm[g]]
        vM = {}
        for i, g in enumerate(masked):
            vM[g] = base_V + i + 1
        aE = [base_A + g + 1 for g in range(ng)]
        bb = lambda g: 4 * (g % 2)

        ftw = [(self.cA, self._at["qev"]), (self.cS, self._at["cS_h"])]

        def ft(eng):
            last = None
            for g in range(ng):
                if g >= 2:
                    eng.wait_ge(self.cA.sem, aE[g - 2])
                last = self._emit_sgroup(eng, g, qt, base=bb(g))
                if g < ng - 1:
                    self.cT.inc(last)
                tS.append(self.cT.n + (1 if g == ng - 1 else 0))
            return last
        self.body("tensor", ft, waits=ftw)

        def fv(eng):
            last = None
            for g in masked:
                eng.wait_ge(self.cT.sem, tS[g])
                last = self._emit_masks(eng, g, qt, base=bb(g))
                if g != masked[-1]:
                    self.cV.inc(last)
            return last if last is not None else eng.nop()
        self.body("vector", fv, waits=snap0)

        def fa(eng):
            last = None
            for g in range(ng):
                eng.wait_ge(self.cT.sem, tS[g])
                if hasm[g]:
                    eng.wait_ge(self.cV.sem, vM[g])
                eng.activation(out=self.ptA_f[:, 2 * g:2 * g + 2, :].rearrange("p a b -> p (a b)"),
                               in_=self.psum[:, bb(g) * 512:(bb(g) + 2) * 512],
                               func=AF.Exp, bias=self.consts[:, 0:1], scale=1.0)
                last = eng.activation(out=self.ptB_f[:, 2 * g:2 * g + 2, :].rearrange("p a b -> p (a b)"),
                                      in_=self.psum[:, (bb(g) + 2) * 512:(bb(g) + 4) * 512],
                                      func=AF.Exp, bias=self.consts[:, 0:1], scale=1.0)
                if g < ng - 1:
                    self.cA.inc(last)
            return last
        self.body("scalar", fa, waits=snap0)

        expL = self.cA.n
        expL = self.cA.n
        for qb in range(4):
            wts = [(self.cA, expL), (self.cA, self._at["fa2"]),
                   (self.cV, self._at["fv"]), (self.cS, self._at["cS_h"])]
            def ftav(eng, qb=qb):
                last = None
                for pt, b, lo in ((self.ptA_f, 4, 0), (self.ptB_f, 5, 65)):
                    for kc in range(nkk):
                        last = eng.matmul(self.bank(b, 0, 65),
                                          pt[:, kc, qb * 128:(qb + 1) * 128],
                                          self.vpair[:, kc, lo:lo + 65],
                                          start=(kc == 0), stop=(kc == nkk - 1))
                return last
            self.body("tensor", ftav, waits=wts)
            def fa2(eng):
                ins = eng.activation(out=self.qsf[:, 0, 0:1], in_=self.bank(4, 64, 65),
                                     func=AF.Ln, bias=self.consts[:, 1:2], scale=1.0)
                self.ae(eng, ins)
                ins = eng.activation(out=self.qsf[:, 0, 1:2], in_=self.qsf[:, 0, 0:1],
                                     func=AF.Exp, bias=self.consts[:, 1:2], scale=-1.0)
                self.ae(eng, ins)
                ins = eng.activation(out=self.qsf[:, 0, 2:3], in_=self.bank(5, 64, 65),
                                     func=AF.Ln, bias=self.consts[:, 1:2], scale=1.0)
                self.ae(eng, ins)
                return eng.activation(out=self.qsf[:, 0, 3:4], in_=self.qsf[:, 0, 2:3],
                                      func=AF.Exp, bias=self.consts[:, 1:2], scale=-1.0)
            self.body("scalar", fa2)
            def fv2(eng):
                eng.tensor_scalar(out=self.tstage[:, 0, :], in0=self.bank(4, 0, 64),
                                  scalar1=self.qsf[:, 0, 1:2], scalar2=None, op0=ALU.mult)
                return eng.tensor_scalar(out=self.tstage[:, 1, :], in0=self.bank(5, 0, 64),
                                         scalar1=self.qsf[:, 0, 3:4], scalar2=None, op0=ALU.mult)
            self.body("vector", fv2)
            self._at["fv"] = self.cV.n
            self._at["fa2"] = self.cA.n
            sbase = qt * 512 + qb * 128
            self.body("sync", lambda eng, hp=hp, sbase=sbase: self.dS(
                eng, self.t_scr[2 * hp:2 * hp + 2, sbase:sbase + 128, :].rearrange("n (j p) h -> p j n h", p=128),
                self.tstage[:].unsqueeze(1)) and None)

    def ph_l0_oproj(self):
        NQT = self.NQT
        self.body("gpsimd", lambda eng: self.dG(
            eng, self.wbufA[:], self.woT.rearrange("(c p) d -> p c d", p=128)) and None)
        for qt in range(NQT):
            for db in range(NC8):
                b = (qt * NC8 + db) % 2
                def ft(eng, qt=qt, db=db, b=b):
                    last = None
                    for vc in range(NC8):
                        last = eng.matmul(self.bank(b), self.wbufA[:, vc, db * 128:(db + 1) * 128],
                                          self.attn_fm[:, vc, qt * 512:(qt + 1) * 512],
                                          start=(vc == 0), stop=(vc == NC8 - 1))
                    return last
                self.body("tensor", ft)
                self.body("vector", lambda eng, qt=qt, db=db, b=b: eng.tensor_tensor(
                    out=self.x[:, db, qt * 512:(qt + 1) * 512],
                    in0=self.x[:, db, qt * 512:(qt + 1) * 512],
                    in1=self.bank(b), op=ALU.add))

    def ph_l0_ffn(self):
        NQT = self.NQT
        bw1 = self.bw1T_scr.rearrange("(c p) f -> p c f", p=128)
        bw2 = self.bw2T_scr.rearrange("(c p) f -> p c f", p=128)
        w1v = self.wbufA[:].rearrange("p c v -> p (c v)")[:, 0:NC8 * 512].rearrange("p (c f) -> p c f", f=512)
        for qt in range(NQT):
            snapF = self.snap()
            tG = {}
            aG = {}
            tT1 = {}
            for hg in range(8):
                cw = self.cW[hg % 2]
                wts = list(snapF) if hg < 2 else [(self.cT, tT1[(hg - 2) * 4 + 3])]
                def fg(eng, hg=hg, cw=cw):
                    cw.inc(eng.dma_start(
                        out=self.wbufA[:, :, 512 * (hg % 2):512 * (hg % 2) + 512],
                        in_=bw1[:, :, hg * 512:(hg + 1) * 512]), 16)
                self.body("sync", fg, waits=wts)
                wG = cw.n
                for hj in range(4):
                    hc = hg * 4 + hj
                    b = hc % 2
                    wts = [(cw, wG)]
                    if hc >= 2:
                        wts.append((self.cA, aG[hc - 2]))
                    elif hg < 1:
                        wts += list(snapF)
                    def ft(eng, hj=hj, b=b, qt=qt, hg=hg):
                        last = None
                        for dc in range(NC8):
                            last = eng.matmul(
                                self.bank(b),
                                self.wbufA[:, dc, 512 * (hg % 2) + hj * 128:512 * (hg % 2) + (hj + 1) * 128],
                                self.xn[:, dc, qt * 512:(qt + 1) * 512],
                                start=(dc == 0), stop=(dc == NC8 - 1))
                        return last
                    self.body("tensor", ft, waits=wts)
                    tT1[hc] = self.cT.n
                    self.body("scalar", lambda eng, hc=hc, b=b: eng.activation(
                        out=self.h_l0[:, hc, :], in_=self.bank(b), func=AF.Gelu,
                        bias=self.b1_sb[:, hc:hc + 1], scale=1.0),
                        waits=[(self.cT, tT1[hc])] + ([(self.cA, self.cA.n)] if hc == 0 else []))
                    aG[hc] = self.cA.n
            tT2 = {}
            vU = {}
            aAll = self.cA.n
            for db in range(NC8):
                for half in range(2):
                    cw = self.cW[half]
                    wts = [(self.cT, tT2.get((db - 1, half), 0))] if db >= 1 else [(self.cT, self.cT.n)]
                    def fg2(eng, db=db, half=half, cw=cw):
                        cw.inc(eng.dma_start(
                            out=self.wbufB[:, 16 * half:16 * half + 16, :],
                            in_=bw2[:, 16 * half:16 * half + 16, db * 128:(db + 1) * 128]), 16)
                    self.body("sync", fg2, waits=wts)
                    tG[(db, half)] = cw.n
                b = 2 + db % 2
                wts = [(self.cW[0], tG[(db, 0)]), (self.cW[1], tG[(db, 1)]), (self.cA, aAll)]
                if db >= 2:
                    wts.append((self.cV, vU[db - 2]))
                def ft2(eng, db=db, b=b):
                    last = None
                    for hc in range(32):
                        last = eng.matmul(self.bank(b), self.wbufB[:, hc, :], self.h_l0[:, hc, :],
                                          start=(hc == 0), stop=(hc == 31))
                        if hc == 15:
                            pass
                    return last
                self.body("tensor", ft2, waits=wts)
                tT2[(db, 0)] = self.cT.n
                tT2[(db, 1)] = self.cT.n
                self.body("vector", lambda eng, qt=qt, db=db, b=b: eng.scalar_tensor_tensor(
                    out=self.x[:, db, qt * 512:(qt + 1) * 512], in0=self.bank(b),
                    scalar=self.b2_sb[:, db:db + 1], in1=self.x[:, db, qt * 512:(qt + 1) * 512],
                    op0=ALU.add, op1=ALU.add), waits=[(self.cT, tT2[(db, 0)]), (self.cV, self.cV.n)])
                vU[db] = self.cV.n

    def ph_l1_scatter(self):
        S = self.S
        for dc in range(NC8):
            G = S // 64
            def fy(eng, dc=dc, G=G):
                self.dS(eng, self.y_sb[0:64, 0:S].rearrange("p (c h) -> p c h", h=64),
                        self.t_scr[2 * dc].rearrange("(a c) h -> a c h", c=G))
                self.dS(eng, self.y_sb[64:128, 0:S].rearrange("p (c h) -> p c h", h=64),
                        self.t_scr[2 * dc + 1].rearrange("(a c) h -> a c h", c=G))
            self.body("sync", fy)
            self.body("vector", lambda eng, dc=dc: eng.scalar_tensor_tensor(
                out=self.x[:, dc, :], in0=self.y_sb[:, 0:S],
                scalar=self.wsum_c[:, dc:dc + 1], in1=self.x[:, dc, :],
                op0=ALU.mult, op1=ALU.add))

    def emit_quant(self, in_aps, out_ap, W, twist):
        nb = W // 16
        q0 = self.qsc[:, 0, 0:W]
        extra_w = getattr(self, "_prep_quant_wait", None)
        self._prep_quant_wait = None

        def fv1(eng):
            if extra_w is not None:
                eng.wait_ge(extra_w[0].sem, extra_w[1])
            if in_aps is not None:
                o = 0
                for ap in in_aps:
                    w = ap.shape[-1]
                    if twist:
                        ins = eng.transpose(out=self.qsc[:, 0, o:o + w], in_=ap)
                    else:
                        ins = eng.tensor_copy(out=self.qsc[:, 0, o:o + w], in_=ap)
                    o += w
                    self.ve(eng, ins)
            ins = eng.tensor_reduce(out=self.qsf[:, 0, 0:nb],
                                    in_=q0.rearrange("p (g s) -> p g s", s=16),
                                    axis=AX.X, op=ALU.max, apply_absolute_value=True)
            self.vq(eng, ins)
            return eng.tensor_scalar(out=self.qsf[:, 1, 0:nb], in0=self.qsf[:, 0, 0:nb],
                                     scalar1=1.0 / 6.0, scalar2=1e-12, op0=ALU.mult, op1=ALU.max)
        self.body("vector", fv1)

        def fa1(eng):
            ins = eng.activation(out=self.qsf[:, 2, 0:nb], in_=self.qsf[:, 1, 0:nb],
                                 func=AF.Ln, bias=self.consts[:, 1:2], scale=1.0)
            self.ae(eng, ins)
            eng.activation(out=self.qsf[:, 3, 0:nb], in_=self.qsf[:, 2, 0:nb],
                           func=AF.Exp, bias=self.consts[:, 1:2], scale=-1.0)
            eng.activation(out=self.qsc[:, 1, 0:W], in_=q0,
                           func=AF.Abs, bias=self.consts[:, 1:2], scale=1.0)
            return eng.activation(out=self.qsc[:, 3, 0:W], in_=q0,
                                  func=AF.Sign, bias=self.consts[:, 1:2], scale=1.0)
        self.body("scalar", fa1)

        def fv2(eng):
            a = self.qsc[:, 1, 0:W]
            st = self.qsc[:, 2, 0:W]
            a3 = lambda t: t.rearrange("p (g s) -> p g s", s=16)
            rsc = self.qsf[:, 3, 0:nb].unsqueeze(2).broadcast_to((128, nb, 16))
            ins = eng.tensor_tensor(out=a3(a), in0=a3(a), in1=rsc, op=ALU.mult)
            self.vq(eng, ins)
            ins = eng.tensor_scalar(out=st, in0=a, scalar1=2.5, scalar2=None, op0=ALU.is_gt)
            self.vq(eng, ins)
            for th in (3.5, 5.0, 5.0):
                ins = eng.scalar_tensor_tensor(out=st, in0=a, scalar=th, in1=st,
                                               op0=ALU.is_gt, op1=ALU.add)
                self.vq(eng, ins)
            ins = eng.tensor_scalar(out=st, in0=st, scalar1=2.0, scalar2=None, op0=ALU.mult)
            self.vq(eng, ins)
            for th in (0.25, 0.75, 1.25, 1.75):
                ins = eng.scalar_tensor_tensor(out=st, in0=a, scalar=th, in1=st,
                                               op0=ALU.is_gt, op1=ALU.add)
                self.vq(eng, ins)
            ins = eng.tensor_scalar(out=self.qsf[:, 2, 0:nb], in0=self.qsf[:, 1, 0:nb],
                                    scalar1=0.5, scalar2=None, op0=ALU.mult)
            self.vq(eng, ins)
            hsc = self.qsf[:, 2, 0:nb].unsqueeze(2).broadcast_to((128, nb, 16))
            ins = eng.tensor_tensor(out=a3(st), in0=a3(st), in1=hsc, op=ALU.mult)
            self.vq(eng, ins)
            ins = eng.tensor_tensor(out=a, in0=st, in1=self.qsc[:, 3, 0:W], op=ALU.mult)
            self.ve(eng, ins)
            if twist:
                ins = eng.transpose(out=out_ap, in_=a)
                self.ve(eng, ins)
                return eng.nop()
            if out_ap is None:
                return eng.nop()
            return eng.tensor_copy(out=out_ap, in_=a)
        self.body("vector", fv2)

    def ph_l1_ffn(self):
        NQT = self.NQT
        wgT = self.wgT_scr.rearrange("(c p) f -> p c f", p=128)
        wuT = self.wuT_scr.rearrange("(c p) f -> p c f", p=128)
        wdT = self.wdT_scr.rearrange("(c p) f -> p c f", p=128)
        for qt in range(NQT):
            qts = slice(qt * 512, (qt + 1) * 512)
            for pp in range(NC8 // 2):
                self.emit_quant([self.xn[:, 2 * pp, qts], self.xn[:, 2 * pp + 1, qts]],
                                self.xq[:, 2 * pp:2 * pp + 2, :].rearrange("p a b -> p (a b)"),
                                1024, twist=True)
            tT, tA, tV, tS = {}, {}, {}, {}
            snapA = self.snap()
            snapA_cV = self.cV.n
            for ic in range(NI):
                sl = 16 * (ic % 2)
                bg, bu = (0, 1) if ic % 2 == 0 else (2, 3)
                wts = list(snapA)
                if ic >= 2:
                    wts = [(self.cT, tT[ic - 2])]
                cw = self.cW[ic % 2]
                def fwg(eng, ic=ic, sl=sl, cw=cw):
                    cw.inc(eng.dma_start(out=self.wbufB[:, sl:sl + 8, :],
                                         in_=wgT[:, :, ic * 128:(ic + 1) * 128]), 16)
                    cw.inc(eng.dma_start(out=self.wbufB[:, sl + 8:sl + 16, :],
                                         in_=wuT[:, :, ic * 128:(ic + 1) * 128]), 16)
                self.body("sync", fwg, waits=wts)
                tS[ic] = cw.n
                wts = [(cw, tS[ic])]
                if ic >= 2:
                    wts.append((self.cV, tV[ic - 2]))
                else:
                    wts.append((self.cV, snapA_cV))
                def ft(eng, ic=ic, sl=sl, bg=bg, bu=bu):
                    for dc in range(NC8):
                        eng.matmul(self.bank(bg), self.wbufB[:, sl + dc, :], self.xq[:, dc, :],
                                   start=(dc == 0), stop=(dc == NC8 - 1))
                    last = None
                    for dc in range(NC8):
                        last = eng.matmul(self.bank(bu), self.wbufB[:, sl + 8 + dc, :], self.xq[:, dc, :],
                                          start=(dc == 0), stop=(dc == NC8 - 1))
                    return last
                self.body("tensor", ft, waits=wts)
                tT[ic] = self.cT.n
                wts = [(self.cT, tT[ic])]
                if ic >= 1:
                    wts.append((self.cV, tV[ic - 1]))
                self.body("scalar", lambda eng, bg=bg: eng.activation(
                    out=self.scrA[:], in_=self.bank(bg), func=AF.Silu,
                    bias=self.consts[:, 1:2], scale=1.0), waits=wts)
                tA[ic] = self.cA.n
                self.body("vector", lambda eng, ic=ic, bu=bu: eng.tensor_tensor(
                    out=self.hq[:, ic, :], in0=self.scrA[:], in1=self.bank(bu), op=ALU.mult),
                    waits=[(self.cA, tA[ic])])
                tV[ic] = self.cV.n
                if ic % 2 == 1:
                    pp = ic // 2
                    self.emit_quant([self.hq[:, 2 * pp:2 * pp + 2, :].rearrange("p a b -> p (a b)")],
                                    self.hq[:, 2 * pp:2 * pp + 2, :].rearrange("p a b -> p (a b)"),
                                    1024, twist=True)
            assert NI % 2 == 0
            for db in range(NC8):
                self.body("sync", lambda eng, db=db: self.dS(
                    eng, self.wbufB[:, 0:NI, :], wdT[:, :, db * 128:(db + 1) * 128]) and None)
                b = 2 + db % 2
                def ft2(eng, b=b):
                    last = None
                    for ic in range(NI):
                        last = eng.matmul(self.bank(b), self.wbufB[:, ic, :], self.hq[:, ic, :],
                                          start=(ic == 0), stop=(ic == NI - 1))
                    return last
                self.body("tensor", ft2)
                self.body("vector", lambda eng, qt=qt, db=db, b=b: eng.tensor_tensor(
                    out=self.x[:, db, qt * 512:(qt + 1) * 512],
                    in0=self.x[:, db, qt * 512:(qt + 1) * 512],
                    in1=self.bank(b), op=ALU.add))

    def ph_store_x(self, s):
        self.body("sync", lambda eng, s=s: self.dS(
            eng, self.out[s].rearrange("(c p) q -> p c q", p=128), self.x[:]) and None)

    def ph_final(self):
        snap = self.snap()
        eng = Rec(self.streams["sync"])
        for c, n in snap:
            if n > 0:
                eng.wait_ge(c.sem, n)


def build_masks():
    m = np.zeros((4, 128, 512), dtype=np.float32)
    for oi, o in enumerate((0, 128, 256, 384)):
        i = np.arange(128)[:, None]
        j = np.arange(512)[None, :]
        m[oi] = np.where(i + o <= j, 0.0, NEG)
    return m.astype(ml_dtypes.bfloat16)


def host_inputs(inputs, cfg, core):
    nseq = cfg["nseq"]
    x = np.asarray(inputs["L0_attn_x"], dtype=np.float32)
    xs = x[core * nseq:(core + 1) * nseq]
    x_t = np.ascontiguousarray(xs.transpose(0, 2, 1))
    f32 = lambda a: np.ascontiguousarray(np.asarray(a, dtype=np.float32))
    tr = lambda a: np.ascontiguousarray(np.asarray(a, dtype=np.float32).T)
    return {
        "x_t": x_t,
        "msk": build_masks(),
        "L0_wqT": tr(inputs["L0_attn_q_proj_weight"]),
        "L0_wkT": tr(inputs["L0_attn_k_proj_weight"]),
        "L0_wvT": tr(inputs["L0_attn_v_proj_weight"]),
        "L0_woT": tr(inputs["L0_attn_o_proj_weight"]),
        "L0_w1": f32(inputs["L0_ffn_w1"]),
        "L0_w2": f32(inputs["L0_ffn_w2"]),
        "L0_b1h": np.ascontiguousarray(f32(inputs["L0_ffn_b1"]).reshape(32, 128).T),
        "L0_b2h": np.ascontiguousarray(f32(inputs["L0_ffn_b2"]).reshape(8, 128).T),
        "L1_wqkv": f32(inputs["L1_attn_W_qkv"]),
        "L1_wout": f32(inputs["L1_attn_W_out"]),
        "L1_wg": f32(inputs["L1_ffn_w_gate"]),
        "L1_wu": f32(inputs["L1_ffn_w_up"]),
        "L1_wd": f32(inputs["L1_ffn_w_down"]),
    }


def kernel(**inputs):
    cfg = default_cfg()
    p = Prog(cfg)
    nc = p.build()
    in_maps = [host_inputs(inputs, cfg, core) for core in range(8)]
    res = run_bass_kernel_spmd(nc, in_maps, core_ids=list(range(8)))
    outs = []
    for core in range(8):
        o = np.asarray(res.results[core]["out_t"], dtype=np.float32)
        outs.append(o.transpose(0, 2, 1))
    return np.ascontiguousarray(np.concatenate(outs, axis=0))


# version 10
# speedup vs baseline: 1.0055x; 1.0055x over previous
import sys
sys.path.insert(0, "/opt/trn_rl_repo")
import contextlib
import numpy as np
import ml_dtypes
import concourse.bass as bass
import concourse.mybir as mybir
from concourse.bass_utils import run_bass_kernel_spmd

F32 = mybir.dt.float32
BF16 = mybir.dt.bfloat16
AF = mybir.ActivationFunctionType
ALU = mybir.AluOpType
AX = mybir.AxisListType

D = 1024
NH = 16
HD = 64
DFF0 = 4096
DFF1 = 2816
NC8 = D // 128
NI = DFF1 // 128
EXP_C = 16.0
NEG = -1e10


def default_cfg():
    return dict(S=2048, nseq=4, layers=(0, 1), dumps=(), skip_attn=False, skip_ffn=False)


class Ctr:
    def __init__(self, es, nc, name):
        self.sem = es.enter_context(nc.semaphore(name))
        self.n = 0

    def inc(self, ins, k=1):
        ins.then_inc(self.sem, k)
        self.n += k


class FakeIns:
    __slots__ = ("incs",)

    def __init__(self):
        self.incs = []

    def then_inc(self, sem, v):
        self.incs.append((sem, v))
        return self


class Rec:

    def __init__(self, stream):
        self._stream = stream

    def __getattr__(self, name):
        st = self._stream

        def call(*a, **k):
            fi = FakeIns()
            st.append((name, a, k, fi))
            return fi
        return call


class Prog:
    def __init__(self, cfg):
        self.cfg = cfg
        self.S = cfg["S"]
        self.NQT = self.S // 512
        self.NKC = self.S // 128
        self.nc = bass.Bass("TRN2", target_bir_lowering=False)
        self.es = contextlib.ExitStack()
        self.dump_names = []

    def sb(self, name, shape, dt):
        return self.es.enter_context(self.nc.sbuf_tensor(name, shape, dt))

    def bank(self, i, lo=0, hi=512):
        return self.psum[:, i * 512 + lo:i * 512 + hi]

    def build(self):
        cfg, S, nc = self.cfg, self.S, self.nc
        di = lambda n, s, dt=F32: nc.dram_tensor(n, list(s), dt, kind="ExternalInput").ap()
        dt_ = lambda n, s, dt: nc.dram_tensor(n, list(s), dt, kind="Internal").ap()
        self.x_in = di("x_t", (cfg["nseq"], D, S))
        self.out = nc.dram_tensor("out_t", [cfg["nseq"], D, S], F32, kind="ExternalOutput").ap()
        self.msk = di("msk", (4, 128, 512), BF16)
        self.wqT = di("L0_wqT", (D, D))
        self.wkT = di("L0_wkT", (D, D))
        self.wvT = di("L0_wvT", (D, D))
        self.woT = di("L0_woT", (D, D))
        self.w1n = di("L0_w1", (DFF0, D))
        self.w2n = di("L0_w2", (D, DFF0))
        self.b1h = di("L0_b1h", (128, DFF0 // 128))
        self.b2h = di("L0_b2h", (128, NC8))
        self.wqkv = di("L1_wqkv", (NH, D, 192))
        self.wout = di("L1_wout", (NH, HD, D))
        self.wg = di("L1_wg", (DFF1, D))
        self.wu = di("L1_wu", (DFF1, D))
        self.wd = di("L1_wd", (D, DFF1))

        self.v_scr = dt_("v_scr", (S, D), BF16)
        self.t_scr = dt_("t_scr", (NH, S, HD), BF16)
        self.bw1T_scr = dt_("bw1T_scr", (D, DFF0), BF16)
        self.bw2T_scr = dt_("bw2T_scr", (DFF0, D), BF16)
        self.wgT_scr = dt_("wgT_scr", (D, DFF1), BF16)
        self.wuT_scr = dt_("wuT_scr", (D, DFF1), BF16)
        self.wdT_scr = dt_("wdT_scr", (DFF1, D), BF16)
        self.row_scr = dt_("row_scr", (1024,), F32)

        self.x = self.sb("x", [128, NC8, S], F32)
        self.xn = self.sb("xn", [128, NC8, S], BF16)
        self.arena = self.sb("arena", [128, 20480], BF16)
        self.kpair = self.sb("kpair", [128, S], BF16)
        self.qsmall = self.sb("qsmall", [128, 512], BF16)
        self.vpair = self.sb("vpair", [128, self.NKC, 132], BF16)
        self.wbufA = self.sb("wbufA", [128, NC8, 1024], BF16)
        self.wbufB = self.sb("wbufB", [128, 32, 128], BF16)
        self.mskt = self.sb("mskt", [128, 4, 512], BF16)
        self.consts = self.sb("consts", [128, 4], F32)
        self.ones1 = self.sb("ones1", [128, 1], BF16)
        self.onesK = self.sb("onesK", [1, 128], BF16)
        self.eA = self.sb("eA", [1, 128], BF16)
        self.eB = self.sb("eB", [1, 128], BF16)
        self.rowb = self.sb("rowb", [1, 2 * max(S, 2048)], BF16)
        self.st_sb = self.sb("st_sb", [1, 2 * S], BF16)
        self.dAB = self.sb("dAB", [1, 1024], BF16)
        self.wsum_c = self.sb("wsum_c", [128, NC8], F32)
        self.scrA = self.sb("scrA", [128, 512], F32)
        self.scrB = self.sb("scrB", [128, 512], F32)
        self.qsc = self.sb("qsc", [128, 4, 1024], BF16)
        self.qsf = self.sb("qsf", [128, 4, 64], F32)
        self.h_stage = self.sb("h_stage", [128, 512], BF16)
        self.tstage = self.sb("tstage", [128, 2, 64], BF16)
        self.b1_sb = self.sb("b1_sb", [128, 32], F32)
        self.b2_sb = self.sb("b2_sb", [128, NC8], F32)
        self.psum = self.es.enter_context(self.nc.psum_tensor("psum", [128, 4096], F32))

        A = self.arena
        self.attn_fm = A[:, 0:NC8 * S].rearrange("p (c q) -> p c q", c=NC8)
        off = NC8 * S
        self.ptA_g = A[:, off:off + 2048]
        self.ptB_g = A[:, off + 2048:off + 4096]
        self.ptA_f = A[:, 0:self.NKC * 512].rearrange("p (k r) -> p k r", r=512)
        self.ptB_f = A[:, self.NKC * 512:2 * self.NKC * 512].rearrange("p (k r) -> p k r", r=512)
        self.h_l0 = A[:, 0:32 * 512].rearrange("p (k r) -> p k r", r=512)
        self.xq = A[:, 0:NC8 * 512].rearrange("p (k r) -> p k r", r=512)
        self.hq = A[:, NC8 * 512:(NC8 + NI) * 512].rearrange("p (k r) -> p k r", r=512)
        self.prep_bf = A[:, 0:4096]
        self.y_sb = self.qsc[:, 0:2, :].rearrange("p a b -> p (a b)")

        es = self.es
        nc_ = self.nc
        self.cT = Ctr(es, nc_, "cT")
        self.cV = Ctr(es, nc_, "cV")
        self.cA = Ctr(es, nc_, "cA")
        self.cG = Ctr(es, nc_, "cG")
        self.cS = Ctr(es, nc_, "cS")
        self.cW = [Ctr(es, nc_, "cW0"), Ctr(es, nc_, "cW1")]
        self.cWG = [Ctr(es, nc_, "cWG0"), Ctr(es, nc_, "cWG1")]
        self.all_ctrs = [self.cT, self.cV, self.cA, self.cG, self.cS] + self.cW + self.cWG

        self.streams = {k: [] for k in ("tensor", "vector", "scalar", "gpsimd", "sync")}
        self.emit_all()
        with es:
            with nc.Block() as blk:
                for kind in ("tensor", "vector", "scalar", "gpsimd", "sync"):
                    getattr(blk, kind)(self._replayer(kind))
        return nc

    def _replayer(self, kind):
        stream = self.streams[kind]

        def run(eng):
            seen = {}
            for name, a, k, fi in stream:
                if name == "wait_ge":
                    sem, v = a
                    key = id(sem)
                    if seen.get(key, -1) >= v:
                        continue
                    seen[key] = v
                    eng.wait_ge(sem, v)
                    continue
                ins = getattr(eng, name)(*a, **k)
                for sem, v in fi.incs:
                    ins.then_inc(sem, v)
        return run

    def snap(self):
        return [(c, c.n) for c in self.all_ctrs]

    def body(self, kind, fn, dma_inc=None, waits=None):
        ctr = {"tensor": self.cT, "vector": self.cV, "scalar": self.cA,
               "gpsimd": self.cG, "sync": self.cS}[kind]
        is_dma = kind in ("gpsimd", "sync")
        if waits is None:
            waits = self.snap()
        waits = [(c, n) for (c, n) in waits if n > 0]
        eng = Rec(self.streams[kind])
        for c, n in waits:
            eng.wait_ge(c.sem, n)
        last = fn(eng)
        if is_dma:
            assert last is None, "DMA bodies must inc per-DMA via dS/dG"
        else:
            ctr.inc(last, 1)

    def dS(self, eng, out, in_, transpose=False):
        if transpose:
            ins = eng.dma_start_transpose(out=out, in_=in_)
        else:
            ins = eng.dma_start(out=out, in_=in_)
        self.cS.inc(ins, 16)
        return ins

    def dG(self, eng, out, in_):
        ins = eng.dma_start(out=out, in_=in_)
        self.cG.inc(ins, 16)
        return ins

    QUANT_EDGES = False

    def ve(self, eng, ins):
        self.cV.inc(ins)
        eng.wait_ge(self.cV.sem, self.cV.n)

    def vq(self, eng, ins):
        if self.QUANT_EDGES:
            self.ve(eng, ins)

    def ae(self, eng, ins):
        self.cA.inc(ins)
        eng.wait_ge(self.cA.sem, self.cA.n)

    def dump(self, name, ap, shape, dt=F32):
        if name not in self.cfg["dumps"]:
            return
        o = self.nc.dram_tensor("dump_" + name, list(shape), dt, kind="ExternalOutput").ap()
        self.dump_names.append("dump_" + name)
        self.body("sync", lambda eng: self.dS(eng, o, ap) and None)

    def drain(self, gens, n):
        for _ in range(n):
            for g in list(gens):
                try:
                    next(g)
                    break
                except StopIteration:
                    gens.remove(g)

    def drain_all(self, gens):
        for g in list(gens):
            for _ in g:
                pass
            gens.remove(g)

    def emit_all(self):
        cfg = self.cfg
        self.ph_init()
        if 1 in cfg["layers"]:
            self.ph_prep_wsum()
        self._g0 = [self.gen_prep_l0()] if 0 in cfg["layers"] else []
        self._g1 = [self.gen_prep_l1()] if 1 in cfg["layers"] else []
        for s in range(cfg["nseq"]):
            self.ph_load_x(s)
            for L in cfg["layers"]:
                self.ph_layernorm()
                self.dump(f"xn{L}_s{s}", self.xn[:].rearrange("p c q -> p (c q)"),
                          (128, NC8 * self.S), BF16)
                if not cfg.get("skip_attn"):
                    self.ph_project_v(L)
                    self.ph_attention(L, drains=(self._g0 if L == 0 else self._g1) if s == 0 else None)
                if L == 0:
                    self.drain_all(self._g0)
                    if not cfg.get("skip_attn"):
                        self.dump(f"afm_s{s}", self.attn_fm[:].rearrange("p c q -> p (c q)"),
                                  (128, NC8 * self.S), BF16)
                        self.ph_l0_oproj()
                    if not cfg.get("skip_ffn"):
                        self.ph_l0_ffn()
                else:
                    self.drain_all(self._g1)
                    if not cfg.get("skip_attn"):
                        self.ph_l1_scatter()
                    if not cfg.get("skip_ffn"):
                        self.ph_l1_ffn()
                self.dump(f"x{L}_s{s}", self.x[:].rearrange("p c q -> p (c q)"),
                          (128, NC8 * self.S), F32)
            self.ph_store_x(s)
        self.ph_final()

    def ph_init(self):
        def fv(eng):
            eng.memset(self.consts[:, 0:1], -EXP_C)
            eng.memset(self.consts[:, 1:2], 0.0)
            eng.memset(self.consts[:, 2:3], 1e-5)
            eng.memset(self.ones1[:], 1.0)
            eng.memset(self.onesK[:], 1.0)
            eng.memset(self.eA[:, 0:64], 1.0)
            eng.memset(self.eA[:, 64:128], 0.0)
            eng.memset(self.eB[:, 0:64], 0.0)
            eng.memset(self.eB[:, 64:128], 1.0)
            eng.memset(self.vpair[:, :, 64:65].rearrange("p a b -> p (a b)"), 1.0)
            return eng.memset(self.vpair[:, :, 129:130].rearrange("p a b -> p (a b)"), 1.0)
        self.body("vector", fv)

        def fs(eng):
            self.dS(eng, self.mskt[:], self.msk.rearrange("m p r -> p m r"))
            self.dS(eng, self.b1_sb[:], self.b1h)
            self.dS(eng, self.b2_sb[:], self.b2h)
        self.body("sync", fs)

    def gen_prep_l0(self):
        wflat = self.wbufA[:].rearrange("p c v -> p (c v)")
        pin = wflat[:, 0:2048].rearrange("p (s w) -> p s w", s=2)
        sg = wflat[:, 2048:6144].rearrange("p (s w) -> p s w", s=4)
        alf = self.qsf[:, 0, 0:4]
        snapP = self.snap()
        first = True
        for wn, scr, R, F in [(self.w1n, self.bw1T_scr, DFF0, D),
                              (self.w2n, self.bw2T_scr, D, DFF0)]:
            nchunk = R // 128
            ncb = F // 1024
            wv = wn.rearrange("(c p) f -> p c f", p=128)
            scrTv = scr.rearrange("(b p) f -> p b f", p=128)
            sPrev = getattr(self, "_g0_sprev", 0)
            for c in range(nchunk):
                for b in range(ncb):
                    cw = self.cWG[b % 2]
                    wts = list(snapP) if first else [(self.cA, self.cA.n), (self.cV, self.cV.n)]
                    first = False
                    def fdma(eng, c=c, b=b, cw=cw, wv=wv):
                        cw.inc(eng.dma_start(out=pin[:, b % 2, :],
                                             in_=wv[:, c, b * 1024:(b + 1) * 1024]), 16)
                    self.body("gpsimd", fdma, waits=wts)
                    wD = cw.n
                    def fred(eng, b=b, F=F):
                        ins = eng.tensor_reduce(out=alf[:, 2:3], in_=pin[:, b % 2, :],
                                                axis=AX.X, op=ALU.add, apply_absolute_value=True)
                        self.ve(eng, ins)
                        if b == 0:
                            return eng.tensor_scalar(out=alf[:, 0:1], in0=alf[:, 2:3],
                                                     scalar1=1.0 / F, scalar2=None, op0=ALU.mult)
                        return eng.scalar_tensor_tensor(out=alf[:, 0:1], in0=alf[:, 2:3],
                                                        scalar=1.0 / F, in1=alf[:, 0:1],
                                                        op0=ALU.mult, op1=ALU.add)
                    self.body("vector", fred, waits=[(cw, wD), (self.cV, self.cV.n)])
                    self.body("scalar", lambda eng, b=b: eng.activation(
                        out=sg[:, b, :], in_=pin[:, b % 2, :], func=AF.Sign,
                        bias=self.consts[:, 1:2], scale=1.0),
                        waits=[(cw, wD), (self.cA, self.cA.n), (self.cS, sPrev)])
                aSg = self.cA.n
                def fmul(eng, ncb=ncb):
                    last = None
                    for b in range(ncb):
                        last = eng.tensor_scalar(out=sg[:, b, :], in0=sg[:, b, :],
                                                 scalar1=alf[:, 0:1], scalar2=None, op0=ALU.mult)
                        self.ve(eng, last)
                    return eng.nop()
                self.body("vector", fmul, waits=[(self.cA, aSg), (self.cV, self.cV.n)])
                vMul = self.cV.n
                def ftr(eng, c=c, ncb=ncb, scrTv=scrTv):
                    for b in range(ncb):
                        for j in range(8):
                            self.dS(eng, self.qsc[:, 3, j * 128:(j + 1) * 128],
                                    sg[:, b, j * 128:(j + 1) * 128], transpose=True)
                        eng.wait_ge(self.cS.sem, self.cS.n)
                        self.dS(eng, scrTv[:, b * 8:b * 8 + 8, c * 128:(c + 1) * 128],
                                self.qsc[:, 3, :].rearrange("p (j i) -> p j i", i=128))
                        eng.wait_ge(self.cS.sem, self.cS.n)
                self.body("sync", ftr, waits=[(self.cV, vMul), (self.cS, self.cS.n)])
                sPrev = self.cS.n
                self._g0_sprev = sPrev
                yield

    def gen_prep_l1(self):
        pin = self.wbufA[:].rearrange("p c v -> p (c v)")[:, 4096:6144].rearrange("p (s w) -> p s w", s=2)
        for w, scrT, R, C in [(self.wg, self.wgT_scr, DFF1, D),
                              (self.wu, self.wuT_scr, DFF1, D),
                              (self.wd, self.wdT_scr, D, DFF1)]:
            nchunk = R // 128
            wv = w.rearrange("(c p) f -> p c f", p=128)
            scrTv = scrT.rearrange("(b p) f -> p b f", p=128)
            cbs = []
            o = 0
            while o < C:
                wdt = min(1024, C - o)
                cbs.append((o, wdt))
                o += wdt
            blocks = [(c, co, wdt) for c in range(nchunk) for (co, wdt) in cbs]
            snapP = self.snap()
            vQ = {}
            for i, (c, co, wdt) in enumerate(blocks):
                cw = self.cWG[i % 2]
                wts = list(snapP) if i < 2 else [(self.cV, vQ[i - 2])]
                self.body("gpsimd", lambda eng, c=c, co=co, wdt=wdt, i=i, cw=cw, wv=wv: cw.inc(
                    eng.dma_start(out=pin[:, i % 2, 0:wdt], in_=wv[:, c, co:co + wdt]), 16) and None,
                    waits=wts, dma_inc=0)
                wD = cw.n
                self._prep_quant_wait = (cw, wD)
                self.emit_quant([pin[:, i % 2, 0:wdt]], None, wdt, twist=False)
                vQ[i] = self.cV.n
                npc = wdt // 128
                def ftr(eng, npc=npc):
                    for j in range(npc):
                        self.dS(eng, self.qsc[:, 3, j * 128:(j + 1) * 128],
                                self.qsc[:, 1, j * 128:(j + 1) * 128], transpose=True)
                self.body("sync", ftr)
                self.body("sync", lambda eng, c=c, co=co, npc=npc, scrTv=scrTv: self.dS(
                    eng, scrTv[:, co // 128:co // 128 + npc, c * 128:(c + 1) * 128],
                    self.qsc[:, 3, 0:npc * 128].rearrange("p (j i) -> p j i", i=128)) and None)
                yield

    def ph_prep_wsum(self):
        wv = self.wout.rearrange("n h d -> (n h) d").rearrange("(c p) d -> p c d", p=128)
        for c in range(NC8):
            self.body("gpsimd", lambda eng, c=c, wv=wv: self.dG(
                eng, self.prep_bf[:, 0:1024], wv[:, c, :]) and None)
            def ft(eng, c=c):
                eng.matmul(self.bank(0)[0:1, :], self.ones1[:], self.prep_bf[:, 0:512],
                           start=(c == 0), stop=(c == NC8 - 1))
                return eng.matmul(self.bank(1)[0:1, :], self.ones1[:], self.prep_bf[:, 512:1024],
                                  start=(c == 0), stop=(c == NC8 - 1))
            self.body("tensor", ft)
        def fev(eng):
            eng.copy(out=self.scrA[0:1, :], in_=self.bank(0)[0:1, :])
            return eng.copy(out=self.scrB[0:1, :], in_=self.bank(1)[0:1, :])
        self.body("scalar", fev)
        def fst(eng):
            eng.dma_start(out=self.row_scr[0:512], in_=self.scrA[0:1, :].rearrange("a b -> (a b)"))
            eng.dma_start(out=self.row_scr[512:1024], in_=self.scrB[0:1, :].rearrange("a b -> (a b)"))
            return eng.dma_start(out=self.wsum_c[:], in_=self.row_scr.rearrange("(c p) -> p c", p=128))
        def fst(eng):
            self.dS(eng, self.row_scr[0:512].unsqueeze(0), self.scrA[0:1, :])
            self.dS(eng, self.row_scr[512:1024].unsqueeze(0), self.scrB[0:1, :])
            eng.wait_ge(self.cS.sem, self.cS.n)
            ins = eng.dma_start(out=self.wsum_c[:], in_=self.row_scr.rearrange("(c p) -> p c", p=128),
                                allow_slow_non_contiguous=True)
            self.cS.inc(ins, 16)
        self.body("sync", fst)

    def ph_load_x(self, s):
        self.body("sync", lambda eng, s=s: self.dS(
            eng, self.x[:], self.x_in[s].rearrange("(c p) q -> p c q", p=128)) and None)

    def ph_layernorm(self):
        S, NQT = self.S, self.NQT
        inv = 1.0 / D
        for qt in range(NQT):
            b0, b1 = (0, 1) if qt % 2 == 0 else (2, 3)
            for dc in range(NC8):
                o = 512 * (dc % 2)
                def fa(eng, qt=qt, dc=dc, o=o):
                    eng.copy(out=self.qsc[:, 0, o:o + 512], in_=self.x[:, dc, qt * 512:(qt + 1) * 512])
                    return eng.activation(out=self.qsc[:, 1, o:o + 512],
                                          in_=self.x[:, dc, qt * 512:(qt + 1) * 512],
                                          func=AF.Square, bias=self.consts[:, 1:2], scale=1.0)
                self.body("scalar", fa)
                def ft(eng, dc=dc, o=o, b0=b0, b1=b1):
                    eng.matmul(self.bank(b0)[0:1, :], self.ones1[:], self.qsc[:, 0, o:o + 512],
                               start=(dc == 0), stop=(dc == NC8 - 1))
                    return eng.matmul(self.bank(b1)[0:1, :], self.ones1[:], self.qsc[:, 1, o:o + 512],
                                      start=(dc == 0), stop=(dc == NC8 - 1))
                self.body("tensor", ft)
            def fe(eng, qt=qt, b0=b0, b1=b1):
                eng.mul(out=self.st_sb[0:1, qt * 512:(qt + 1) * 512],
                        in_=self.bank(b0)[0:1, :], mul=inv)
                return eng.mul(out=self.st_sb[0:1, S + qt * 512:S + (qt + 1) * 512],
                               in_=self.bank(b1)[0:1, :], mul=inv)
            self.body("scalar", fe)
        def fv(eng):
            ins = eng.tensor_tensor(out=self.rowb[0:1, S:2 * S], in0=self.st_sb[0:1, 0:S],
                                    in1=self.st_sb[0:1, 0:S], op=ALU.mult)
            self.ve(eng, ins)
            return eng.tensor_tensor(out=self.st_sb[0:1, S:2 * S], in0=self.st_sb[0:1, S:2 * S],
                                     in1=self.rowb[0:1, S:2 * S], op=ALU.subtract)
        self.body("vector", fv)
        def fs(eng):
            ins = eng.activation(out=self.st_sb[0:1, S:2 * S], in_=self.st_sb[0:1, S:2 * S],
                                 func=AF.Ln, bias=self.consts[0:1, 2:3], scale=1.0)
            self.ae(eng, ins)
            return eng.activation(out=self.rowb[0:1, 0:S], in_=self.st_sb[0:1, S:2 * S],
                                  func=AF.Exp, bias=self.consts[0:1, 1:2], scale=-0.5)
        self.body("scalar", fs)
        self.body("vector", lambda eng: eng.scalar_tensor_tensor(
            out=self.rowb[0:1, S:2 * S], in0=self.st_sb[0:1, 0:S], scalar=-1.0,
            in1=self.rowb[0:1, 0:S], op0=ALU.mult, op1=ALU.mult))
        for qt in range(NQT):
            b0, b1 = (0, 1) if qt % 2 == 0 else (2, 3)
            def fbc(eng, qt=qt, b0=b0, b1=b1):
                eng.matmul(self.bank(b0), self.onesK[:], self.rowb[0:1, qt * 512:(qt + 1) * 512],
                           start=True, stop=True)
                return eng.matmul(self.bank(b1), self.onesK[:],
                                  self.rowb[0:1, S + qt * 512:S + (qt + 1) * 512],
                                  start=True, stop=True)
            self.body("tensor", fbc)
            def fap(eng, qt=qt, b0=b0, b1=b1):
                last = None
                for dc in range(NC8):
                    ins = eng.tensor_tensor(out=self.scrA[:],
                                            in0=self.x[:, dc, qt * 512:(qt + 1) * 512],
                                            in1=self.bank(b0), op=ALU.mult)
                    self.ve(eng, ins)
                    last = eng.tensor_tensor(out=self.xn[:, dc, qt * 512:(qt + 1) * 512],
                                             in0=self.scrA[:], in1=self.bank(b1), op=ALU.add)
                    if dc < NC8 - 1:
                        self.ve(eng, last)
                return last
            self.body("vector", fap)

    def ph_project_v(self, L):
        S = self.S
        if L == 0:
            self.body("gpsimd", lambda eng: self.dG(
                eng, self.wbufA[:], self.wvT.rearrange("(c p) v -> p c v", p=128)) and None)
        else:
            def fwv(eng):
                for c in range(NC8):
                    self.dG(eng, self.wbufA[:, c, :].rearrange("p (n h) -> p n h", n=NH),
                            self.wqkv[:, :, 128:192].rearrange("n (c p) h -> p c n h", p=128)[:, c, :, :])
            self.body("gpsimd", fwv)
        snapV = self.snap()
        aE = {}
        sD = {}
        for qc in range(S // 128):
            b0, b1 = (0, 1) if qc % 2 == 0 else (2, 3)
            wts = list(snapV) if qc < 2 else [(self.cA, aE[qc - 2])]
            def ft(eng, qc=qc, b0=b0, b1=b1):
                last = None
                for half, b in ((0, b0), (1, b1)):
                    for dc in range(NC8):
                        last = eng.matmul(self.bank(b), self.xn[:, dc, qc * 128:(qc + 1) * 128],
                                          self.wbufA[:, dc, half * 512:(half + 1) * 512],
                                          start=(dc == 0), stop=(dc == NC8 - 1))
                return last
            self.body("tensor", ft, waits=wts)
            tT = self.cT.n
            o = 1024 * (qc % 2)
            wts = [(self.cT, tT)] + ([(self.cS, sD[qc - 2])] if qc >= 2 else [])
            def fe(eng, b0=b0, b1=b1, o=o):
                eng.copy(out=self.qsc[:, 0, :].rearrange("p a -> p a")[:, 0:0] if False else
                         self.qsc[:, 0 + 2 * (o // 1024), :][:, 0:512], in_=self.bank(b0))
                return eng.copy(out=self.qsc[:, 0 + 2 * (o // 1024), :][:, 512:1024], in_=self.bank(b1))
            self.body("scalar", fe, waits=wts)
            aE[qc] = self.cA.n
            self.body("sync", lambda eng, qc=qc, o=o: self.dS(
                eng, self.v_scr[qc * 128:(qc + 1) * 128, :],
                self.qsc[:, 0 + 2 * (o // 1024), :]) and None,
                waits=[(self.cA, aE[qc]), (self.cS, self.cS.n)])
            sD[qc] = self.cS.n

    def ph_attention(self, L, drains=None):
        S, NQT = self.S, self.NQT
        self._at = dict(fv=self.cV.n, fa2=self.cA.n)
        for hp in range(NH // 2):
            if L == 0:
                def fwl(eng, hp=hp):
                    self.dG(eng, self.wbufB[:, 0:8, :],
                            self.wqT.rearrange("(c p) k -> p c k", p=128)[:, :, hp * 128:(hp + 1) * 128])
                    self.dG(eng, self.wbufB[:, 8:16, :],
                            self.wkT.rearrange("(c p) k -> p c k", p=128)[:, :, hp * 128:(hp + 1) * 128])
            else:
                def fwl(eng, hp=hp):
                    for e in range(2):
                        self.dG(eng, self.wbufB[:, 0:8, e * 64:(e + 1) * 64],
                                self.wqkv[2 * hp + e, :, 0:64].rearrange("(c p) h -> p c h", p=128))
                        self.dG(eng, self.wbufB[:, 8:16, e * 64:(e + 1) * 64],
                                self.wqkv[2 * hp + e, :, 64:128].rearrange("(c p) h -> p c h", p=128))
            self.body("gpsimd", fwl)
            def fvl(eng, hp=hp):
                self.dS(eng, self.vpair[:, :, 0:64],
                        self.v_scr[:, hp * 128:hp * 128 + 64].rearrange("(k p) v -> p k v", p=128))
                self.dS(eng, self.vpair[:, :, 65:129],
                        self.v_scr[:, hp * 128 + 64:hp * 128 + 128].rearrange("(k p) v -> p k v", p=128))
            self.body("sync", fvl)
            cG_h = self.cG.n
            cS_h = self.cS.n
            snapH = self.snap()
            kev = {}
            for qt in range(NQT):
                bk = 4 if qt % 2 == 0 else 5
                if qt < 2:
                    wts = [(self.cG, cG_h), (self.cA, self._at["fa2"]), (self.cV, self._at["fv"])]
                else:
                    wts = [(self.cA, kev[qt - 2])]
                def ftk(eng, qt=qt, bk=bk):
                    last = None
                    for dc in range(NC8):
                        last = eng.matmul(self.bank(bk), self.wbufB[:, 8 + dc, :],
                                          self.xn[:, dc, qt * 512:(qt + 1) * 512],
                                          start=(dc == 0), stop=(dc == NC8 - 1))
                    return last
                self.body("tensor", ftk, waits=wts)
                tK = self.cT.n
                self.body("scalar", lambda eng, qt=qt, bk=bk: eng.copy(
                    out=self.kpair[:, qt * 512:(qt + 1) * 512], in_=self.bank(bk)),
                    waits=[(self.cT, tK), (self.cA, self.cA.n)])
                kev[qt] = self.cA.n
            for qt in range(NQT):
                nkk = 4 * (qt + 1)
                def ftq(eng, qt=qt):
                    last = None
                    for dc in range(NC8):
                        last = eng.matmul(self.bank(6), self.wbufB[:, dc, :],
                                          self.xn[:, dc, qt * 512:(qt + 1) * 512],
                                          start=(dc == 0), stop=(dc == NC8 - 1))
                    return last
                self.body("tensor", ftq,
                          waits=[(self.cG, cG_h), (self.cA, self._at["fa2"]), (self.cA, kev[qt])])
                tQ = self.cT.n
                self.body("scalar", lambda eng: eng.mul(out=self.qsmall[:], in_=self.bank(6), mul=0.125),
                          waits=[(self.cT, tQ), (self.cA, self.cA.n)])
                self._at["qev"] = self.cA.n
                self._at["cS_h"] = cS_h
                if L == 0:
                    self.attn_l0_tile(hp, qt, nkk)
                else:
                    self.attn_l1_tile(hp, qt, nkk)
                if drains:
                    self.drain(drains, 10 if L == 0 else 2)

    def _emit_sgroup(self, eng, g, qt, base=0):
        last = None
        for j in range(2):
            kc = 2 * g + j
            last = eng.matmul(self.bank(base + j), self.kpair[0:64, kc * 128:(kc + 1) * 128],
                              self.qsmall[0:64, :], start=True, stop=True)
            last = eng.matmul(self.bank(base + 2 + j), self.kpair[64:128, kc * 128:(kc + 1) * 128],
                              self.qsmall[64:128, :], start=True, stop=True)
        return last

    def _group_has_mask(self, g, qt):
        return any(0 <= (2 * g + j) * 128 - qt * 512 < 512 for j in range(2))

    def _emit_masks(self, eng, g, qt, base=0):
        last = None
        for j in range(2):
            kc = 2 * g + j
            o = kc * 128 - qt * 512
            if 0 <= o < 512:
                for hb in (base + j, base + 2 + j):
                    last = eng.tensor_tensor(out=self.bank(hb), in0=self.bank(hb),
                                             in1=self.mskt[:, o // 128, :], op=ALU.add)
        return last

    def attn_l0_tile(self, hp, qt, nkk):
        ng = nkk // 2
        snap0 = self.snap()
        tS = []
        base_V = self.cV.n
        base_A = self.cA.n
        hasm = [self._group_has_mask(g, qt) for g in range(ng)]
        masked = [g for g in range(ng) if hasm[g]]
        vM = {}
        for i, g in enumerate(masked):
            vM[g] = base_V + i + 1
        aE = [base_A + g + 1 for g in range(ng)]

        ftw = [(self.cA, self._at["qev"]), (self.cS, self._at["cS_h"]),
               (self.cV, self._at["fv"])]

        def emit_av(eng, g):
            last = None
            sl = 1024 * (g % 2)
            for j in range(2):
                kc = 2 * g + j
                first = (kc == 0)
                lastk = (kc == nkk - 1)
                eng.matmul(self.bank(4)[0:64, :], self.vpair[:, kc, 0:64],
                           self.ptA_g[:, sl + j * 512:sl + (j + 1) * 512],
                           start=first, stop=lastk, tile_position=(0, 0))
                eng.matmul(self.bank(4)[64:128, :], self.vpair[:, kc, 65:129],
                           self.ptB_g[:, sl + j * 512:sl + (j + 1) * 512],
                           start=first, stop=lastk, tile_position=(0, 64))
                eng.matmul(self.bank(5)[0:1, :], self.ones1[:],
                           self.ptA_g[:, sl + j * 512:sl + (j + 1) * 512],
                           start=first, stop=lastk)
                last = eng.matmul(self.bank(6)[0:1, :], self.ones1[:],
                                  self.ptB_g[:, sl + j * 512:sl + (j + 1) * 512],
                                  start=first, stop=lastk)
            return last

        def ft(eng):
            last = self._emit_sgroup(eng, 0, qt)
            self.cT.inc(last)
            tS.append(self.cT.n)
            for g in range(1, ng):
                eng.wait_ge(self.cA.sem, aE[g - 1])
                last = self._emit_sgroup(eng, g, qt)
                self.cT.inc(last)
                tS.append(self.cT.n)
                last = emit_av(eng, g - 1)
                if g < ng - 1:
                    self.cT.inc(last)
            eng.wait_ge(self.cA.sem, aE[ng - 1])
            return emit_av(eng, ng - 1)
        self.body("tensor", ft, waits=ftw)

        def fv(eng):
            last = None
            for g in masked:
                eng.wait_ge(self.cT.sem, tS[g])
                last = self._emit_masks(eng, g, qt)
                if g != masked[-1]:
                    self.cV.inc(last)
            return last if last is not None else eng.nop()
        self.body("vector", fv, waits=snap0)
        assert self.cV.n == base_V + max(len(masked), 1), (self.cV.n, base_V, masked)

        def fa(eng):
            last = None
            for g in range(ng):
                eng.wait_ge(self.cT.sem, tS[g])
                if hasm[g]:
                    eng.wait_ge(self.cV.sem, vM[g])
                sl = 1024 * (g % 2)
                eng.activation(out=self.ptA_g[:, sl:sl + 1024], in_=self.psum[:, 0:1024],
                               func=AF.Exp, bias=self.consts[:, 0:1], scale=1.0)
                last = eng.activation(out=self.ptB_g[:, sl:sl + 1024], in_=self.psum[:, 1024:2048],
                                      func=AF.Exp, bias=self.consts[:, 0:1], scale=1.0)
                if g < ng - 1:
                    self.cA.inc(last)
            return last
        self.body("scalar", fa, waits=snap0)
        assert self.cA.n == aE[-1], (self.cA.n, aE)

        def fa2(eng):
            eng.copy(out=self.dAB[0:1, 0:512], in_=self.bank(5)[0:1, :])
            return eng.copy(out=self.dAB[0:1, 512:1024], in_=self.bank(6)[0:1, :])
        self.body("scalar", fa2)
        self._at["fa2"] = self.cA.n
        def ft2(eng):
            eng.matmul(self.bank(7), self.eA[:], self.dAB[0:1, 0:512], start=True, stop=False)
            return eng.matmul(self.bank(7), self.eB[:], self.dAB[0:1, 512:1024], start=False, stop=True)
        self.body("tensor", ft2)
        def fa3(eng):
            ins = eng.activation(out=self.scrA[:], in_=self.bank(7), func=AF.Ln,
                                 bias=self.consts[:, 1:2], scale=1.0)
            self.ae(eng, ins)
            return eng.activation(out=self.scrB[:], in_=self.scrA[:], func=AF.Exp,
                                  bias=self.consts[:, 1:2], scale=-1.0)
        self.body("scalar", fa3)
        self.body("vector", lambda eng: eng.tensor_tensor(
            out=self.attn_fm[:, hp, qt * 512:(qt + 1) * 512], in0=self.bank(4),
            in1=self.scrB[:], op=ALU.mult))
        self._at["fv"] = self.cV.n

    def attn_l1_tile(self, hp, qt, nkk):
        ng = nkk // 2
        snap0 = self.snap()
        tS = []
        base_V = self.cV.n
        base_A = self.cA.n
        hasm = [self._group_has_mask(g, qt) for g in range(ng)]
        masked = [g for g in range(ng) if hasm[g]]
        vM = {}
        for i, g in enumerate(masked):
            vM[g] = base_V + i + 1
        aE = [base_A + g + 1 for g in range(ng)]
        bb = lambda g: 4 * (g % 2)

        ftw = [(self.cA, self._at["qev"]), (self.cS, self._at["cS_h"])]

        def ft(eng):
            last = None
            for g in range(ng):
                if g >= 2:
                    eng.wait_ge(self.cA.sem, aE[g - 2])
                last = self._emit_sgroup(eng, g, qt, base=bb(g))
                if g < ng - 1:
                    self.cT.inc(last)
                tS.append(self.cT.n + (1 if g == ng - 1 else 0))
            return last
        self.body("tensor", ft, waits=ftw)

        def fv(eng):
            last = None
            for g in masked:
                eng.wait_ge(self.cT.sem, tS[g])
                last = self._emit_masks(eng, g, qt, base=bb(g))
                if g != masked[-1]:
                    self.cV.inc(last)
            return last if last is not None else eng.nop()
        self.body("vector", fv, waits=snap0)

        def fa(eng):
            last = None
            for g in range(ng):
                eng.wait_ge(self.cT.sem, tS[g])
                if hasm[g]:
                    eng.wait_ge(self.cV.sem, vM[g])
                eng.activation(out=self.ptA_f[:, 2 * g:2 * g + 2, :].rearrange("p a b -> p (a b)"),
                               in_=self.psum[:, bb(g) * 512:(bb(g) + 2) * 512],
                               func=AF.Exp, bias=self.consts[:, 0:1], scale=1.0)
                last = eng.activation(out=self.ptB_f[:, 2 * g:2 * g + 2, :].rearrange("p a b -> p (a b)"),
                                      in_=self.psum[:, (bb(g) + 2) * 512:(bb(g) + 4) * 512],
                                      func=AF.Exp, bias=self.consts[:, 0:1], scale=1.0)
                if g < ng - 1:
                    self.cA.inc(last)
            return last
        self.body("scalar", fa, waits=snap0)

        expL = self.cA.n
        expL = self.cA.n
        for qb in range(4):
            wts = [(self.cA, expL), (self.cA, self._at["fa2"]),
                   (self.cV, self._at["fv"]), (self.cS, self._at["cS_h"])]
            def ftav(eng, qb=qb):
                last = None
                for pt, b, lo in ((self.ptA_f, 4, 0), (self.ptB_f, 5, 65)):
                    for kc in range(nkk):
                        last = eng.matmul(self.bank(b, 0, 65),
                                          pt[:, kc, qb * 128:(qb + 1) * 128],
                                          self.vpair[:, kc, lo:lo + 65],
                                          start=(kc == 0), stop=(kc == nkk - 1))
                return last
            self.body("tensor", ftav, waits=wts)
            def fa2(eng):
                ins = eng.activation(out=self.qsf[:, 0, 0:1], in_=self.bank(4, 64, 65),
                                     func=AF.Ln, bias=self.consts[:, 1:2], scale=1.0)
                self.ae(eng, ins)
                ins = eng.activation(out=self.qsf[:, 0, 1:2], in_=self.qsf[:, 0, 0:1],
                                     func=AF.Exp, bias=self.consts[:, 1:2], scale=-1.0)
                self.ae(eng, ins)
                ins = eng.activation(out=self.qsf[:, 0, 2:3], in_=self.bank(5, 64, 65),
                                     func=AF.Ln, bias=self.consts[:, 1:2], scale=1.0)
                self.ae(eng, ins)
                return eng.activation(out=self.qsf[:, 0, 3:4], in_=self.qsf[:, 0, 2:3],
                                      func=AF.Exp, bias=self.consts[:, 1:2], scale=-1.0)
            self.body("scalar", fa2)
            def fv2(eng):
                eng.tensor_scalar(out=self.tstage[:, 0, :], in0=self.bank(4, 0, 64),
                                  scalar1=self.qsf[:, 0, 1:2], scalar2=None, op0=ALU.mult)
                return eng.tensor_scalar(out=self.tstage[:, 1, :], in0=self.bank(5, 0, 64),
                                         scalar1=self.qsf[:, 0, 3:4], scalar2=None, op0=ALU.mult)
            self.body("vector", fv2)
            self._at["fv"] = self.cV.n
            self._at["fa2"] = self.cA.n
            sbase = qt * 512 + qb * 128
            self.body("sync", lambda eng, hp=hp, sbase=sbase: self.dS(
                eng, self.t_scr[2 * hp:2 * hp + 2, sbase:sbase + 128, :].rearrange("n (j p) h -> p j n h", p=128),
                self.tstage[:].unsqueeze(1)) and None)

    def ph_l0_oproj(self):
        NQT = self.NQT
        self.body("gpsimd", lambda eng: self.dG(
            eng, self.wbufA[:], self.woT.rearrange("(c p) d -> p c d", p=128)) and None)
        for qt in range(NQT):
            for db in range(NC8):
                b = (qt * NC8 + db) % 2
                def ft(eng, qt=qt, db=db, b=b):
                    last = None
                    for vc in range(NC8):
                        last = eng.matmul(self.bank(b), self.wbufA[:, vc, db * 128:(db + 1) * 128],
                                          self.attn_fm[:, vc, qt * 512:(qt + 1) * 512],
                                          start=(vc == 0), stop=(vc == NC8 - 1))
                    return last
                self.body("tensor", ft)
                self.body("vector", lambda eng, qt=qt, db=db, b=b: eng.tensor_tensor(
                    out=self.x[:, db, qt * 512:(qt + 1) * 512],
                    in0=self.x[:, db, qt * 512:(qt + 1) * 512],
                    in1=self.bank(b), op=ALU.add))

    def ph_l0_ffn(self):
        NQT = self.NQT
        bw1 = self.bw1T_scr.rearrange("(c p) f -> p c f", p=128)
        bw2 = self.bw2T_scr.rearrange("(c p) f -> p c f", p=128)
        w1v = self.wbufA[:].rearrange("p c v -> p (c v)")[:, 0:NC8 * 512].rearrange("p (c f) -> p c f", f=512)
        for qt in range(NQT):
            snapF = self.snap()
            tG = {}
            aG = {}
            tT1 = {}
            for hg in range(8):
                cw = self.cW[hg % 2]
                wts = list(snapF) if hg < 2 else [(self.cT, tT1[(hg - 2) * 4 + 3])]
                def fg(eng, hg=hg, cw=cw):
                    cw.inc(eng.dma_start(
                        out=self.wbufA[:, :, 512 * (hg % 2):512 * (hg % 2) + 512],
                        in_=bw1[:, :, hg * 512:(hg + 1) * 512]), 16)
                self.body("sync", fg, waits=wts)
                wG = cw.n
                for hj in range(4):
                    hc = hg * 4 + hj
                    b = hc % 2
                    wts = [(cw, wG)]
                    if hc >= 2:
                        wts.append((self.cA, aG[hc - 2]))
                    elif hg < 1:
                        wts += list(snapF)
                    def ft(eng, hj=hj, b=b, qt=qt, hg=hg):
                        last = None
                        for dc in range(NC8):
                            last = eng.matmul(
                                self.bank(b),
                                self.wbufA[:, dc, 512 * (hg % 2) + hj * 128:512 * (hg % 2) + (hj + 1) * 128],
                                self.xn[:, dc, qt * 512:(qt + 1) * 512],
                                start=(dc == 0), stop=(dc == NC8 - 1))
                        return last
                    self.body("tensor", ft, waits=wts)
                    tT1[hc] = self.cT.n
                    self.body("scalar", lambda eng, hc=hc, b=b: eng.activation(
                        out=self.h_l0[:, hc, :], in_=self.bank(b), func=AF.Gelu,
                        bias=self.b1_sb[:, hc:hc + 1], scale=1.0),
                        waits=[(self.cT, tT1[hc])] + ([(self.cA, self.cA.n)] if hc == 0 else []))
                    aG[hc] = self.cA.n
            tT2 = {}
            vU = {}
            aAll = self.cA.n
            for db in range(NC8):
                for half in range(2):
                    cw = self.cW[half]
                    wts = [(self.cT, tT2.get((db - 1, half), 0))] if db >= 1 else [(self.cT, self.cT.n)]
                    def fg2(eng, db=db, half=half, cw=cw):
                        cw.inc(eng.dma_start(
                            out=self.wbufB[:, 16 * half:16 * half + 16, :],
                            in_=bw2[:, 16 * half:16 * half + 16, db * 128:(db + 1) * 128]), 16)
                    self.body("sync", fg2, waits=wts)
                    tG[(db, half)] = cw.n
                b = 2 + db % 2
                wts = [(self.cW[0], tG[(db, 0)]), (self.cW[1], tG[(db, 1)]), (self.cA, aAll)]
                if db >= 2:
                    wts.append((self.cV, vU[db - 2]))
                def ft2(eng, db=db, b=b):
                    last = None
                    for hc in range(32):
                        last = eng.matmul(self.bank(b), self.wbufB[:, hc, :], self.h_l0[:, hc, :],
                                          start=(hc == 0), stop=(hc == 31))
                        if hc == 15:
                            pass
                    return last
                self.body("tensor", ft2, waits=wts)
                tT2[(db, 0)] = self.cT.n
                tT2[(db, 1)] = self.cT.n
                self.body("vector", lambda eng, qt=qt, db=db, b=b: eng.scalar_tensor_tensor(
                    out=self.x[:, db, qt * 512:(qt + 1) * 512], in0=self.bank(b),
                    scalar=self.b2_sb[:, db:db + 1], in1=self.x[:, db, qt * 512:(qt + 1) * 512],
                    op0=ALU.add, op1=ALU.add), waits=[(self.cT, tT2[(db, 0)]), (self.cV, self.cV.n)])
                vU[db] = self.cV.n

    def ph_l1_scatter(self):
        S = self.S
        for dc in range(NC8):
            G = S // 64
            def fy(eng, dc=dc, G=G):
                self.dS(eng, self.y_sb[0:64, 0:S].rearrange("p (c h) -> p c h", h=64),
                        self.t_scr[2 * dc].rearrange("(a c) h -> a c h", c=G))
                self.dS(eng, self.y_sb[64:128, 0:S].rearrange("p (c h) -> p c h", h=64),
                        self.t_scr[2 * dc + 1].rearrange("(a c) h -> a c h", c=G))
            self.body("sync", fy)
            self.body("vector", lambda eng, dc=dc: eng.scalar_tensor_tensor(
                out=self.x[:, dc, :], in0=self.y_sb[:, 0:S],
                scalar=self.wsum_c[:, dc:dc + 1], in1=self.x[:, dc, :],
                op0=ALU.mult, op1=ALU.add))

    def emit_quant(self, in_aps, out_ap, W, twist, waits=None):
        nb = W // 16
        q0 = self.qsc[:, 0, 0:W]
        extra_w = getattr(self, "_prep_quant_wait", None)
        self._prep_quant_wait = None
        if waits is None:
            waits = self.snap()
        fv1_waits = list(waits) + [(self.cV, self.cV.n), (self.cA, self.cA.n)]

        def fv1(eng):
            if extra_w is not None:
                eng.wait_ge(extra_w[0].sem, extra_w[1])
            if in_aps is not None:
                o = 0
                for ap in in_aps:
                    w = ap.shape[-1]
                    if twist:
                        ins = eng.transpose(out=self.qsc[:, 0, o:o + w], in_=ap)
                    else:
                        ins = eng.tensor_copy(out=self.qsc[:, 0, o:o + w], in_=ap)
                    o += w
                    self.ve(eng, ins)
            ins = eng.tensor_reduce(out=self.qsf[:, 0, 0:nb],
                                    in_=q0.rearrange("p (g s) -> p g s", s=16),
                                    axis=AX.X, op=ALU.max, apply_absolute_value=True)
            self.vq(eng, ins)
            return eng.tensor_scalar(out=self.qsf[:, 1, 0:nb], in0=self.qsf[:, 0, 0:nb],
                                     scalar1=1.0 / 6.0, scalar2=1e-12, op0=ALU.mult, op1=ALU.max)
        self.body("vector", fv1, waits=fv1_waits)
        cV1 = self.cV.n

        def fa1(eng):
            ins = eng.activation(out=self.qsf[:, 2, 0:nb], in_=self.qsf[:, 1, 0:nb],
                                 func=AF.Ln, bias=self.consts[:, 1:2], scale=1.0)
            self.ae(eng, ins)
            eng.activation(out=self.qsf[:, 3, 0:nb], in_=self.qsf[:, 2, 0:nb],
                           func=AF.Exp, bias=self.consts[:, 1:2], scale=-1.0)
            eng.activation(out=self.qsc[:, 1, 0:W], in_=q0,
                           func=AF.Abs, bias=self.consts[:, 1:2], scale=1.0)
            return eng.activation(out=self.qsc[:, 3, 0:W], in_=q0,
                                  func=AF.Sign, bias=self.consts[:, 1:2], scale=1.0)
        self.body("scalar", fa1, waits=[(self.cV, cV1), (self.cA, self.cA.n)])
        cA1 = self.cA.n

        def fv2(eng):
            a = self.qsc[:, 1, 0:W]
            st = self.qsc[:, 2, 0:W]
            a3 = lambda t: t.rearrange("p (g s) -> p g s", s=16)
            rsc = self.qsf[:, 3, 0:nb].unsqueeze(2).broadcast_to((128, nb, 16))
            ins = eng.tensor_tensor(out=a3(a), in0=a3(a), in1=rsc, op=ALU.mult)
            self.vq(eng, ins)
            ins = eng.tensor_scalar(out=st, in0=a, scalar1=2.5, scalar2=None, op0=ALU.is_gt)
            self.vq(eng, ins)
            for th in (3.5, 5.0, 5.0):
                ins = eng.scalar_tensor_tensor(out=st, in0=a, scalar=th, in1=st,
                                               op0=ALU.is_gt, op1=ALU.add)
                self.vq(eng, ins)
            ins = eng.tensor_scalar(out=st, in0=st, scalar1=2.0, scalar2=None, op0=ALU.mult)
            self.vq(eng, ins)
            for th in (0.25, 0.75, 1.25, 1.75):
                ins = eng.scalar_tensor_tensor(out=st, in0=a, scalar=th, in1=st,
                                               op0=ALU.is_gt, op1=ALU.add)
                self.vq(eng, ins)
            ins = eng.tensor_scalar(out=self.qsf[:, 2, 0:nb], in0=self.qsf[:, 1, 0:nb],
                                    scalar1=0.5, scalar2=None, op0=ALU.mult)
            self.vq(eng, ins)
            hsc = self.qsf[:, 2, 0:nb].unsqueeze(2).broadcast_to((128, nb, 16))
            ins = eng.tensor_tensor(out=a3(st), in0=a3(st), in1=hsc, op=ALU.mult)
            self.vq(eng, ins)
            ins = eng.tensor_tensor(out=a, in0=st, in1=self.qsc[:, 3, 0:W], op=ALU.mult)
            self.ve(eng, ins)
            if twist:
                ins = eng.transpose(out=out_ap, in_=a)
                self.ve(eng, ins)
                return eng.nop()
            if out_ap is None:
                return eng.nop()
            return eng.tensor_copy(out=out_ap, in_=a)
        self.body("vector", fv2, waits=[(self.cA, cA1)])

    def ph_l1_ffn(self):
        NQT = self.NQT
        wgT = self.wgT_scr.rearrange("(c p) f -> p c f", p=128)
        wuT = self.wuT_scr.rearrange("(c p) f -> p c f", p=128)
        wdT = self.wdT_scr.rearrange("(c p) f -> p c f", p=128)
        for qt in range(NQT):
            qts = slice(qt * 512, (qt + 1) * 512)
            for pp in range(NC8 // 2):
                self.emit_quant([self.xn[:, 2 * pp, qts], self.xn[:, 2 * pp + 1, qts]],
                                self.xq[:, 2 * pp:2 * pp + 2, :].rearrange("p a b -> p (a b)"),
                                1024, twist=True, waits=[])
            tT, tA, tV, tS = {}, {}, {}, {}
            snapA = self.snap()
            snapA_cV = self.cV.n
            for ic in range(NI):
                sl = 16 * (ic % 2)
                bg, bu = (0, 1) if ic % 2 == 0 else (2, 3)
                wts = list(snapA)
                if ic >= 2:
                    wts = [(self.cT, tT[ic - 2])]
                cw = self.cW[ic % 2]
                def fwg(eng, ic=ic, sl=sl, cw=cw):
                    cw.inc(eng.dma_start(out=self.wbufB[:, sl:sl + 8, :],
                                         in_=wgT[:, :, ic * 128:(ic + 1) * 128]), 16)
                    cw.inc(eng.dma_start(out=self.wbufB[:, sl + 8:sl + 16, :],
                                         in_=wuT[:, :, ic * 128:(ic + 1) * 128]), 16)
                self.body("sync", fwg, waits=wts)
                tS[ic] = cw.n
                wts = [(cw, tS[ic])]
                if ic >= 2:
                    wts.append((self.cV, tV[ic - 2]))
                else:
                    wts.append((self.cV, snapA_cV))
                def ft(eng, ic=ic, sl=sl, bg=bg, bu=bu):
                    for dc in range(NC8):
                        eng.matmul(self.bank(bg), self.wbufB[:, sl + dc, :], self.xq[:, dc, :],
                                   start=(dc == 0), stop=(dc == NC8 - 1))
                    last = None
                    for dc in range(NC8):
                        last = eng.matmul(self.bank(bu), self.wbufB[:, sl + 8 + dc, :], self.xq[:, dc, :],
                                          start=(dc == 0), stop=(dc == NC8 - 1))
                    return last
                self.body("tensor", ft, waits=wts)
                tT[ic] = self.cT.n
                wts = [(self.cT, tT[ic])]
                if ic >= 1:
                    wts.append((self.cV, tV[ic - 1]))
                self.body("scalar", lambda eng, bg=bg: eng.activation(
                    out=self.scrA[:], in_=self.bank(bg), func=AF.Silu,
                    bias=self.consts[:, 1:2], scale=1.0), waits=wts)
                tA[ic] = self.cA.n
                self.body("vector", lambda eng, ic=ic, bu=bu: eng.tensor_tensor(
                    out=self.hq[:, ic, :], in0=self.scrA[:], in1=self.bank(bu), op=ALU.mult),
                    waits=[(self.cA, tA[ic])])
                tV[ic] = self.cV.n
                if ic % 2 == 1:
                    pp = ic // 2
                    self.emit_quant([self.hq[:, 2 * pp:2 * pp + 2, :].rearrange("p a b -> p (a b)")],
                                    self.hq[:, 2 * pp:2 * pp + 2, :].rearrange("p a b -> p (a b)"),
                                    1024, twist=True, waits=[])
            assert NI % 2 == 0
            for db in range(NC8):
                self.body("sync", lambda eng, db=db: self.dS(
                    eng, self.wbufB[:, 0:NI, :], wdT[:, :, db * 128:(db + 1) * 128]) and None)
                b = 2 + db % 2
                def ft2(eng, b=b):
                    last = None
                    for ic in range(NI):
                        last = eng.matmul(self.bank(b), self.wbufB[:, ic, :], self.hq[:, ic, :],
                                          start=(ic == 0), stop=(ic == NI - 1))
                    return last
                self.body("tensor", ft2)
                self.body("vector", lambda eng, qt=qt, db=db, b=b: eng.tensor_tensor(
                    out=self.x[:, db, qt * 512:(qt + 1) * 512],
                    in0=self.x[:, db, qt * 512:(qt + 1) * 512],
                    in1=self.bank(b), op=ALU.add))

    def ph_store_x(self, s):
        self.body("sync", lambda eng, s=s: self.dS(
            eng, self.out[s].rearrange("(c p) q -> p c q", p=128), self.x[:]) and None)

    def ph_final(self):
        snap = self.snap()
        eng = Rec(self.streams["sync"])
        for c, n in snap:
            if n > 0:
                eng.wait_ge(c.sem, n)


def build_masks():
    m = np.zeros((4, 128, 512), dtype=np.float32)
    for oi, o in enumerate((0, 128, 256, 384)):
        i = np.arange(128)[:, None]
        j = np.arange(512)[None, :]
        m[oi] = np.where(i + o <= j, 0.0, NEG)
    return m.astype(ml_dtypes.bfloat16)


def host_inputs(inputs, cfg, core):
    nseq = cfg["nseq"]
    x = np.asarray(inputs["L0_attn_x"], dtype=np.float32)
    xs = x[core * nseq:(core + 1) * nseq]
    x_t = np.ascontiguousarray(xs.transpose(0, 2, 1))
    f32 = lambda a: np.ascontiguousarray(np.asarray(a, dtype=np.float32))
    tr = lambda a: np.ascontiguousarray(np.asarray(a, dtype=np.float32).T)
    return {
        "x_t": x_t,
        "msk": build_masks(),
        "L0_wqT": tr(inputs["L0_attn_q_proj_weight"]),
        "L0_wkT": tr(inputs["L0_attn_k_proj_weight"]),
        "L0_wvT": tr(inputs["L0_attn_v_proj_weight"]),
        "L0_woT": tr(inputs["L0_attn_o_proj_weight"]),
        "L0_w1": f32(inputs["L0_ffn_w1"]),
        "L0_w2": f32(inputs["L0_ffn_w2"]),
        "L0_b1h": np.ascontiguousarray(f32(inputs["L0_ffn_b1"]).reshape(32, 128).T),
        "L0_b2h": np.ascontiguousarray(f32(inputs["L0_ffn_b2"]).reshape(8, 128).T),
        "L1_wqkv": f32(inputs["L1_attn_W_qkv"]),
        "L1_wout": f32(inputs["L1_attn_W_out"]),
        "L1_wg": f32(inputs["L1_ffn_w_gate"]),
        "L1_wu": f32(inputs["L1_ffn_w_up"]),
        "L1_wd": f32(inputs["L1_ffn_w_down"]),
    }


def kernel(**inputs):
    cfg = default_cfg()
    p = Prog(cfg)
    nc = p.build()
    in_maps = [host_inputs(inputs, cfg, core) for core in range(8)]
    res = run_bass_kernel_spmd(nc, in_maps, core_ids=list(range(8)))
    outs = []
    for core in range(8):
        o = np.asarray(res.results[core]["out_t"], dtype=np.float32)
        outs.append(o.transpose(0, 2, 1))
    return np.ascontiguousarray(np.concatenate(outs, axis=0))


# version 13
# speedup vs baseline: 1.0110x; 1.0054x over previous
import sys
sys.path.insert(0, "/opt/trn_rl_repo")
import contextlib
import numpy as np
import ml_dtypes
import concourse.bass as bass
import concourse.mybir as mybir
from concourse.bass_utils import run_bass_kernel_spmd

F32 = mybir.dt.float32
BF16 = mybir.dt.bfloat16
AF = mybir.ActivationFunctionType
ALU = mybir.AluOpType
AX = mybir.AxisListType

D = 1024
NH = 16
HD = 64
DFF0 = 4096
DFF1 = 2816
NC8 = D // 128
NI = DFF1 // 128
EXP_C = 16.0
NEG = -1e10


def default_cfg():
    return dict(S=2048, nseq=4, layers=(0, 1), dumps=(), skip_attn=False, skip_ffn=False)


class Ctr:
    def __init__(self, es, nc, name):
        self.sem = es.enter_context(nc.semaphore(name))
        self.n = 0

    def inc(self, ins, k=1):
        ins.then_inc(self.sem, k)
        self.n += k


class FakeIns:
    __slots__ = ("incs",)

    def __init__(self):
        self.incs = []

    def then_inc(self, sem, v):
        self.incs.append((sem, v))
        return self


class Rec:

    def __init__(self, stream):
        self._stream = stream

    def __getattr__(self, name):
        st = self._stream

        def call(*a, **k):
            fi = FakeIns()
            st.append((name, a, k, fi))
            return fi
        return call


class Prog:
    def __init__(self, cfg):
        self.cfg = cfg
        self.S = cfg["S"]
        self.NQT = self.S // 512
        self.NKC = self.S // 128
        self.nc = bass.Bass("TRN2", target_bir_lowering=False)
        self.es = contextlib.ExitStack()
        self.dump_names = []

    def sb(self, name, shape, dt):
        return self.es.enter_context(self.nc.sbuf_tensor(name, shape, dt))

    def bank(self, i, lo=0, hi=512):
        return self.psum[:, i * 512 + lo:i * 512 + hi]

    def build(self):
        cfg, S, nc = self.cfg, self.S, self.nc
        di = lambda n, s, dt=F32: nc.dram_tensor(n, list(s), dt, kind="ExternalInput").ap()
        dt_ = lambda n, s, dt: nc.dram_tensor(n, list(s), dt, kind="Internal").ap()
        self.x_in = di("x_t", (cfg["nseq"], D, S))
        self.out = nc.dram_tensor("out_t", [cfg["nseq"], D, S], F32, kind="ExternalOutput").ap()
        self.msk = di("msk", (4, 128, 512), BF16)
        self.wqT = di("L0_wqT", (D, D))
        self.wkT = di("L0_wkT", (D, D))
        self.wvT = di("L0_wvT", (D, D))
        self.woT = di("L0_woT", (D, D))
        self.w1n = di("L0_w1", (DFF0, D))
        self.w2n = di("L0_w2", (D, DFF0))
        self.b1h = di("L0_b1h", (128, DFF0 // 128))
        self.b2h = di("L0_b2h", (128, NC8))
        self.wqkv = di("L1_wqkv", (NH, D, 192))
        self.wout = di("L1_wout", (NH, HD, D))
        self.wg = di("L1_wg", (DFF1, D))
        self.wu = di("L1_wu", (DFF1, D))
        self.wd = di("L1_wd", (D, DFF1))

        self.v_scr = dt_("v_scr", (S, D), BF16)
        self.t_scr = dt_("t_scr", (NH, S, HD), BF16)
        self.bw1T_scr = dt_("bw1T_scr", (D, DFF0), BF16)
        self.bw2T_scr = dt_("bw2T_scr", (DFF0, D), BF16)
        self.wgT_scr = dt_("wgT_scr", (D, DFF1), BF16)
        self.wuT_scr = dt_("wuT_scr", (D, DFF1), BF16)
        self.wdT_scr = dt_("wdT_scr", (DFF1, D), BF16)
        self.row_scr = dt_("row_scr", (1024,), F32)

        self.x = self.sb("x", [128, NC8, S], F32)
        self.xn = self.sb("xn", [128, NC8, S], BF16)
        self.arena = self.sb("arena", [128, 20480], BF16)
        self.kpair = self.sb("kpair", [128, S], BF16)
        self.qsmall = self.sb("qsmall", [128, 512], BF16)
        self.vpair = self.sb("vpair", [128, self.NKC, 132], BF16)
        self.wbufA = self.sb("wbufA", [128, NC8, 1024], BF16)
        self.wbufB = self.sb("wbufB", [128, 32, 128], BF16)
        self.mskt = self.sb("mskt", [128, 4, 512], BF16)
        self.consts = self.sb("consts", [128, 4], F32)
        self.ones1 = self.sb("ones1", [128, 1], BF16)
        self.onesK = self.sb("onesK", [1, 128], BF16)
        self.eA = self.sb("eA", [1, 128], BF16)
        self.eB = self.sb("eB", [1, 128], BF16)
        self.rowb = self.sb("rowb", [1, 2 * max(S, 2048)], BF16)
        self.st_sb = self.sb("st_sb", [1, 2 * S], BF16)
        self.dAB = self.sb("dAB", [1, 1024], BF16)
        self.wsum_c = self.sb("wsum_c", [128, NC8], F32)
        self.scrA = self.sb("scrA", [128, 512], F32)
        self.scrB = self.sb("scrB", [128, 512], F32)
        self.qsc = self.sb("qsc", [128, 4, 1024], BF16)
        self.qsf = self.sb("qsf", [128, 4, 64], F32)
        self.h_stage = self.sb("h_stage", [128, 512], BF16)
        self.tstage = self.sb("tstage", [128, 2, 64], BF16)
        self.b1_sb = self.sb("b1_sb", [128, 32], F32)
        self.b2_sb = self.sb("b2_sb", [128, NC8], F32)
        self.psum = self.es.enter_context(self.nc.psum_tensor("psum", [128, 4096], F32))

        A = self.arena
        self.attn_fm = A[:, 0:NC8 * S].rearrange("p (c q) -> p c q", c=NC8)
        off = NC8 * S
        self.ptA_g = A[:, off:off + 2048]
        self.ptB_g = A[:, off + 2048:off + 4096]
        self.ptA_f = A[:, 0:self.NKC * 512].rearrange("p (k r) -> p k r", r=512)
        self.ptB_f = A[:, self.NKC * 512:2 * self.NKC * 512].rearrange("p (k r) -> p k r", r=512)
        self.h_l0 = A[:, 0:32 * 512].rearrange("p (k r) -> p k r", r=512)
        self.xq = A[:, 0:NC8 * 512].rearrange("p (k r) -> p k r", r=512)
        self.hq = A[:, NC8 * 512:(NC8 + NI) * 512].rearrange("p (k r) -> p k r", r=512)
        self.prep_bf = A[:, 0:4096]
        self.y_sb = self.qsc[:, 0:2, :].rearrange("p a b -> p (a b)")

        es = self.es
        nc_ = self.nc
        self.cT = Ctr(es, nc_, "cT")
        self.cV = Ctr(es, nc_, "cV")
        self.cA = Ctr(es, nc_, "cA")
        self.cG = Ctr(es, nc_, "cG")
        self.cS = Ctr(es, nc_, "cS")
        self.cW = [Ctr(es, nc_, "cW0"), Ctr(es, nc_, "cW1")]
        self.cWG = [Ctr(es, nc_, "cWG0"), Ctr(es, nc_, "cWG1")]
        self.all_ctrs = [self.cT, self.cV, self.cA, self.cG, self.cS] + self.cW + self.cWG

        self.streams = {k: [] for k in ("tensor", "vector", "scalar", "gpsimd", "sync")}
        self.emit_all()
        with es:
            with nc.Block() as blk:
                for kind in ("tensor", "vector", "scalar", "gpsimd", "sync"):
                    getattr(blk, kind)(self._replayer(kind))
        return nc

    def _replayer(self, kind):
        stream = self.streams[kind]

        def run(eng):
            seen = {}
            for name, a, k, fi in stream:
                if name == "wait_ge":
                    sem, v = a
                    key = id(sem)
                    if seen.get(key, -1) >= v:
                        continue
                    seen[key] = v
                    eng.wait_ge(sem, v)
                    continue
                ins = getattr(eng, name)(*a, **k)
                for sem, v in fi.incs:
                    ins.then_inc(sem, v)
        return run

    def snap(self):
        return [(c, c.n) for c in self.all_ctrs]

    def body(self, kind, fn, dma_inc=None, waits=None):
        ctr = {"tensor": self.cT, "vector": self.cV, "scalar": self.cA,
               "gpsimd": self.cG, "sync": self.cS}[kind]
        is_dma = kind in ("gpsimd", "sync")
        if waits is None:
            waits = self.snap()
        waits = [(c, n) for (c, n) in waits if n > 0]
        eng = Rec(self.streams[kind])
        for c, n in waits:
            eng.wait_ge(c.sem, n)
        last = fn(eng)
        if is_dma:
            assert last is None, "DMA bodies must inc per-DMA via dS/dG"
        else:
            ctr.inc(last, 1)

    def dS(self, eng, out, in_, transpose=False):
        if transpose:
            ins = eng.dma_start_transpose(out=out, in_=in_)
        else:
            ins = eng.dma_start(out=out, in_=in_)
        self.cS.inc(ins, 16)
        return ins

    def dG(self, eng, out, in_):
        ins = eng.dma_start(out=out, in_=in_)
        self.cG.inc(ins, 16)
        return ins

    QUANT_EDGES = False

    def ve(self, eng, ins):
        self.cV.inc(ins)
        eng.wait_ge(self.cV.sem, self.cV.n)

    def vq(self, eng, ins):
        if self.QUANT_EDGES:
            self.ve(eng, ins)

    def ae(self, eng, ins):
        self.cA.inc(ins)
        eng.wait_ge(self.cA.sem, self.cA.n)

    def dump(self, name, ap, shape, dt=F32):
        if name not in self.cfg["dumps"]:
            return
        o = self.nc.dram_tensor("dump_" + name, list(shape), dt, kind="ExternalOutput").ap()
        self.dump_names.append("dump_" + name)
        self.body("sync", lambda eng: self.dS(eng, o, ap) and None)

    def drain(self, gens, n):
        for _ in range(n):
            for g in list(gens):
                try:
                    next(g)
                    break
                except StopIteration:
                    gens.remove(g)

    def drain_all(self, gens):
        for g in list(gens):
            for _ in g:
                pass
            gens.remove(g)

    def emit_all(self):
        cfg = self.cfg
        self.ph_init()
        if 1 in cfg["layers"]:
            self.ph_prep_wsum()
        self._g0 = [self.gen_prep_l0()] if 0 in cfg["layers"] else []
        self._g1 = [self.gen_prep_l1()] if 1 in cfg["layers"] else []
        for s in range(cfg["nseq"]):
            self.ph_load_x(s)
            for L in cfg["layers"]:
                self.ph_layernorm()
                self.dump(f"xn{L}_s{s}", self.xn[:].rearrange("p c q -> p (c q)"),
                          (128, NC8 * self.S), BF16)
                if not cfg.get("skip_attn"):
                    self.ph_project_v(L)
                    self.ph_attention(L, drains=(self._g0 if L == 0 else self._g1) if s == 0 else None)
                if L == 0:
                    self.drain_all(self._g0)
                    if not cfg.get("skip_attn"):
                        self.dump(f"afm_s{s}", self.attn_fm[:].rearrange("p c q -> p (c q)"),
                                  (128, NC8 * self.S), BF16)
                        self.ph_l0_oproj()
                    if not cfg.get("skip_ffn"):
                        self.ph_l0_ffn()
                else:
                    self.drain_all(self._g1)
                    if not cfg.get("skip_attn"):
                        self.ph_l1_scatter()
                    if not cfg.get("skip_ffn"):
                        self.ph_l1_ffn()
                self.dump(f"x{L}_s{s}", self.x[:].rearrange("p c q -> p (c q)"),
                          (128, NC8 * self.S), F32)
            self.ph_store_x(s)
        self.ph_final()

    def ph_init(self):
        def fv(eng):
            eng.memset(self.consts[:, 0:1], -EXP_C)
            eng.memset(self.consts[:, 1:2], 0.0)
            eng.memset(self.consts[:, 2:3], 1e-5)
            eng.memset(self.ones1[:], 1.0)
            eng.memset(self.onesK[:], 1.0)
            eng.memset(self.eA[:, 0:64], 1.0)
            eng.memset(self.eA[:, 64:128], 0.0)
            eng.memset(self.eB[:, 0:64], 0.0)
            eng.memset(self.eB[:, 64:128], 1.0)
            eng.memset(self.vpair[:, :, 64:65].rearrange("p a b -> p (a b)"), 1.0)
            return eng.memset(self.vpair[:, :, 129:130].rearrange("p a b -> p (a b)"), 1.0)
        self.body("vector", fv)

        def fs(eng):
            self.dS(eng, self.mskt[:], self.msk.rearrange("m p r -> p m r"))
            self.dS(eng, self.b1_sb[:], self.b1h)
            self.dS(eng, self.b2_sb[:], self.b2h)
        self.body("sync", fs)

    def gen_prep_l0(self):
        wflat = self.wbufA[:].rearrange("p c v -> p (c v)")
        pin = wflat[:, 0:2048].rearrange("p (s w) -> p s w", s=2)
        sg = wflat[:, 2048:6144].rearrange("p (s w) -> p s w", s=4)
        alf = self.qsf[:, 0, 0:4]
        snapP = self.snap()
        first = True
        for wn, scr, R, F in [(self.w1n, self.bw1T_scr, DFF0, D),
                              (self.w2n, self.bw2T_scr, D, DFF0)]:
            nchunk = R // 128
            ncb = F // 1024
            wv = wn.rearrange("(c p) f -> p c f", p=128)
            scrTv = scr.rearrange("(b p) f -> p b f", p=128)
            sPrev = getattr(self, "_g0_sprev", 0)
            for c in range(nchunk):
                for b in range(ncb):
                    cw = self.cWG[b % 2]
                    wts = list(snapP) if first else [(self.cA, self.cA.n), (self.cV, self.cV.n)]
                    first = False
                    def fdma(eng, c=c, b=b, cw=cw, wv=wv):
                        cw.inc(eng.dma_start(out=pin[:, b % 2, :],
                                             in_=wv[:, c, b * 1024:(b + 1) * 1024]), 16)
                    self.body("gpsimd", fdma, waits=wts)
                    wD = cw.n
                    def fred(eng, b=b, F=F):
                        ins = eng.tensor_reduce(out=alf[:, 2:3], in_=pin[:, b % 2, :],
                                                axis=AX.X, op=ALU.add, apply_absolute_value=True)
                        self.ve(eng, ins)
                        if b == 0:
                            return eng.tensor_scalar(out=alf[:, 0:1], in0=alf[:, 2:3],
                                                     scalar1=1.0 / F, scalar2=None, op0=ALU.mult)
                        return eng.scalar_tensor_tensor(out=alf[:, 0:1], in0=alf[:, 2:3],
                                                        scalar=1.0 / F, in1=alf[:, 0:1],
                                                        op0=ALU.mult, op1=ALU.add)
                    self.body("vector", fred, waits=[(cw, wD), (self.cV, self.cV.n)])
                    self.body("scalar", lambda eng, b=b: eng.activation(
                        out=sg[:, b, :], in_=pin[:, b % 2, :], func=AF.Sign,
                        bias=self.consts[:, 1:2], scale=1.0),
                        waits=[(cw, wD), (self.cA, self.cA.n), (self.cS, sPrev)])
                aSg = self.cA.n
                def fmul(eng, ncb=ncb):
                    last = None
                    for b in range(ncb):
                        last = eng.tensor_scalar(out=sg[:, b, :], in0=sg[:, b, :],
                                                 scalar1=alf[:, 0:1], scalar2=None, op0=ALU.mult)
                        self.ve(eng, last)
                    return eng.nop()
                self.body("vector", fmul, waits=[(self.cA, aSg), (self.cV, self.cV.n)])
                vMul = self.cV.n
                def ftr(eng, c=c, ncb=ncb, scrTv=scrTv):
                    for b in range(ncb):
                        for j in range(8):
                            self.dS(eng, self.qsc[:, 3, j * 128:(j + 1) * 128],
                                    sg[:, b, j * 128:(j + 1) * 128], transpose=True)
                        eng.wait_ge(self.cS.sem, self.cS.n)
                        self.dS(eng, scrTv[:, b * 8:b * 8 + 8, c * 128:(c + 1) * 128],
                                self.qsc[:, 3, :].rearrange("p (j i) -> p j i", i=128))
                        eng.wait_ge(self.cS.sem, self.cS.n)
                self.body("sync", ftr, waits=[(self.cV, vMul), (self.cS, self.cS.n)])
                sPrev = self.cS.n
                self._g0_sprev = sPrev
                yield

    def gen_prep_l1(self):
        pin = self.wbufA[:].rearrange("p c v -> p (c v)")[:, 4096:6144].rearrange("p (s w) -> p s w", s=2)
        for w, scrT, R, C in [(self.wg, self.wgT_scr, DFF1, D),
                              (self.wu, self.wuT_scr, DFF1, D),
                              (self.wd, self.wdT_scr, D, DFF1)]:
            nchunk = R // 128
            wv = w.rearrange("(c p) f -> p c f", p=128)
            scrTv = scrT.rearrange("(b p) f -> p b f", p=128)
            cbs = []
            o = 0
            while o < C:
                wdt = min(1024, C - o)
                cbs.append((o, wdt))
                o += wdt
            blocks = [(c, co, wdt) for c in range(nchunk) for (co, wdt) in cbs]
            snapP = self.snap()
            vQ = {}
            for i, (c, co, wdt) in enumerate(blocks):
                cw = self.cWG[i % 2]
                wts = list(snapP) if i < 2 else [(self.cV, vQ[i - 2])]
                self.body("gpsimd", lambda eng, c=c, co=co, wdt=wdt, i=i, cw=cw, wv=wv: cw.inc(
                    eng.dma_start(out=pin[:, i % 2, 0:wdt], in_=wv[:, c, co:co + wdt]), 16) and None,
                    waits=wts, dma_inc=0)
                wD = cw.n
                self._prep_quant_wait = (cw, wD)
                self.emit_quant([pin[:, i % 2, 0:wdt]], None, wdt, twist=False)
                vQ[i] = self.cV.n
                npc = wdt // 128
                def ftr(eng, npc=npc):
                    for j in range(npc):
                        self.dS(eng, self.qsc[:, 3, j * 128:(j + 1) * 128],
                                self.qsc[:, 1, j * 128:(j + 1) * 128], transpose=True)
                self.body("sync", ftr)
                self.body("sync", lambda eng, c=c, co=co, npc=npc, scrTv=scrTv: self.dS(
                    eng, scrTv[:, co // 128:co // 128 + npc, c * 128:(c + 1) * 128],
                    self.qsc[:, 3, 0:npc * 128].rearrange("p (j i) -> p j i", i=128)) and None)
                yield

    def ph_prep_wsum(self):
        wv = self.wout.rearrange("n h d -> (n h) d").rearrange("(c p) d -> p c d", p=128)
        for c in range(NC8):
            self.body("gpsimd", lambda eng, c=c, wv=wv: self.dG(
                eng, self.prep_bf[:, 0:1024], wv[:, c, :]) and None)
            def ft(eng, c=c):
                eng.matmul(self.bank(0)[0:1, :], self.ones1[:], self.prep_bf[:, 0:512],
                           start=(c == 0), stop=(c == NC8 - 1))
                return eng.matmul(self.bank(1)[0:1, :], self.ones1[:], self.prep_bf[:, 512:1024],
                                  start=(c == 0), stop=(c == NC8 - 1))
            self.body("tensor", ft)
        def fev(eng):
            eng.copy(out=self.scrA[0:1, :], in_=self.bank(0)[0:1, :])
            return eng.copy(out=self.scrB[0:1, :], in_=self.bank(1)[0:1, :])
        self.body("scalar", fev)
        def fst(eng):
            eng.dma_start(out=self.row_scr[0:512], in_=self.scrA[0:1, :].rearrange("a b -> (a b)"))
            eng.dma_start(out=self.row_scr[512:1024], in_=self.scrB[0:1, :].rearrange("a b -> (a b)"))
            return eng.dma_start(out=self.wsum_c[:], in_=self.row_scr.rearrange("(c p) -> p c", p=128))
        def fst(eng):
            self.dS(eng, self.row_scr[0:512].unsqueeze(0), self.scrA[0:1, :])
            self.dS(eng, self.row_scr[512:1024].unsqueeze(0), self.scrB[0:1, :])
            eng.wait_ge(self.cS.sem, self.cS.n)
            ins = eng.dma_start(out=self.wsum_c[:], in_=self.row_scr.rearrange("(c p) -> p c", p=128),
                                allow_slow_non_contiguous=True)
            self.cS.inc(ins, 16)
        self.body("sync", fst)

    def ph_load_x(self, s):
        self.body("sync", lambda eng, s=s: self.dS(
            eng, self.x[:], self.x_in[s].rearrange("(c p) q -> p c q", p=128)) and None)

    def ph_layernorm(self):
        S, NQT = self.S, self.NQT
        inv = 1.0 / D
        for qt in range(NQT):
            b0, b1 = (0, 1) if qt % 2 == 0 else (2, 3)
            for dc in range(NC8):
                o = 512 * (dc % 2)
                def fa(eng, qt=qt, dc=dc, o=o):
                    eng.copy(out=self.qsc[:, 0, o:o + 512], in_=self.x[:, dc, qt * 512:(qt + 1) * 512])
                    return eng.activation(out=self.qsc[:, 1, o:o + 512],
                                          in_=self.x[:, dc, qt * 512:(qt + 1) * 512],
                                          func=AF.Square, bias=self.consts[:, 1:2], scale=1.0)
                self.body("scalar", fa)
                def ft(eng, dc=dc, o=o, b0=b0, b1=b1):
                    eng.matmul(self.bank(b0)[0:1, :], self.ones1[:], self.qsc[:, 0, o:o + 512],
                               start=(dc == 0), stop=(dc == NC8 - 1))
                    return eng.matmul(self.bank(b1)[0:1, :], self.ones1[:], self.qsc[:, 1, o:o + 512],
                                      start=(dc == 0), stop=(dc == NC8 - 1))
                self.body("tensor", ft)
            def fe(eng, qt=qt, b0=b0, b1=b1):
                eng.mul(out=self.st_sb[0:1, qt * 512:(qt + 1) * 512],
                        in_=self.bank(b0)[0:1, :], mul=inv)
                return eng.mul(out=self.st_sb[0:1, S + qt * 512:S + (qt + 1) * 512],
                               in_=self.bank(b1)[0:1, :], mul=inv)
            self.body("scalar", fe)
        def fv(eng):
            ins = eng.tensor_tensor(out=self.rowb[0:1, S:2 * S], in0=self.st_sb[0:1, 0:S],
                                    in1=self.st_sb[0:1, 0:S], op=ALU.mult)
            self.ve(eng, ins)
            return eng.tensor_tensor(out=self.st_sb[0:1, S:2 * S], in0=self.st_sb[0:1, S:2 * S],
                                     in1=self.rowb[0:1, S:2 * S], op=ALU.subtract)
        self.body("vector", fv)
        def fs(eng):
            ins = eng.activation(out=self.st_sb[0:1, S:2 * S], in_=self.st_sb[0:1, S:2 * S],
                                 func=AF.Ln, bias=self.consts[0:1, 2:3], scale=1.0)
            self.ae(eng, ins)
            return eng.activation(out=self.rowb[0:1, 0:S], in_=self.st_sb[0:1, S:2 * S],
                                  func=AF.Exp, bias=self.consts[0:1, 1:2], scale=-0.5)
        self.body("scalar", fs)
        self.body("vector", lambda eng: eng.scalar_tensor_tensor(
            out=self.rowb[0:1, S:2 * S], in0=self.st_sb[0:1, 0:S], scalar=-1.0,
            in1=self.rowb[0:1, 0:S], op0=ALU.mult, op1=ALU.mult))
        for qt in range(NQT):
            b0, b1 = (0, 1) if qt % 2 == 0 else (2, 3)
            def fbc(eng, qt=qt, b0=b0, b1=b1):
                eng.matmul(self.bank(b0), self.onesK[:], self.rowb[0:1, qt * 512:(qt + 1) * 512],
                           start=True, stop=True)
                return eng.matmul(self.bank(b1), self.onesK[:],
                                  self.rowb[0:1, S + qt * 512:S + (qt + 1) * 512],
                                  start=True, stop=True)
            self.body("tensor", fbc)
            def fap(eng, qt=qt, b0=b0, b1=b1):
                last = None
                for dc in range(NC8):
                    ins = eng.tensor_tensor(out=self.scrA[:],
                                            in0=self.x[:, dc, qt * 512:(qt + 1) * 512],
                                            in1=self.bank(b0), op=ALU.mult)
                    self.ve(eng, ins)
                    last = eng.tensor_tensor(out=self.xn[:, dc, qt * 512:(qt + 1) * 512],
                                             in0=self.scrA[:], in1=self.bank(b1), op=ALU.add)
                    if dc < NC8 - 1:
                        self.ve(eng, last)
                return last
            self.body("vector", fap)

    def ph_project_v(self, L):
        S = self.S
        if L == 0:
            self.body("gpsimd", lambda eng: self.dG(
                eng, self.wbufA[:], self.wvT.rearrange("(c p) v -> p c v", p=128)) and None)
        else:
            def fwv(eng):
                for c in range(NC8):
                    self.dG(eng, self.wbufA[:, c, :].rearrange("p (n h) -> p n h", n=NH),
                            self.wqkv[:, :, 128:192].rearrange("n (c p) h -> p c n h", p=128)[:, c, :, :])
            self.body("gpsimd", fwv)
        snapV = self.snap()
        aE = {}
        sD = {}
        for qc in range(S // 128):
            b0, b1 = (0, 1) if qc % 2 == 0 else (2, 3)
            wts = list(snapV) if qc < 2 else [(self.cA, aE[qc - 2])]
            def ft(eng, qc=qc, b0=b0, b1=b1):
                last = None
                for half, b in ((0, b0), (1, b1)):
                    for dc in range(NC8):
                        last = eng.matmul(self.bank(b), self.xn[:, dc, qc * 128:(qc + 1) * 128],
                                          self.wbufA[:, dc, half * 512:(half + 1) * 512],
                                          start=(dc == 0), stop=(dc == NC8 - 1))
                return last
            self.body("tensor", ft, waits=wts)
            tT = self.cT.n
            o = 1024 * (qc % 2)
            wts = [(self.cT, tT)] + ([(self.cS, sD[qc - 2])] if qc >= 2 else [])
            def fe(eng, b0=b0, b1=b1, o=o):
                eng.copy(out=self.qsc[:, 0, :].rearrange("p a -> p a")[:, 0:0] if False else
                         self.qsc[:, 0 + 2 * (o // 1024), :][:, 0:512], in_=self.bank(b0))
                return eng.copy(out=self.qsc[:, 0 + 2 * (o // 1024), :][:, 512:1024], in_=self.bank(b1))
            self.body("scalar", fe, waits=wts)
            aE[qc] = self.cA.n
            self.body("sync", lambda eng, qc=qc, o=o: self.dS(
                eng, self.v_scr[qc * 128:(qc + 1) * 128, :],
                self.qsc[:, 0 + 2 * (o // 1024), :]) and None,
                waits=[(self.cA, aE[qc]), (self.cS, self.cS.n)])
            sD[qc] = self.cS.n

    def ph_attention(self, L, drains=None):
        S, NQT = self.S, self.NQT
        self._at = dict(fv=self.cV.n, fa2=self.cA.n)
        for hp in range(NH // 2):
            if L == 0:
                def fwl(eng, hp=hp):
                    self.dG(eng, self.wbufB[:, 0:8, :],
                            self.wqT.rearrange("(c p) k -> p c k", p=128)[:, :, hp * 128:(hp + 1) * 128])
                    self.dG(eng, self.wbufB[:, 8:16, :],
                            self.wkT.rearrange("(c p) k -> p c k", p=128)[:, :, hp * 128:(hp + 1) * 128])
            else:
                def fwl(eng, hp=hp):
                    for e in range(2):
                        self.dG(eng, self.wbufB[:, 0:8, e * 64:(e + 1) * 64],
                                self.wqkv[2 * hp + e, :, 0:64].rearrange("(c p) h -> p c h", p=128))
                        self.dG(eng, self.wbufB[:, 8:16, e * 64:(e + 1) * 64],
                                self.wqkv[2 * hp + e, :, 64:128].rearrange("(c p) h -> p c h", p=128))
            self.body("gpsimd", fwl)
            def fvl(eng, hp=hp):
                self.dS(eng, self.vpair[:, :, 0:64],
                        self.v_scr[:, hp * 128:hp * 128 + 64].rearrange("(k p) v -> p k v", p=128))
                self.dS(eng, self.vpair[:, :, 65:129],
                        self.v_scr[:, hp * 128 + 64:hp * 128 + 128].rearrange("(k p) v -> p k v", p=128))
            self.body("sync", fvl)
            cG_h = self.cG.n
            cS_h = self.cS.n
            snapH = self.snap()
            kev = {}
            for qt in range(NQT):
                bk = 4 if qt % 2 == 0 else 5
                if qt < 2:
                    wts = [(self.cG, cG_h), (self.cA, self._at["fa2"]), (self.cV, self._at["fv"])]
                else:
                    wts = [(self.cA, kev[qt - 2])]
                def ftk(eng, qt=qt, bk=bk):
                    last = None
                    for dc in range(NC8):
                        last = eng.matmul(self.bank(bk), self.wbufB[:, 8 + dc, :],
                                          self.xn[:, dc, qt * 512:(qt + 1) * 512],
                                          start=(dc == 0), stop=(dc == NC8 - 1))
                    return last
                self.body("tensor", ftk, waits=wts)
                tK = self.cT.n
                self.body("scalar", lambda eng, qt=qt, bk=bk: eng.copy(
                    out=self.kpair[:, qt * 512:(qt + 1) * 512], in_=self.bank(bk)),
                    waits=[(self.cT, tK), (self.cA, self.cA.n)])
                kev[qt] = self.cA.n
            for qt in range(NQT):
                nkk = 4 * (qt + 1)
                def ftq(eng, qt=qt):
                    last = None
                    for dc in range(NC8):
                        last = eng.matmul(self.bank(6), self.wbufB[:, dc, :],
                                          self.xn[:, dc, qt * 512:(qt + 1) * 512],
                                          start=(dc == 0), stop=(dc == NC8 - 1))
                    return last
                self.body("tensor", ftq,
                          waits=[(self.cG, cG_h), (self.cA, self._at["fa2"]), (self.cA, kev[qt])])
                tQ = self.cT.n
                self.body("scalar", lambda eng: eng.mul(out=self.qsmall[:], in_=self.bank(6), mul=0.125),
                          waits=[(self.cT, tQ), (self.cA, self.cA.n)])
                self._at["qev"] = self.cA.n
                self._at["cS_h"] = cS_h
                if L == 0:
                    self.attn_l0_tile(hp, qt, nkk)
                else:
                    self.attn_l1_tile(hp, qt, nkk)
                if drains:
                    self.drain(drains, 10 if L == 0 else 2)

    def _emit_sgroup(self, eng, g, qt, base=0):
        last = None
        for j in range(2):
            kc = 2 * g + j
            last = eng.matmul(self.bank(base + j), self.kpair[0:64, kc * 128:(kc + 1) * 128],
                              self.qsmall[0:64, :], start=True, stop=True)
            last = eng.matmul(self.bank(base + 2 + j), self.kpair[64:128, kc * 128:(kc + 1) * 128],
                              self.qsmall[64:128, :], start=True, stop=True)
        return last

    def _group_has_mask(self, g, qt):
        return any(0 <= (2 * g + j) * 128 - qt * 512 < 512 for j in range(2))

    def _emit_masks(self, eng, g, qt, base=0):
        last = None
        for j in range(2):
            kc = 2 * g + j
            o = kc * 128 - qt * 512
            if 0 <= o < 512:
                for hb in (base + j, base + 2 + j):
                    last = eng.tensor_tensor(out=self.bank(hb), in0=self.bank(hb),
                                             in1=self.mskt[:, o // 128, :], op=ALU.add)
        return last

    def attn_l0_tile(self, hp, qt, nkk):
        ng = nkk // 2
        snap0 = self.snap()
        tS = []
        base_V = self.cV.n
        base_A = self.cA.n
        hasm = [self._group_has_mask(g, qt) for g in range(ng)]
        masked = [g for g in range(ng) if hasm[g]]
        vM = {}
        for i, g in enumerate(masked):
            vM[g] = base_V + i + 1
        aE = [base_A + g + 1 for g in range(ng)]

        ftw = [(self.cA, self._at["qev"]), (self.cS, self._at["cS_h"]),
               (self.cV, self._at["fv"])]

        def emit_av(eng, g):
            last = None
            sl = 1024 * (g % 2)
            for j in range(2):
                kc = 2 * g + j
                first = (kc == 0)
                lastk = (kc == nkk - 1)
                eng.matmul(self.bank(4)[0:65, :], self.vpair[:, kc, 0:65],
                           self.ptA_g[:, sl + j * 512:sl + (j + 1) * 512],
                           start=first, stop=lastk, tile_position=(0, 0))
                last = eng.matmul(self.bank(5)[0:65, :], self.vpair[:, kc, 65:130],
                                  self.ptB_g[:, sl + j * 512:sl + (j + 1) * 512],
                                  start=first, stop=lastk, tile_position=(0, 0))
            return last

        def ft(eng):
            last = self._emit_sgroup(eng, 0, qt)
            self.cT.inc(last)
            tS.append(self.cT.n)
            for g in range(1, ng):
                eng.wait_ge(self.cA.sem, aE[g - 1])
                last = self._emit_sgroup(eng, g, qt)
                self.cT.inc(last)
                tS.append(self.cT.n)
                last = emit_av(eng, g - 1)
                if g < ng - 1:
                    self.cT.inc(last)
            eng.wait_ge(self.cA.sem, aE[ng - 1])
            return emit_av(eng, ng - 1)
        self.body("tensor", ft, waits=ftw)

        def fv(eng):
            last = None
            for g in masked:
                eng.wait_ge(self.cT.sem, tS[g])
                last = self._emit_masks(eng, g, qt)
                if g != masked[-1]:
                    self.cV.inc(last)
            return last if last is not None else eng.nop()
        self.body("vector", fv, waits=snap0)
        assert self.cV.n == base_V + max(len(masked), 1), (self.cV.n, base_V, masked)

        def fa(eng):
            last = None
            for g in range(ng):
                eng.wait_ge(self.cT.sem, tS[g])
                if hasm[g]:
                    eng.wait_ge(self.cV.sem, vM[g])
                sl = 1024 * (g % 2)
                eng.activation(out=self.ptA_g[:, sl:sl + 1024], in_=self.psum[:, 0:1024],
                               func=AF.Exp, bias=self.consts[:, 0:1], scale=1.0)
                last = eng.activation(out=self.ptB_g[:, sl:sl + 1024], in_=self.psum[:, 1024:2048],
                                      func=AF.Exp, bias=self.consts[:, 0:1], scale=1.0)
                if g < ng - 1:
                    self.cA.inc(last)
            return last
        self.body("scalar", fa, waits=snap0)
        assert self.cA.n == aE[-1], (self.cA.n, aE)

        def fa2(eng):
            eng.copy(out=self.dAB[0:1, 0:512], in_=self.bank(4)[64:65, :])
            return eng.copy(out=self.dAB[0:1, 512:1024], in_=self.bank(5)[64:65, :])
        self.body("scalar", fa2)
        self._at["fa2"] = self.cA.n
        def ft2(eng):
            eng.matmul(self.bank(7), self.eA[:], self.dAB[0:1, 0:512], start=True, stop=False)
            return eng.matmul(self.bank(7), self.eB[:], self.dAB[0:1, 512:1024], start=False, stop=True)
        self.body("tensor", ft2)
        def fa3(eng):
            ins = eng.activation(out=self.scrA[:], in_=self.bank(7), func=AF.Ln,
                                 bias=self.consts[:, 1:2], scale=1.0)
            self.ae(eng, ins)
            return eng.activation(out=self.scrB[:], in_=self.scrA[:], func=AF.Exp,
                                  bias=self.consts[:, 1:2], scale=-1.0)
        self.body("scalar", fa3)
        def fvE(eng):
            eng.tensor_tensor(
                out=self.attn_fm[0:64, hp, qt * 512:(qt + 1) * 512],
                in0=self.bank(4)[0:64, :], in1=self.scrB[0:64, :], op=ALU.mult)
            return eng.tensor_tensor(
                out=self.attn_fm[64:128, hp, qt * 512:(qt + 1) * 512],
                in0=self.bank(5)[0:64, :], in1=self.scrB[64:128, :], op=ALU.mult)
        self.body("vector", fvE)
        self._at["fv"] = self.cV.n

    def attn_l1_tile(self, hp, qt, nkk):
        ng = nkk // 2
        snap0 = self.snap()
        tS = []
        base_V = self.cV.n
        base_A = self.cA.n
        hasm = [self._group_has_mask(g, qt) for g in range(ng)]
        masked = [g for g in range(ng) if hasm[g]]
        vM = {}
        for i, g in enumerate(masked):
            vM[g] = base_V + i + 1
        aE = [base_A + g + 1 for g in range(ng)]
        bb = lambda g: 4 * (g % 2)

        ftw = [(self.cA, self._at["qev"]), (self.cS, self._at["cS_h"])]

        def ft(eng):
            last = None
            for g in range(ng):
                if g >= 2:
                    eng.wait_ge(self.cA.sem, aE[g - 2])
                last = self._emit_sgroup(eng, g, qt, base=bb(g))
                if g < ng - 1:
                    self.cT.inc(last)
                tS.append(self.cT.n + (1 if g == ng - 1 else 0))
            return last
        self.body("tensor", ft, waits=ftw)

        def fv(eng):
            last = None
            for g in masked:
                eng.wait_ge(self.cT.sem, tS[g])
                last = self._emit_masks(eng, g, qt, base=bb(g))
                if g != masked[-1]:
                    self.cV.inc(last)
            return last if last is not None else eng.nop()
        self.body("vector", fv, waits=snap0)

        def fa(eng):
            last = None
            for g in range(ng):
                eng.wait_ge(self.cT.sem, tS[g])
                if hasm[g]:
                    eng.wait_ge(self.cV.sem, vM[g])
                eng.activation(out=self.ptA_f[:, 2 * g:2 * g + 2, :].rearrange("p a b -> p (a b)"),
                               in_=self.psum[:, bb(g) * 512:(bb(g) + 2) * 512],
                               func=AF.Exp, bias=self.consts[:, 0:1], scale=1.0)
                last = eng.activation(out=self.ptB_f[:, 2 * g:2 * g + 2, :].rearrange("p a b -> p (a b)"),
                                      in_=self.psum[:, (bb(g) + 2) * 512:(bb(g) + 4) * 512],
                                      func=AF.Exp, bias=self.consts[:, 0:1], scale=1.0)
                if g < ng - 1:
                    self.cA.inc(last)
            return last
        self.body("scalar", fa, waits=snap0)

        expL = self.cA.n
        expL = self.cA.n
        for qb in range(4):
            wts = [(self.cA, expL), (self.cA, self._at["fa2"]),
                   (self.cV, self._at["fv"]), (self.cS, self._at["cS_h"])]
            def ftav(eng, qb=qb):
                last = None
                for pt, b, lo in ((self.ptA_f, 4, 0), (self.ptB_f, 5, 65)):
                    for kc in range(nkk):
                        last = eng.matmul(self.bank(b, 0, 65),
                                          pt[:, kc, qb * 128:(qb + 1) * 128],
                                          self.vpair[:, kc, lo:lo + 65],
                                          start=(kc == 0), stop=(kc == nkk - 1))
                return last
            self.body("tensor", ftav, waits=wts)
            def fa2(eng):
                ins = eng.activation(out=self.qsf[:, 0, 0:1], in_=self.bank(4, 64, 65),
                                     func=AF.Ln, bias=self.consts[:, 1:2], scale=1.0)
                self.ae(eng, ins)
                ins = eng.activation(out=self.qsf[:, 0, 1:2], in_=self.qsf[:, 0, 0:1],
                                     func=AF.Exp, bias=self.consts[:, 1:2], scale=-1.0)
                self.ae(eng, ins)
                ins = eng.activation(out=self.qsf[:, 0, 2:3], in_=self.bank(5, 64, 65),
                                     func=AF.Ln, bias=self.consts[:, 1:2], scale=1.0)
                self.ae(eng, ins)
                return eng.activation(out=self.qsf[:, 0, 3:4], in_=self.qsf[:, 0, 2:3],
                                      func=AF.Exp, bias=self.consts[:, 1:2], scale=-1.0)
            self.body("scalar", fa2)
            def fv2(eng):
                eng.tensor_scalar(out=self.tstage[:, 0, :], in0=self.bank(4, 0, 64),
                                  scalar1=self.qsf[:, 0, 1:2], scalar2=None, op0=ALU.mult)
                return eng.tensor_scalar(out=self.tstage[:, 1, :], in0=self.bank(5, 0, 64),
                                         scalar1=self.qsf[:, 0, 3:4], scalar2=None, op0=ALU.mult)
            self.body("vector", fv2)
            self._at["fv"] = self.cV.n
            self._at["fa2"] = self.cA.n
            sbase = qt * 512 + qb * 128
            self.body("sync", lambda eng, hp=hp, sbase=sbase: self.dS(
                eng, self.t_scr[2 * hp:2 * hp + 2, sbase:sbase + 128, :].rearrange("n (j p) h -> p j n h", p=128),
                self.tstage[:].unsqueeze(1)) and None)

    def ph_l0_oproj(self):
        NQT = self.NQT
        self.body("gpsimd", lambda eng: self.dG(
            eng, self.wbufA[:], self.woT.rearrange("(c p) d -> p c d", p=128)) and None)
        for qt in range(NQT):
            for db in range(NC8):
                b = (qt * NC8 + db) % 2
                def ft(eng, qt=qt, db=db, b=b):
                    last = None
                    for vc in range(NC8):
                        last = eng.matmul(self.bank(b), self.wbufA[:, vc, db * 128:(db + 1) * 128],
                                          self.attn_fm[:, vc, qt * 512:(qt + 1) * 512],
                                          start=(vc == 0), stop=(vc == NC8 - 1))
                    return last
                self.body("tensor", ft)
                self.body("vector", lambda eng, qt=qt, db=db, b=b: eng.tensor_tensor(
                    out=self.x[:, db, qt * 512:(qt + 1) * 512],
                    in0=self.x[:, db, qt * 512:(qt + 1) * 512],
                    in1=self.bank(b), op=ALU.add))

    def ph_l0_ffn(self):
        NQT = self.NQT
        bw1 = self.bw1T_scr.rearrange("(c p) f -> p c f", p=128)
        bw2 = self.bw2T_scr.rearrange("(c p) f -> p c f", p=128)
        w1v = self.wbufA[:].rearrange("p c v -> p (c v)")[:, 0:NC8 * 512].rearrange("p (c f) -> p c f", f=512)
        for qt in range(NQT):
            snapF = self.snap()
            tG = {}
            aG = {}
            tT1 = {}
            for hg in range(8):
                cw = self.cW[hg % 2]
                wts = list(snapF) if hg < 2 else [(self.cT, tT1[(hg - 2) * 4 + 3])]
                def fg(eng, hg=hg, cw=cw):
                    cw.inc(eng.dma_start(
                        out=self.wbufA[:, :, 512 * (hg % 2):512 * (hg % 2) + 512],
                        in_=bw1[:, :, hg * 512:(hg + 1) * 512]), 16)
                self.body("sync", fg, waits=wts)
                wG = cw.n
                for hj in range(4):
                    hc = hg * 4 + hj
                    b = hc % 2
                    wts = [(cw, wG)]
                    if hc >= 2:
                        wts.append((self.cA, aG[hc - 2]))
                    elif hg < 1:
                        wts += list(snapF)
                    def ft(eng, hj=hj, b=b, qt=qt, hg=hg):
                        last = None
                        for dc in range(NC8):
                            last = eng.matmul(
                                self.bank(b),
                                self.wbufA[:, dc, 512 * (hg % 2) + hj * 128:512 * (hg % 2) + (hj + 1) * 128],
                                self.xn[:, dc, qt * 512:(qt + 1) * 512],
                                start=(dc == 0), stop=(dc == NC8 - 1))
                        return last
                    self.body("tensor", ft, waits=wts)
                    tT1[hc] = self.cT.n
                    self.body("scalar", lambda eng, hc=hc, b=b: eng.activation(
                        out=self.h_l0[:, hc, :], in_=self.bank(b), func=AF.Gelu,
                        bias=self.b1_sb[:, hc:hc + 1], scale=1.0),
                        waits=[(self.cT, tT1[hc])] + ([(self.cA, self.cA.n)] if hc == 0 else []))
                    aG[hc] = self.cA.n
            tT2 = {}
            vU = {}
            aAll = self.cA.n
            for db in range(NC8):
                for half in range(2):
                    cw = self.cW[half]
                    wts = [(self.cT, tT2.get((db - 1, half), 0))] if db >= 1 else [(self.cT, self.cT.n)]
                    def fg2(eng, db=db, half=half, cw=cw):
                        cw.inc(eng.dma_start(
                            out=self.wbufB[:, 16 * half:16 * half + 16, :],
                            in_=bw2[:, 16 * half:16 * half + 16, db * 128:(db + 1) * 128]), 16)
                    self.body("sync", fg2, waits=wts)
                    tG[(db, half)] = cw.n
                b = 2 + db % 2
                wts = [(self.cW[0], tG[(db, 0)]), (self.cW[1], tG[(db, 1)]), (self.cA, aAll)]
                if db >= 2:
                    wts.append((self.cV, vU[db - 2]))
                def ft2(eng, db=db, b=b):
                    last = None
                    for hc in range(32):
                        last = eng.matmul(self.bank(b), self.wbufB[:, hc, :], self.h_l0[:, hc, :],
                                          start=(hc == 0), stop=(hc == 31))
                        if hc == 15:
                            pass
                    return last
                self.body("tensor", ft2, waits=wts)
                tT2[(db, 0)] = self.cT.n
                tT2[(db, 1)] = self.cT.n
                self.body("vector", lambda eng, qt=qt, db=db, b=b: eng.scalar_tensor_tensor(
                    out=self.x[:, db, qt * 512:(qt + 1) * 512], in0=self.bank(b),
                    scalar=self.b2_sb[:, db:db + 1], in1=self.x[:, db, qt * 512:(qt + 1) * 512],
                    op0=ALU.add, op1=ALU.add), waits=[(self.cT, tT2[(db, 0)]), (self.cV, self.cV.n)])
                vU[db] = self.cV.n

    def ph_l1_scatter(self):
        S = self.S
        for dc in range(NC8):
            G = S // 64
            def fy(eng, dc=dc, G=G):
                self.dS(eng, self.y_sb[0:64, 0:S].rearrange("p (c h) -> p c h", h=64),
                        self.t_scr[2 * dc].rearrange("(a c) h -> a c h", c=G))
                self.dS(eng, self.y_sb[64:128, 0:S].rearrange("p (c h) -> p c h", h=64),
                        self.t_scr[2 * dc + 1].rearrange("(a c) h -> a c h", c=G))
            self.body("sync", fy)
            self.body("vector", lambda eng, dc=dc: eng.scalar_tensor_tensor(
                out=self.x[:, dc, :], in0=self.y_sb[:, 0:S],
                scalar=self.wsum_c[:, dc:dc + 1], in1=self.x[:, dc, :],
                op0=ALU.mult, op1=ALU.add))

    def emit_quant(self, in_aps, out_ap, W, twist, waits=None):
        nb = W // 16
        q0 = self.qsc[:, 0, 0:W]
        extra_w = getattr(self, "_prep_quant_wait", None)
        self._prep_quant_wait = None
        if waits is None:
            waits = self.snap()
        fv1_waits = list(waits) + [(self.cV, self.cV.n), (self.cA, self.cA.n)]

        def fv1(eng):
            if extra_w is not None:
                eng.wait_ge(extra_w[0].sem, extra_w[1])
            if in_aps is not None:
                o = 0
                for ap in in_aps:
                    w = ap.shape[-1]
                    if twist:
                        ins = eng.transpose(out=self.qsc[:, 0, o:o + w], in_=ap)
                    else:
                        ins = eng.tensor_copy(out=self.qsc[:, 0, o:o + w], in_=ap)
                    o += w
                    self.ve(eng, ins)
            ins = eng.tensor_reduce(out=self.qsf[:, 0, 0:nb],
                                    in_=q0.rearrange("p (g s) -> p g s", s=16),
                                    axis=AX.X, op=ALU.max, apply_absolute_value=True)
            self.vq(eng, ins)
            return eng.tensor_scalar(out=self.qsf[:, 1, 0:nb], in0=self.qsf[:, 0, 0:nb],
                                     scalar1=1.0 / 6.0, scalar2=1e-12, op0=ALU.mult, op1=ALU.max)
        self.body("vector", fv1, waits=fv1_waits)
        cV1 = self.cV.n

        def fa1(eng):
            ins = eng.activation(out=self.qsf[:, 2, 0:nb], in_=self.qsf[:, 1, 0:nb],
                                 func=AF.Ln, bias=self.consts[:, 1:2], scale=1.0)
            self.ae(eng, ins)
            eng.activation(out=self.qsf[:, 3, 0:nb], in_=self.qsf[:, 2, 0:nb],
                           func=AF.Exp, bias=self.consts[:, 1:2], scale=-1.0)
            eng.activation(out=self.qsc[:, 1, 0:W], in_=q0,
                           func=AF.Abs, bias=self.consts[:, 1:2], scale=1.0)
            return eng.activation(out=self.qsc[:, 3, 0:W], in_=q0,
                                  func=AF.Sign, bias=self.consts[:, 1:2], scale=1.0)
        self.body("scalar", fa1, waits=[(self.cV, cV1), (self.cA, self.cA.n)])
        cA1 = self.cA.n

        def fv2(eng):
            a = self.qsc[:, 1, 0:W]
            st = self.qsc[:, 2, 0:W]
            a3 = lambda t: t.rearrange("p (g s) -> p g s", s=16)
            rsc = self.qsf[:, 3, 0:nb].unsqueeze(2).broadcast_to((128, nb, 16))
            ins = eng.tensor_tensor(out=a3(a), in0=a3(a), in1=rsc, op=ALU.mult)
            self.vq(eng, ins)
            ins = eng.tensor_scalar(out=st, in0=a, scalar1=2.5, scalar2=None, op0=ALU.is_gt)
            self.vq(eng, ins)
            for th in (3.5, 5.0, 5.0):
                ins = eng.scalar_tensor_tensor(out=st, in0=a, scalar=th, in1=st,
                                               op0=ALU.is_gt, op1=ALU.add)
                self.vq(eng, ins)
            ins = eng.tensor_scalar(out=st, in0=st, scalar1=2.0, scalar2=None, op0=ALU.mult)
            self.vq(eng, ins)
            for th in (0.25, 0.75, 1.25, 1.75):
                ins = eng.scalar_tensor_tensor(out=st, in0=a, scalar=th, in1=st,
                                               op0=ALU.is_gt, op1=ALU.add)
                self.vq(eng, ins)
            ins = eng.tensor_scalar(out=self.qsf[:, 2, 0:nb], in0=self.qsf[:, 1, 0:nb],
                                    scalar1=0.5, scalar2=None, op0=ALU.mult)
            self.vq(eng, ins)
            hsc = self.qsf[:, 2, 0:nb].unsqueeze(2).broadcast_to((128, nb, 16))
            ins = eng.tensor_tensor(out=a3(st), in0=a3(st), in1=hsc, op=ALU.mult)
            self.vq(eng, ins)
            ins = eng.tensor_tensor(out=a, in0=st, in1=self.qsc[:, 3, 0:W], op=ALU.mult)
            self.ve(eng, ins)
            if twist:
                ins = eng.transpose(out=out_ap, in_=a)
                self.ve(eng, ins)
                return eng.nop()
            if out_ap is None:
                return eng.nop()
            return eng.tensor_copy(out=out_ap, in_=a)
        self.body("vector", fv2, waits=[(self.cA, cA1)])

    def ph_l1_ffn(self):
        NQT = self.NQT
        wgT = self.wgT_scr.rearrange("(c p) f -> p c f", p=128)
        wuT = self.wuT_scr.rearrange("(c p) f -> p c f", p=128)
        wdT = self.wdT_scr.rearrange("(c p) f -> p c f", p=128)
        for qt in range(NQT):
            qts = slice(qt * 512, (qt + 1) * 512)
            for pp in range(NC8 // 2):
                self.emit_quant([self.xn[:, 2 * pp, qts], self.xn[:, 2 * pp + 1, qts]],
                                self.xq[:, 2 * pp:2 * pp + 2, :].rearrange("p a b -> p (a b)"),
                                1024, twist=True, waits=[])
            tT, tA, tV, tS = {}, {}, {}, {}
            snapA = self.snap()
            snapA_cV = self.cV.n
            for ic in range(NI):
                sl = 16 * (ic % 2)
                bg, bu = (0, 1) if ic % 2 == 0 else (2, 3)
                wts = list(snapA)
                if ic >= 2:
                    wts = [(self.cT, tT[ic - 2])]
                cw = self.cW[ic % 2]
                def fwg(eng, ic=ic, sl=sl, cw=cw):
                    cw.inc(eng.dma_start(out=self.wbufB[:, sl:sl + 8, :],
                                         in_=wgT[:, :, ic * 128:(ic + 1) * 128]), 16)
                    cw.inc(eng.dma_start(out=self.wbufB[:, sl + 8:sl + 16, :],
                                         in_=wuT[:, :, ic * 128:(ic + 1) * 128]), 16)
                self.body("sync", fwg, waits=wts)
                tS[ic] = cw.n
                wts = [(cw, tS[ic])]
                if ic >= 2:
                    wts.append((self.cV, tV[ic - 2]))
                else:
                    wts.append((self.cV, snapA_cV))
                def ft(eng, ic=ic, sl=sl, bg=bg, bu=bu):
                    for dc in range(NC8):
                        eng.matmul(self.bank(bg), self.wbufB[:, sl + dc, :], self.xq[:, dc, :],
                                   start=(dc == 0), stop=(dc == NC8 - 1))
                    last = None
                    for dc in range(NC8):
                        last = eng.matmul(self.bank(bu), self.wbufB[:, sl + 8 + dc, :], self.xq[:, dc, :],
                                          start=(dc == 0), stop=(dc == NC8 - 1))
                    return last
                self.body("tensor", ft, waits=wts)
                tT[ic] = self.cT.n
                wts = [(self.cT, tT[ic])]
                if ic >= 1:
                    wts.append((self.cV, tV[ic - 1]))
                self.body("scalar", lambda eng, bg=bg: eng.activation(
                    out=self.scrA[:], in_=self.bank(bg), func=AF.Silu,
                    bias=self.consts[:, 1:2], scale=1.0), waits=wts)
                tA[ic] = self.cA.n
                self.body("vector", lambda eng, ic=ic, bu=bu: eng.tensor_tensor(
                    out=self.hq[:, ic, :], in0=self.scrA[:], in1=self.bank(bu), op=ALU.mult),
                    waits=[(self.cA, tA[ic])])
                tV[ic] = self.cV.n
                if ic % 2 == 1:
                    pp = ic // 2
                    self.emit_quant([self.hq[:, 2 * pp:2 * pp + 2, :].rearrange("p a b -> p (a b)")],
                                    self.hq[:, 2 * pp:2 * pp + 2, :].rearrange("p a b -> p (a b)"),
                                    1024, twist=True, waits=[])
            assert NI % 2 == 0
            for db in range(NC8):
                self.body("sync", lambda eng, db=db: self.dS(
                    eng, self.wbufB[:, 0:NI, :], wdT[:, :, db * 128:(db + 1) * 128]) and None)
                b = 2 + db % 2
                def ft2(eng, b=b):
                    last = None
                    for ic in range(NI):
                        last = eng.matmul(self.bank(b), self.wbufB[:, ic, :], self.hq[:, ic, :],
                                          start=(ic == 0), stop=(ic == NI - 1))
                    return last
                self.body("tensor", ft2)
                self.body("vector", lambda eng, qt=qt, db=db, b=b: eng.tensor_tensor(
                    out=self.x[:, db, qt * 512:(qt + 1) * 512],
                    in0=self.x[:, db, qt * 512:(qt + 1) * 512],
                    in1=self.bank(b), op=ALU.add))

    def ph_store_x(self, s):
        self.body("sync", lambda eng, s=s: self.dS(
            eng, self.out[s].rearrange("(c p) q -> p c q", p=128), self.x[:]) and None)

    def ph_final(self):
        snap = self.snap()
        eng = Rec(self.streams["sync"])
        for c, n in snap:
            if n > 0:
                eng.wait_ge(c.sem, n)


def build_masks():
    m = np.zeros((4, 128, 512), dtype=np.float32)
    for oi, o in enumerate((0, 128, 256, 384)):
        i = np.arange(128)[:, None]
        j = np.arange(512)[None, :]
        m[oi] = np.where(i + o <= j, 0.0, NEG)
    return m.astype(ml_dtypes.bfloat16)


def host_inputs(inputs, cfg, core):
    nseq = cfg["nseq"]
    x = np.asarray(inputs["L0_attn_x"], dtype=np.float32)
    xs = x[core * nseq:(core + 1) * nseq]
    x_t = np.ascontiguousarray(xs.transpose(0, 2, 1))
    f32 = lambda a: np.ascontiguousarray(np.asarray(a, dtype=np.float32))
    tr = lambda a: np.ascontiguousarray(np.asarray(a, dtype=np.float32).T)
    return {
        "x_t": x_t,
        "msk": build_masks(),
        "L0_wqT": tr(inputs["L0_attn_q_proj_weight"]),
        "L0_wkT": tr(inputs["L0_attn_k_proj_weight"]),
        "L0_wvT": tr(inputs["L0_attn_v_proj_weight"]),
        "L0_woT": tr(inputs["L0_attn_o_proj_weight"]),
        "L0_w1": f32(inputs["L0_ffn_w1"]),
        "L0_w2": f32(inputs["L0_ffn_w2"]),
        "L0_b1h": np.ascontiguousarray(f32(inputs["L0_ffn_b1"]).reshape(32, 128).T),
        "L0_b2h": np.ascontiguousarray(f32(inputs["L0_ffn_b2"]).reshape(8, 128).T),
        "L1_wqkv": f32(inputs["L1_attn_W_qkv"]),
        "L1_wout": f32(inputs["L1_attn_W_out"]),
        "L1_wg": f32(inputs["L1_ffn_w_gate"]),
        "L1_wu": f32(inputs["L1_ffn_w_up"]),
        "L1_wd": f32(inputs["L1_ffn_w_down"]),
    }


def kernel(**inputs):
    cfg = default_cfg()
    p = Prog(cfg)
    nc = p.build()
    in_maps = [host_inputs(inputs, cfg, core) for core in range(8)]
    res = run_bass_kernel_spmd(nc, in_maps, core_ids=list(range(8)))
    outs = []
    for core in range(8):
        o = np.asarray(res.results[core]["out_t"], dtype=np.float32)
        outs.append(o.transpose(0, 2, 1))
    return np.ascontiguousarray(np.concatenate(outs, axis=0))


# version 14 (final)
# speedup vs baseline: 1.0300x; 1.0188x over previous
import sys
sys.path.insert(0, "/opt/trn_rl_repo")
import contextlib
import numpy as np
import ml_dtypes
import concourse.bass as bass
import concourse.mybir as mybir
from concourse.bass_utils import run_bass_kernel_spmd

F32 = mybir.dt.float32
BF16 = mybir.dt.bfloat16
AF = mybir.ActivationFunctionType
ALU = mybir.AluOpType
AX = mybir.AxisListType

D = 1024
NH = 16
HD = 64
DFF0 = 4096
DFF1 = 2816
NC8 = D // 128
NI = DFF1 // 128
EXP_C = 16.0
NEG = -1e10


def default_cfg():
    return dict(S=2048, nseq=4, layers=(0, 1), dumps=(), skip_attn=False, skip_ffn=False)


class Ctr:
    def __init__(self, es, nc, name):
        self.sem = es.enter_context(nc.semaphore(name))
        self.n = 0

    def inc(self, ins, k=1):
        ins.then_inc(self.sem, k)
        self.n += k


class FakeIns:
    __slots__ = ("incs",)

    def __init__(self):
        self.incs = []

    def then_inc(self, sem, v):
        self.incs.append((sem, v))
        return self


class Rec:

    def __init__(self, stream):
        self._stream = stream

    def __getattr__(self, name):
        st = self._stream

        def call(*a, **k):
            fi = FakeIns()
            st.append((name, a, k, fi))
            return fi
        return call


class Prog:
    def __init__(self, cfg):
        self.cfg = cfg
        self.S = cfg["S"]
        self.NQT = self.S // 512
        self.NKC = self.S // 128
        self.nc = bass.Bass("TRN2", target_bir_lowering=False)
        self.es = contextlib.ExitStack()
        self.dump_names = []

    def sb(self, name, shape, dt):
        return self.es.enter_context(self.nc.sbuf_tensor(name, shape, dt))

    def bank(self, i, lo=0, hi=512):
        return self.psum[:, i * 512 + lo:i * 512 + hi]

    def build(self):
        cfg, S, nc = self.cfg, self.S, self.nc
        di = lambda n, s, dt=F32: nc.dram_tensor(n, list(s), dt, kind="ExternalInput").ap()
        dt_ = lambda n, s, dt: nc.dram_tensor(n, list(s), dt, kind="Internal").ap()
        self.x_in = di("x_t", (cfg["nseq"], D, S))
        self.out = nc.dram_tensor("out_t", [cfg["nseq"], D, S], F32, kind="ExternalOutput").ap()
        self.msk = di("msk", (4, 128, 512), BF16)
        self.wqT = di("L0_wqT", (D, D))
        self.wkT = di("L0_wkT", (D, D))
        self.wvT = di("L0_wvT", (D, D))
        self.woT = di("L0_woT", (D, D))
        self.w1n = di("L0_w1", (DFF0, D))
        self.w2n = di("L0_w2", (D, DFF0))
        self.b1h = di("L0_b1h", (128, DFF0 // 128))
        self.b2h = di("L0_b2h", (128, NC8))
        self.wqkv = di("L1_wqkv", (NH, D, 192))
        self.wout = di("L1_wout", (NH, HD, D))
        self.wg = di("L1_wg", (DFF1, D))
        self.wu = di("L1_wu", (DFF1, D))
        self.wd = di("L1_wd", (D, DFF1))

        self.v_scr = dt_("v_scr", (S, D), BF16)
        self.t_scr = dt_("t_scr", (NH, S, HD), BF16)
        self.bw1T_scr = dt_("bw1T_scr", (D, DFF0), BF16)
        self.bw2T_scr = dt_("bw2T_scr", (DFF0, D), BF16)
        self.wgT_scr = dt_("wgT_scr", (D, DFF1), BF16)
        self.wuT_scr = dt_("wuT_scr", (D, DFF1), BF16)
        self.wdT_scr = dt_("wdT_scr", (DFF1, D), BF16)
        self.row_scr = dt_("row_scr", (1024,), F32)

        self.x = self.sb("x", [128, NC8, S], F32)
        self.xn = self.sb("xn", [128, NC8, S], BF16)
        self.arena = self.sb("arena", [128, 20480], BF16)
        self.kpair = self.sb("kpair", [128, S], BF16)
        self.qsmall = self.sb("qsmall", [128, 512], BF16)
        self.vpair = self.sb("vpair", [128, self.NKC, 132], BF16)
        self.wbufA = self.sb("wbufA", [128, NC8, 1024], BF16)
        self.wbufB = self.sb("wbufB", [128, 32, 128], BF16)
        self.mskt = self.sb("mskt", [128, 4, 512], BF16)
        self.consts = self.sb("consts", [128, 4], F32)
        self.ones1 = self.sb("ones1", [128, 1], BF16)
        self.onesK = self.sb("onesK", [1, 128], BF16)
        self.eA = self.sb("eA", [1, 128], BF16)
        self.eB = self.sb("eB", [1, 128], BF16)
        self.rowb = self.sb("rowb", [1, 2 * max(S, 2048)], BF16)
        self.st_sb = self.sb("st_sb", [1, 2 * S], BF16)
        self.dAB = self.sb("dAB", [1, 1024], BF16)
        self.wsum_c = self.sb("wsum_c", [128, NC8], F32)
        self.scrA = self.sb("scrA", [128, 512], F32)
        self.scrB = self.sb("scrB", [128, 512], F32)
        self.qsc = self.sb("qsc", [128, 4, 1024], BF16)
        self.qsf = self.sb("qsf", [128, 4, 64], F32)
        self.h_stage = self.sb("h_stage", [128, 512], BF16)
        self.tstage = self.sb("tstage", [128, 2, 64], BF16)
        self.b1_sb = self.sb("b1_sb", [128, 32], F32)
        self.b2_sb = self.sb("b2_sb", [128, NC8], F32)
        self.psum = self.es.enter_context(self.nc.psum_tensor("psum", [128, 4096], F32))

        A = self.arena
        self.attn_fm = A[:, 0:NC8 * S].rearrange("p (c q) -> p c q", c=NC8)
        off = NC8 * S
        self.ptA_g = A[:, off:off + 2048]
        self.ptB_g = A[:, off + 2048:off + 4096]
        self.ptA_f = A[:, 0:self.NKC * 512].rearrange("p (k r) -> p k r", r=512)
        self.ptB_f = A[:, self.NKC * 512:2 * self.NKC * 512].rearrange("p (k r) -> p k r", r=512)
        self.h_l0 = A[:, 0:32 * 512].rearrange("p (k r) -> p k r", r=512)
        self.xq = A[:, 0:NC8 * 512].rearrange("p (k r) -> p k r", r=512)
        self.hq = A[:, NC8 * 512:(NC8 + NI) * 512].rearrange("p (k r) -> p k r", r=512)
        self.prep_bf = A[:, 0:4096]
        self.y_sb = self.qsc[:, 0:2, :].rearrange("p a b -> p (a b)")

        es = self.es
        nc_ = self.nc
        self.cT = Ctr(es, nc_, "cT")
        self.cV = Ctr(es, nc_, "cV")
        self.cA = Ctr(es, nc_, "cA")
        self.cG = Ctr(es, nc_, "cG")
        self.cS = Ctr(es, nc_, "cS")
        self.cW = [Ctr(es, nc_, "cW0"), Ctr(es, nc_, "cW1")]
        self.cWG = [Ctr(es, nc_, "cWG0"), Ctr(es, nc_, "cWG1")]
        self.all_ctrs = [self.cT, self.cV, self.cA, self.cG, self.cS] + self.cW + self.cWG

        self.streams = {k: [] for k in ("tensor", "vector", "scalar", "gpsimd", "sync")}
        self.emit_all()
        with es:
            with nc.Block() as blk:
                for kind in ("tensor", "vector", "scalar", "gpsimd", "sync"):
                    getattr(blk, kind)(self._replayer(kind))
        return nc

    def _replayer(self, kind):
        stream = self.streams[kind]

        def run(eng):
            seen = {}
            for name, a, k, fi in stream:
                if name == "wait_ge":
                    sem, v = a
                    key = id(sem)
                    if seen.get(key, -1) >= v:
                        continue
                    seen[key] = v
                    eng.wait_ge(sem, v)
                    continue
                ins = getattr(eng, name)(*a, **k)
                for sem, v in fi.incs:
                    ins.then_inc(sem, v)
        return run

    def snap(self):
        return [(c, c.n) for c in self.all_ctrs]

    def body(self, kind, fn, dma_inc=None, waits=None):
        ctr = {"tensor": self.cT, "vector": self.cV, "scalar": self.cA,
               "gpsimd": self.cG, "sync": self.cS}[kind]
        is_dma = kind in ("gpsimd", "sync")
        if waits is None:
            waits = self.snap()
        waits = [(c, n) for (c, n) in waits if n > 0]
        eng = Rec(self.streams[kind])
        for c, n in waits:
            eng.wait_ge(c.sem, n)
        last = fn(eng)
        if is_dma:
            assert last is None, "DMA bodies must inc per-DMA via dS/dG"
        else:
            ctr.inc(last, 1)

    def dS(self, eng, out, in_, transpose=False):
        if transpose:
            ins = eng.dma_start_transpose(out=out, in_=in_)
        else:
            ins = eng.dma_start(out=out, in_=in_)
        self.cS.inc(ins, 16)
        return ins

    def dG(self, eng, out, in_):
        ins = eng.dma_start(out=out, in_=in_)
        self.cG.inc(ins, 16)
        return ins

    QUANT_EDGES = False

    def ve(self, eng, ins):
        self.cV.inc(ins)
        eng.wait_ge(self.cV.sem, self.cV.n)

    def vq(self, eng, ins):
        if self.QUANT_EDGES:
            self.ve(eng, ins)

    def ae(self, eng, ins):
        self.cA.inc(ins)
        eng.wait_ge(self.cA.sem, self.cA.n)

    def dump(self, name, ap, shape, dt=F32):
        if name not in self.cfg["dumps"]:
            return
        o = self.nc.dram_tensor("dump_" + name, list(shape), dt, kind="ExternalOutput").ap()
        self.dump_names.append("dump_" + name)
        self.body("sync", lambda eng: self.dS(eng, o, ap) and None)

    def drain(self, gens, n):
        for _ in range(n):
            for g in list(gens):
                try:
                    next(g)
                    break
                except StopIteration:
                    gens.remove(g)

    def drain_all(self, gens):
        for g in list(gens):
            for _ in g:
                pass
            gens.remove(g)

    def emit_all(self):
        cfg = self.cfg
        self.ph_init()
        if 1 in cfg["layers"]:
            self.ph_prep_wsum()
        self._g0 = [self.gen_prep_l0()] if 0 in cfg["layers"] else []
        self._g1 = [self.gen_prep_l1()] if 1 in cfg["layers"] else []
        for s in range(cfg["nseq"]):
            self.ph_load_x(s)
            for L in cfg["layers"]:
                self.ph_layernorm()
                self.dump(f"xn{L}_s{s}", self.xn[:].rearrange("p c q -> p (c q)"),
                          (128, NC8 * self.S), BF16)
                if not cfg.get("skip_attn"):
                    self.ph_project_v(L)
                    self.ph_attention(L, drains=(self._g0 if L == 0 else self._g1) if s == 0 else None)
                if L == 0:
                    self.drain_all(self._g0)
                    if not cfg.get("skip_attn"):
                        self.dump(f"afm_s{s}", self.attn_fm[:].rearrange("p c q -> p (c q)"),
                                  (128, NC8 * self.S), BF16)
                        self.ph_l0_oproj()
                    if not cfg.get("skip_ffn"):
                        self.ph_l0_ffn()
                else:
                    self.drain_all(self._g1)
                    if not cfg.get("skip_attn"):
                        self.ph_l1_scatter()
                    if not cfg.get("skip_ffn"):
                        self.ph_l1_ffn()
                self.dump(f"x{L}_s{s}", self.x[:].rearrange("p c q -> p (c q)"),
                          (128, NC8 * self.S), F32)
            self.ph_store_x(s)
        self.ph_final()

    def ph_init(self):
        def fv(eng):
            eng.memset(self.consts[:, 0:1], -EXP_C)
            eng.memset(self.consts[:, 1:2], 0.0)
            eng.memset(self.consts[:, 2:3], 1e-5)
            eng.memset(self.ones1[:], 1.0)
            eng.memset(self.onesK[:], 1.0)
            eng.memset(self.eA[:, 0:64], 1.0)
            eng.memset(self.eA[:, 64:128], 0.0)
            eng.memset(self.eB[:, 0:64], 0.0)
            eng.memset(self.eB[:, 64:128], 1.0)
            eng.memset(self.vpair[:, :, 64:65].rearrange("p a b -> p (a b)"), 1.0)
            return eng.memset(self.vpair[:, :, 129:130].rearrange("p a b -> p (a b)"), 1.0)
        self.body("vector", fv)

        def fs(eng):
            self.dS(eng, self.mskt[:], self.msk.rearrange("m p r -> p m r"))
            self.dS(eng, self.b1_sb[:], self.b1h)
            self.dS(eng, self.b2_sb[:], self.b2h)
        self.body("sync", fs)

    def gen_prep_l0(self):
        wflat = self.wbufA[:].rearrange("p c v -> p (c v)")
        pin = wflat[:, 0:2048].rearrange("p (s w) -> p s w", s=2)
        sg = wflat[:, 2048:6144].rearrange("p (s w) -> p s w", s=4)
        alf = self.qsf[:, 0, 0:4]
        snapP = self.snap()
        first = True
        for wn, scr, R, F in [(self.w1n, self.bw1T_scr, DFF0, D),
                              (self.w2n, self.bw2T_scr, D, DFF0)]:
            nchunk = R // 128
            ncb = F // 1024
            wv = wn.rearrange("(c p) f -> p c f", p=128)
            scrTv = scr.rearrange("(b p) f -> p b f", p=128)
            sPrev = getattr(self, "_g0_sprev", 0)
            for c in range(nchunk):
                for b in range(ncb):
                    cw = self.cWG[b % 2]
                    wts = list(snapP) if first else [(self.cA, self.cA.n), (self.cV, self.cV.n)]
                    first = False
                    def fdma(eng, c=c, b=b, cw=cw, wv=wv):
                        cw.inc(eng.dma_start(out=pin[:, b % 2, :],
                                             in_=wv[:, c, b * 1024:(b + 1) * 1024]), 16)
                    self.body("gpsimd", fdma, waits=wts)
                    wD = cw.n
                    def fred(eng, b=b, F=F):
                        ins = eng.tensor_reduce(out=alf[:, 2:3], in_=pin[:, b % 2, :],
                                                axis=AX.X, op=ALU.add, apply_absolute_value=True)
                        self.ve(eng, ins)
                        if b == 0:
                            return eng.tensor_scalar(out=alf[:, 0:1], in0=alf[:, 2:3],
                                                     scalar1=1.0 / F, scalar2=None, op0=ALU.mult)
                        return eng.scalar_tensor_tensor(out=alf[:, 0:1], in0=alf[:, 2:3],
                                                        scalar=1.0 / F, in1=alf[:, 0:1],
                                                        op0=ALU.mult, op1=ALU.add)
                    self.body("vector", fred, waits=[(cw, wD), (self.cV, self.cV.n)])
                    self.body("scalar", lambda eng, b=b: eng.activation(
                        out=sg[:, b, :], in_=pin[:, b % 2, :], func=AF.Sign,
                        bias=self.consts[:, 1:2], scale=1.0),
                        waits=[(cw, wD), (self.cA, self.cA.n), (self.cS, sPrev)])
                aSg = self.cA.n
                def fmul(eng, ncb=ncb):
                    last = None
                    for b in range(ncb):
                        last = eng.tensor_scalar(out=sg[:, b, :], in0=sg[:, b, :],
                                                 scalar1=alf[:, 0:1], scalar2=None, op0=ALU.mult)
                        self.ve(eng, last)
                    return eng.nop()
                self.body("vector", fmul, waits=[(self.cA, aSg), (self.cV, self.cV.n)])
                vMul = self.cV.n
                def ftr(eng, c=c, ncb=ncb, scrTv=scrTv):
                    for b in range(ncb):
                        for j in range(8):
                            self.dS(eng, self.qsc[:, 3, j * 128:(j + 1) * 128],
                                    sg[:, b, j * 128:(j + 1) * 128], transpose=True)
                        eng.wait_ge(self.cS.sem, self.cS.n)
                        self.dS(eng, scrTv[:, b * 8:b * 8 + 8, c * 128:(c + 1) * 128],
                                self.qsc[:, 3, :].rearrange("p (j i) -> p j i", i=128))
                        eng.wait_ge(self.cS.sem, self.cS.n)
                self.body("sync", ftr, waits=[(self.cV, vMul), (self.cS, self.cS.n)])
                sPrev = self.cS.n
                self._g0_sprev = sPrev
                yield

    def gen_prep_l1(self):
        pin = self.wbufA[:].rearrange("p c v -> p (c v)")[:, 4096:6144].rearrange("p (s w) -> p s w", s=2)
        for w, scrT, R, C in [(self.wg, self.wgT_scr, DFF1, D),
                              (self.wu, self.wuT_scr, DFF1, D),
                              (self.wd, self.wdT_scr, D, DFF1)]:
            nchunk = R // 128
            wv = w.rearrange("(c p) f -> p c f", p=128)
            scrTv = scrT.rearrange("(b p) f -> p b f", p=128)
            cbs = []
            o = 0
            while o < C:
                wdt = min(1024, C - o)
                cbs.append((o, wdt))
                o += wdt
            blocks = [(c, co, wdt) for c in range(nchunk) for (co, wdt) in cbs]
            snapP = self.snap()
            vQ = {}
            for i, (c, co, wdt) in enumerate(blocks):
                cw = self.cWG[i % 2]
                wts = list(snapP) if i < 2 else [(self.cV, vQ[i - 2])]
                self.body("gpsimd", lambda eng, c=c, co=co, wdt=wdt, i=i, cw=cw, wv=wv: cw.inc(
                    eng.dma_start(out=pin[:, i % 2, 0:wdt], in_=wv[:, c, co:co + wdt]), 16) and None,
                    waits=wts, dma_inc=0)
                wD = cw.n
                self._prep_quant_wait = (cw, wD)
                self.emit_quant([pin[:, i % 2, 0:wdt]], None, wdt, twist=False)
                vQ[i] = self.cV.n
                npc = wdt // 128
                def ftr(eng, npc=npc):
                    for j in range(npc):
                        self.dS(eng, self.qsc[:, 3, j * 128:(j + 1) * 128],
                                self.qsc[:, 1, j * 128:(j + 1) * 128], transpose=True)
                self.body("sync", ftr)
                self.body("sync", lambda eng, c=c, co=co, npc=npc, scrTv=scrTv: self.dS(
                    eng, scrTv[:, co // 128:co // 128 + npc, c * 128:(c + 1) * 128],
                    self.qsc[:, 3, 0:npc * 128].rearrange("p (j i) -> p j i", i=128)) and None)
                yield

    def ph_prep_wsum(self):
        wv = self.wout.rearrange("n h d -> (n h) d").rearrange("(c p) d -> p c d", p=128)
        for c in range(NC8):
            self.body("gpsimd", lambda eng, c=c, wv=wv: self.dG(
                eng, self.prep_bf[:, 0:1024], wv[:, c, :]) and None)
            def ft(eng, c=c):
                eng.matmul(self.bank(0)[0:1, :], self.ones1[:], self.prep_bf[:, 0:512],
                           start=(c == 0), stop=(c == NC8 - 1))
                return eng.matmul(self.bank(1)[0:1, :], self.ones1[:], self.prep_bf[:, 512:1024],
                                  start=(c == 0), stop=(c == NC8 - 1))
            self.body("tensor", ft)
        def fev(eng):
            eng.copy(out=self.scrA[0:1, :], in_=self.bank(0)[0:1, :])
            return eng.copy(out=self.scrB[0:1, :], in_=self.bank(1)[0:1, :])
        self.body("scalar", fev)
        def fst(eng):
            eng.dma_start(out=self.row_scr[0:512], in_=self.scrA[0:1, :].rearrange("a b -> (a b)"))
            eng.dma_start(out=self.row_scr[512:1024], in_=self.scrB[0:1, :].rearrange("a b -> (a b)"))
            return eng.dma_start(out=self.wsum_c[:], in_=self.row_scr.rearrange("(c p) -> p c", p=128))
        def fst(eng):
            self.dS(eng, self.row_scr[0:512].unsqueeze(0), self.scrA[0:1, :])
            self.dS(eng, self.row_scr[512:1024].unsqueeze(0), self.scrB[0:1, :])
            eng.wait_ge(self.cS.sem, self.cS.n)
            ins = eng.dma_start(out=self.wsum_c[:], in_=self.row_scr.rearrange("(c p) -> p c", p=128),
                                allow_slow_non_contiguous=True)
            self.cS.inc(ins, 16)
        self.body("sync", fst)

    def ph_load_x(self, s):
        self.body("sync", lambda eng, s=s: self.dS(
            eng, self.x[:], self.x_in[s].rearrange("(c p) q -> p c q", p=128)) and None)

    def ph_layernorm(self):
        S, NQT = self.S, self.NQT
        inv = 1.0 / D
        for qt in range(NQT):
            b0, b1 = (0, 1) if qt % 2 == 0 else (2, 3)
            for dc in range(NC8):
                o = 512 * (dc % 2)
                def fa(eng, qt=qt, dc=dc, o=o):
                    eng.copy(out=self.qsc[:, 0, o:o + 512], in_=self.x[:, dc, qt * 512:(qt + 1) * 512])
                    return eng.activation(out=self.qsc[:, 1, o:o + 512],
                                          in_=self.x[:, dc, qt * 512:(qt + 1) * 512],
                                          func=AF.Square, bias=self.consts[:, 1:2], scale=1.0)
                self.body("scalar", fa)
                def ft(eng, dc=dc, o=o, b0=b0, b1=b1):
                    eng.matmul(self.bank(b0)[0:1, :], self.ones1[:], self.qsc[:, 0, o:o + 512],
                               start=(dc == 0), stop=(dc == NC8 - 1))
                    return eng.matmul(self.bank(b1)[0:1, :], self.ones1[:], self.qsc[:, 1, o:o + 512],
                                      start=(dc == 0), stop=(dc == NC8 - 1))
                self.body("tensor", ft)
            def fe(eng, qt=qt, b0=b0, b1=b1):
                eng.mul(out=self.st_sb[0:1, qt * 512:(qt + 1) * 512],
                        in_=self.bank(b0)[0:1, :], mul=inv)
                return eng.mul(out=self.st_sb[0:1, S + qt * 512:S + (qt + 1) * 512],
                               in_=self.bank(b1)[0:1, :], mul=inv)
            self.body("scalar", fe)
        def fv(eng):
            ins = eng.tensor_tensor(out=self.rowb[0:1, S:2 * S], in0=self.st_sb[0:1, 0:S],
                                    in1=self.st_sb[0:1, 0:S], op=ALU.mult)
            self.ve(eng, ins)
            return eng.tensor_tensor(out=self.st_sb[0:1, S:2 * S], in0=self.st_sb[0:1, S:2 * S],
                                     in1=self.rowb[0:1, S:2 * S], op=ALU.subtract)
        self.body("vector", fv)
        def fs(eng):
            ins = eng.activation(out=self.st_sb[0:1, S:2 * S], in_=self.st_sb[0:1, S:2 * S],
                                 func=AF.Ln, bias=self.consts[0:1, 2:3], scale=1.0)
            self.ae(eng, ins)
            return eng.activation(out=self.rowb[0:1, 0:S], in_=self.st_sb[0:1, S:2 * S],
                                  func=AF.Exp, bias=self.consts[0:1, 1:2], scale=-0.5)
        self.body("scalar", fs)
        self.body("vector", lambda eng: eng.scalar_tensor_tensor(
            out=self.rowb[0:1, S:2 * S], in0=self.st_sb[0:1, 0:S], scalar=-1.0,
            in1=self.rowb[0:1, 0:S], op0=ALU.mult, op1=ALU.mult))
        for qt in range(NQT):
            b0, b1 = (0, 1) if qt % 2 == 0 else (2, 3)
            def fbc(eng, qt=qt, b0=b0, b1=b1):
                eng.matmul(self.bank(b0), self.onesK[:], self.rowb[0:1, qt * 512:(qt + 1) * 512],
                           start=True, stop=True)
                return eng.matmul(self.bank(b1), self.onesK[:],
                                  self.rowb[0:1, S + qt * 512:S + (qt + 1) * 512],
                                  start=True, stop=True)
            self.body("tensor", fbc)
            def fap(eng, qt=qt, b0=b0, b1=b1):
                last = None
                for dc in range(NC8):
                    ins = eng.tensor_tensor(out=self.scrA[:],
                                            in0=self.x[:, dc, qt * 512:(qt + 1) * 512],
                                            in1=self.bank(b0), op=ALU.mult)
                    self.ve(eng, ins)
                    last = eng.tensor_tensor(out=self.xn[:, dc, qt * 512:(qt + 1) * 512],
                                             in0=self.scrA[:], in1=self.bank(b1), op=ALU.add)
                    if dc < NC8 - 1:
                        self.ve(eng, last)
                return last
            self.body("vector", fap)

    def ph_project_v(self, L):
        S = self.S
        if L == 0:
            self.body("gpsimd", lambda eng: self.dG(
                eng, self.wbufA[:], self.wvT.rearrange("(c p) v -> p c v", p=128)) and None)
        else:
            def fwv(eng):
                for c in range(NC8):
                    self.dG(eng, self.wbufA[:, c, :].rearrange("p (n h) -> p n h", n=NH),
                            self.wqkv[:, :, 128:192].rearrange("n (c p) h -> p c n h", p=128)[:, c, :, :])
            self.body("gpsimd", fwv)
        snapV = self.snap()
        aE = {}
        sD = {}
        for qc in range(S // 128):
            b0, b1 = (0, 1) if qc % 2 == 0 else (2, 3)
            wts = list(snapV) if qc < 2 else [(self.cA, aE[qc - 2])]
            def ft(eng, qc=qc, b0=b0, b1=b1):
                last = None
                for half, b in ((0, b0), (1, b1)):
                    for dc in range(NC8):
                        last = eng.matmul(self.bank(b), self.xn[:, dc, qc * 128:(qc + 1) * 128],
                                          self.wbufA[:, dc, half * 512:(half + 1) * 512],
                                          start=(dc == 0), stop=(dc == NC8 - 1))
                return last
            self.body("tensor", ft, waits=wts)
            tT = self.cT.n
            o = 1024 * (qc % 2)
            wts = [(self.cT, tT)] + ([(self.cS, sD[qc - 2])] if qc >= 2 else [])
            def fe(eng, b0=b0, b1=b1, o=o):
                eng.copy(out=self.qsc[:, 0, :].rearrange("p a -> p a")[:, 0:0] if False else
                         self.qsc[:, 0 + 2 * (o // 1024), :][:, 0:512], in_=self.bank(b0))
                return eng.copy(out=self.qsc[:, 0 + 2 * (o // 1024), :][:, 512:1024], in_=self.bank(b1))
            self.body("scalar", fe, waits=wts)
            aE[qc] = self.cA.n
            self.body("sync", lambda eng, qc=qc, o=o: self.dS(
                eng, self.v_scr[qc * 128:(qc + 1) * 128, :],
                self.qsc[:, 0 + 2 * (o // 1024), :]) and None,
                waits=[(self.cA, aE[qc]), (self.cS, self.cS.n)])
            sD[qc] = self.cS.n

    def ph_attention(self, L, drains=None):
        S, NQT = self.S, self.NQT
        self._at = dict(fv=self.cV.n, fa2=self.cA.n)
        for hp in range(NH // 2):
            if L == 0:
                def fwl(eng, hp=hp):
                    self.dG(eng, self.wbufB[:, 0:8, :],
                            self.wqT.rearrange("(c p) k -> p c k", p=128)[:, :, hp * 128:(hp + 1) * 128])
                    self.dG(eng, self.wbufB[:, 8:16, :],
                            self.wkT.rearrange("(c p) k -> p c k", p=128)[:, :, hp * 128:(hp + 1) * 128])
            else:
                def fwl(eng, hp=hp):
                    for e in range(2):
                        self.dG(eng, self.wbufB[:, 0:8, e * 64:(e + 1) * 64],
                                self.wqkv[2 * hp + e, :, 0:64].rearrange("(c p) h -> p c h", p=128))
                        self.dG(eng, self.wbufB[:, 8:16, e * 64:(e + 1) * 64],
                                self.wqkv[2 * hp + e, :, 64:128].rearrange("(c p) h -> p c h", p=128))
            self.body("gpsimd", fwl)
            def fvl(eng, hp=hp):
                self.dS(eng, self.vpair[:, :, 0:64],
                        self.v_scr[:, hp * 128:hp * 128 + 64].rearrange("(k p) v -> p k v", p=128))
                self.dS(eng, self.vpair[:, :, 65:129],
                        self.v_scr[:, hp * 128 + 64:hp * 128 + 128].rearrange("(k p) v -> p k v", p=128))
            self.body("sync", fvl)
            cG_h = self.cG.n
            cS_h = self.cS.n
            snapH = self.snap()
            kev = {}
            for qt in range(NQT):
                bk = 4 if qt % 2 == 0 else 5
                if qt < 2:
                    wts = [(self.cG, cG_h), (self.cA, self._at["fa2"]), (self.cV, self._at["fv"])]
                else:
                    wts = [(self.cA, kev[qt - 2])]
                def ftk(eng, qt=qt, bk=bk):
                    last = None
                    for dc in range(NC8):
                        last = eng.matmul(self.bank(bk), self.wbufB[:, 8 + dc, :],
                                          self.xn[:, dc, qt * 512:(qt + 1) * 512],
                                          start=(dc == 0), stop=(dc == NC8 - 1))
                    return last
                self.body("tensor", ftk, waits=wts)
                tK = self.cT.n
                self.body("scalar", lambda eng, qt=qt, bk=bk: eng.copy(
                    out=self.kpair[:, qt * 512:(qt + 1) * 512], in_=self.bank(bk)),
                    waits=[(self.cT, tK), (self.cA, self.cA.n)])
                kev[qt] = self.cA.n
            for qt in range(NQT):
                nkk = 4 * (qt + 1)
                def ftq(eng, qt=qt):
                    last = None
                    for dc in range(NC8):
                        last = eng.matmul(self.bank(6), self.wbufB[:, dc, :],
                                          self.xn[:, dc, qt * 512:(qt + 1) * 512],
                                          start=(dc == 0), stop=(dc == NC8 - 1))
                    return last
                self.body("tensor", ftq,
                          waits=[(self.cG, cG_h), (self.cA, self._at["fa2"]), (self.cA, kev[qt])])
                tQ = self.cT.n
                self.body("scalar", lambda eng: eng.mul(out=self.qsmall[:], in_=self.bank(6), mul=0.125),
                          waits=[(self.cT, tQ), (self.cA, self.cA.n)])
                self._at["qev"] = self.cA.n
                self._at["cS_h"] = cS_h
                if L == 0:
                    self.attn_l0_tile(hp, qt, nkk)
                else:
                    self.attn_l1_tile(hp, qt, nkk)
                if drains:
                    self.drain(drains, 10 if L == 0 else 2)

    def _emit_sgroup(self, eng, g, qt, base=0):
        last = None
        for j in range(2):
            kc = 2 * g + j
            last = eng.matmul(self.bank(base + j), self.kpair[0:64, kc * 128:(kc + 1) * 128],
                              self.qsmall[0:64, :], start=True, stop=True)
            last = eng.matmul(self.bank(base + 2 + j), self.kpair[64:128, kc * 128:(kc + 1) * 128],
                              self.qsmall[64:128, :], start=True, stop=True)
        return last

    def _group_has_mask(self, g, qt):
        return any(0 <= (2 * g + j) * 128 - qt * 512 < 512 for j in range(2))

    def _emit_masks(self, eng, g, qt, base=0):
        last = None
        for j in range(2):
            kc = 2 * g + j
            o = kc * 128 - qt * 512
            if 0 <= o < 512:
                for hb in (base + j, base + 2 + j):
                    last = eng.tensor_tensor(out=self.bank(hb), in0=self.bank(hb),
                                             in1=self.mskt[:, o // 128, :], op=ALU.add)
        return last

    def attn_l0_tile(self, hp, qt, nkk):
        ng = nkk // 2
        snap0 = self.snap()
        tS = []
        base_V = self.cV.n
        base_A = self.cA.n
        hasm = [self._group_has_mask(g, qt) for g in range(ng)]
        masked = [g for g in range(ng) if hasm[g]]
        vM = {}
        for i, g in enumerate(masked):
            vM[g] = base_V + i + 1
        aE = [base_A + g + 1 for g in range(ng)]

        ftw = [(self.cA, self._at["qev"]), (self.cS, self._at["cS_h"]),
               (self.cV, self._at["fv"])]

        def emit_av(eng, g):
            last = None
            sl = 1024 * (g % 2)
            for j in range(2):
                kc = 2 * g + j
                first = (kc == 0)
                lastk = (kc == nkk - 1)
                eng.matmul(self.bank(4)[0:65, :], self.vpair[:, kc, 0:65],
                           self.ptA_g[:, sl + j * 512:sl + (j + 1) * 512],
                           start=first, stop=lastk, tile_position=(0, 0))
                last = eng.matmul(self.bank(5)[0:65, :], self.vpair[:, kc, 65:130],
                                  self.ptB_g[:, sl + j * 512:sl + (j + 1) * 512],
                                  start=first, stop=lastk, tile_position=(0, 0))
            return last

        def ft(eng):
            last = self._emit_sgroup(eng, 0, qt)
            self.cT.inc(last)
            tS.append(self.cT.n)
            for g in range(1, ng):
                eng.wait_ge(self.cA.sem, aE[g - 1])
                last = self._emit_sgroup(eng, g, qt)
                self.cT.inc(last)
                tS.append(self.cT.n)
                last = emit_av(eng, g - 1)
                if g < ng - 1:
                    self.cT.inc(last)
            eng.wait_ge(self.cA.sem, aE[ng - 1])
            return emit_av(eng, ng - 1)
        self.body("tensor", ft, waits=ftw)

        def fv(eng):
            last = None
            for g in masked:
                eng.wait_ge(self.cT.sem, tS[g])
                last = self._emit_masks(eng, g, qt)
                if g != masked[-1]:
                    self.cV.inc(last)
            return last if last is not None else eng.nop()
        self.body("vector", fv, waits=snap0)
        assert self.cV.n == base_V + max(len(masked), 1), (self.cV.n, base_V, masked)

        def fa(eng):
            last = None
            for g in range(ng):
                eng.wait_ge(self.cT.sem, tS[g])
                if hasm[g]:
                    eng.wait_ge(self.cV.sem, vM[g])
                sl = 1024 * (g % 2)
                eng.activation(out=self.ptA_g[:, sl:sl + 1024], in_=self.psum[:, 0:1024],
                               func=AF.Exp, bias=self.consts[:, 0:1], scale=1.0)
                last = eng.activation(out=self.ptB_g[:, sl:sl + 1024], in_=self.psum[:, 1024:2048],
                                      func=AF.Exp, bias=self.consts[:, 0:1], scale=1.0)
                if g < ng - 1:
                    self.cA.inc(last)
            return last
        self.body("scalar", fa, waits=snap0)
        assert self.cA.n == aE[-1], (self.cA.n, aE)

        def fa2(eng):
            eng.copy(out=self.dAB[0:1, 0:512], in_=self.bank(4)[64:65, :])
            return eng.copy(out=self.dAB[0:1, 512:1024], in_=self.bank(5)[64:65, :])
        self.body("scalar", fa2)
        self._at["fa2"] = self.cA.n
        def ft2(eng):
            eng.matmul(self.bank(7), self.eA[:], self.dAB[0:1, 0:512], start=True, stop=False)
            return eng.matmul(self.bank(7), self.eB[:], self.dAB[0:1, 512:1024], start=False, stop=True)
        self.body("tensor", ft2)
        def fa3(eng):
            ins = eng.activation(out=self.scrA[:], in_=self.bank(7), func=AF.Ln,
                                 bias=self.consts[:, 1:2], scale=1.0)
            self.ae(eng, ins)
            return eng.activation(out=self.scrB[:], in_=self.scrA[:], func=AF.Exp,
                                  bias=self.consts[:, 1:2], scale=-1.0)
        self.body("scalar", fa3)
        def fvE(eng):
            eng.tensor_tensor(
                out=self.attn_fm[0:64, hp, qt * 512:(qt + 1) * 512],
                in0=self.bank(4)[0:64, :], in1=self.scrB[0:64, :], op=ALU.mult)
            return eng.tensor_tensor(
                out=self.attn_fm[64:128, hp, qt * 512:(qt + 1) * 512],
                in0=self.bank(5)[0:64, :], in1=self.scrB[64:128, :], op=ALU.mult)
        self.body("vector", fvE)
        self._at["fv"] = self.cV.n

    def attn_l1_tile(self, hp, qt, nkk):
        ng = nkk // 2
        snap0 = self.snap()
        tS = []
        base_V = self.cV.n
        base_A = self.cA.n
        hasm = [self._group_has_mask(g, qt) for g in range(ng)]
        masked = [g for g in range(ng) if hasm[g]]
        vM = {}
        for i, g in enumerate(masked):
            vM[g] = base_V + i + 1
        aE = [base_A + g + 1 for g in range(ng)]
        bb = lambda g: 4 * (g % 2)

        ftw = [(self.cA, self._at["qev"]), (self.cS, self._at["cS_h"])]

        def ft(eng):
            last = None
            for g in range(ng):
                if g >= 2:
                    eng.wait_ge(self.cA.sem, aE[g - 2])
                last = self._emit_sgroup(eng, g, qt, base=bb(g))
                if g < ng - 1:
                    self.cT.inc(last)
                tS.append(self.cT.n + (1 if g == ng - 1 else 0))
            return last
        self.body("tensor", ft, waits=ftw)

        def fv(eng):
            last = None
            for g in masked:
                eng.wait_ge(self.cT.sem, tS[g])
                last = self._emit_masks(eng, g, qt, base=bb(g))
                if g != masked[-1]:
                    self.cV.inc(last)
            return last if last is not None else eng.nop()
        self.body("vector", fv, waits=snap0)

        def fa(eng):
            last = None
            for g in range(ng):
                eng.wait_ge(self.cT.sem, tS[g])
                if hasm[g]:
                    eng.wait_ge(self.cV.sem, vM[g])
                eng.activation(out=self.ptA_f[:, 2 * g:2 * g + 2, :].rearrange("p a b -> p (a b)"),
                               in_=self.psum[:, bb(g) * 512:(bb(g) + 2) * 512],
                               func=AF.Exp, bias=self.consts[:, 0:1], scale=1.0)
                last = eng.activation(out=self.ptB_f[:, 2 * g:2 * g + 2, :].rearrange("p a b -> p (a b)"),
                                      in_=self.psum[:, (bb(g) + 2) * 512:(bb(g) + 4) * 512],
                                      func=AF.Exp, bias=self.consts[:, 0:1], scale=1.0)
                if g < ng - 1:
                    self.cA.inc(last)
            return last
        self.body("scalar", fa, waits=snap0)

        expL = self.cA.n
        expL = self.cA.n
        for qb in range(4):
            wts = [(self.cA, expL), (self.cA, self._at["fa2"]),
                   (self.cV, self._at["fv"]), (self.cS, self._at["cS_h"])]
            def ftav(eng, qb=qb):
                last = None
                for pt, b, lo in ((self.ptA_f, 4, 0), (self.ptB_f, 5, 65)):
                    for kc in range(nkk):
                        last = eng.matmul(self.bank(b, 0, 65),
                                          pt[:, kc, qb * 128:(qb + 1) * 128],
                                          self.vpair[:, kc, lo:lo + 65],
                                          start=(kc == 0), stop=(kc == nkk - 1))
                return last
            self.body("tensor", ftav, waits=wts)
            def fa2(eng):
                ins = eng.activation(out=self.qsf[:, 0, 0:1], in_=self.bank(4, 64, 65),
                                     func=AF.Ln, bias=self.consts[:, 1:2], scale=1.0)
                self.ae(eng, ins)
                ins = eng.activation(out=self.qsf[:, 0, 1:2], in_=self.qsf[:, 0, 0:1],
                                     func=AF.Exp, bias=self.consts[:, 1:2], scale=-1.0)
                self.ae(eng, ins)
                ins = eng.activation(out=self.qsf[:, 0, 2:3], in_=self.bank(5, 64, 65),
                                     func=AF.Ln, bias=self.consts[:, 1:2], scale=1.0)
                self.ae(eng, ins)
                return eng.activation(out=self.qsf[:, 0, 3:4], in_=self.qsf[:, 0, 2:3],
                                      func=AF.Exp, bias=self.consts[:, 1:2], scale=-1.0)
            self.body("scalar", fa2)
            def fv2(eng):
                eng.tensor_scalar(out=self.tstage[:, 0, :], in0=self.bank(4, 0, 64),
                                  scalar1=self.qsf[:, 0, 1:2], scalar2=None, op0=ALU.mult)
                return eng.tensor_scalar(out=self.tstage[:, 1, :], in0=self.bank(5, 0, 64),
                                         scalar1=self.qsf[:, 0, 3:4], scalar2=None, op0=ALU.mult)
            self.body("vector", fv2)
            self._at["fv"] = self.cV.n
            self._at["fa2"] = self.cA.n
            sbase = qt * 512 + qb * 128
            self.body("sync", lambda eng, hp=hp, sbase=sbase: self.dS(
                eng, self.t_scr[2 * hp:2 * hp + 2, sbase:sbase + 128, :].rearrange("n (j p) h -> p j n h", p=128),
                self.tstage[:].unsqueeze(1)) and None)

    def ph_l0_oproj(self):
        NQT = self.NQT
        self.body("gpsimd", lambda eng: self.dG(
            eng, self.wbufA[:], self.woT.rearrange("(c p) d -> p c d", p=128)) and None)
        for qt in range(NQT):
            for db in range(NC8):
                b = (qt * NC8 + db) % 2
                def ft(eng, qt=qt, db=db, b=b):
                    last = None
                    for vc in range(NC8):
                        last = eng.matmul(self.bank(b), self.wbufA[:, vc, db * 128:(db + 1) * 128],
                                          self.attn_fm[:, vc, qt * 512:(qt + 1) * 512],
                                          start=(vc == 0), stop=(vc == NC8 - 1))
                    return last
                self.body("tensor", ft)
                self.body("vector", lambda eng, qt=qt, db=db, b=b: eng.tensor_tensor(
                    out=self.x[:, db, qt * 512:(qt + 1) * 512],
                    in0=self.x[:, db, qt * 512:(qt + 1) * 512],
                    in1=self.bank(b), op=ALU.add))

    def ph_l0_ffn(self):
        NQT = self.NQT
        bw1 = self.bw1T_scr.rearrange("(c p) f -> p c f", p=128)
        bw2 = self.bw2T_scr.rearrange("(c p) f -> p c f", p=128)
        w1v = self.wbufA[:].rearrange("p c v -> p (c v)")[:, 0:NC8 * 512].rearrange("p (c f) -> p c f", f=512)
        for qt in range(NQT):
            snapF = self.snap()
            tG = {}
            aG = {}
            tT1 = {}
            for hg in range(8):
                cw = self.cW[hg % 2]
                wts = list(snapF) if hg < 2 else [(self.cT, tT1[(hg - 2) * 4 + 3])]
                def fg(eng, hg=hg, cw=cw):
                    cw.inc(eng.dma_start(
                        out=self.wbufA[:, :, 512 * (hg % 2):512 * (hg % 2) + 512],
                        in_=bw1[:, :, hg * 512:(hg + 1) * 512]), 16)
                self.body("sync", fg, waits=wts)
                wG = cw.n
                for hj in range(4):
                    hc = hg * 4 + hj
                    b = hc % 2
                    wts = [(cw, wG)]
                    if hc >= 2:
                        wts.append((self.cA, aG[hc - 2]))
                    elif hg < 1:
                        wts += list(snapF)
                    def ft(eng, hj=hj, b=b, qt=qt, hg=hg):
                        last = None
                        for dc in range(NC8):
                            last = eng.matmul(
                                self.bank(b),
                                self.wbufA[:, dc, 512 * (hg % 2) + hj * 128:512 * (hg % 2) + (hj + 1) * 128],
                                self.xn[:, dc, qt * 512:(qt + 1) * 512],
                                start=(dc == 0), stop=(dc == NC8 - 1))
                        return last
                    self.body("tensor", ft, waits=wts)
                    tT1[hc] = self.cT.n
                    self.body("scalar", lambda eng, hc=hc, b=b: eng.activation(
                        out=self.h_l0[:, hc, :], in_=self.bank(b), func=AF.Gelu,
                        bias=self.b1_sb[:, hc:hc + 1], scale=1.0),
                        waits=[(self.cT, tT1[hc])] + ([(self.cA, self.cA.n)] if hc == 0 else []))
                    aG[hc] = self.cA.n
            tT2 = {}
            vU = {}
            aAll = self.cA.n
            for db in range(NC8):
                for half in range(2):
                    cw = self.cW[half]
                    wts = [(self.cT, tT2.get((db - 1, half), 0))] if db >= 1 else [(self.cT, self.cT.n)]
                    def fg2(eng, db=db, half=half, cw=cw):
                        cw.inc(eng.dma_start(
                            out=self.wbufB[:, 16 * half:16 * half + 16, :],
                            in_=bw2[:, 16 * half:16 * half + 16, db * 128:(db + 1) * 128]), 16)
                    self.body("sync", fg2, waits=wts)
                    tG[(db, half)] = cw.n
                b = 2 + db % 2
                wts = [(self.cW[0], tG[(db, 0)]), (self.cW[1], tG[(db, 1)]), (self.cA, aAll)]
                if db >= 2:
                    wts.append((self.cV, vU[db - 2]))
                def ft2(eng, db=db, b=b):
                    last = None
                    for hc in range(32):
                        last = eng.matmul(self.bank(b), self.wbufB[:, hc, :], self.h_l0[:, hc, :],
                                          start=(hc == 0), stop=(hc == 31))
                        if hc == 15:
                            pass
                    return last
                self.body("tensor", ft2, waits=wts)
                tT2[(db, 0)] = self.cT.n
                tT2[(db, 1)] = self.cT.n
                self.body("vector", lambda eng, qt=qt, db=db, b=b: eng.scalar_tensor_tensor(
                    out=self.x[:, db, qt * 512:(qt + 1) * 512], in0=self.bank(b),
                    scalar=self.b2_sb[:, db:db + 1], in1=self.x[:, db, qt * 512:(qt + 1) * 512],
                    op0=ALU.add, op1=ALU.add), waits=[(self.cT, tT2[(db, 0)]), (self.cV, self.cV.n)])
                vU[db] = self.cV.n

    def ph_l1_scatter(self):
        S = self.S
        for dc in range(NC8):
            G = S // 64
            def fy(eng, dc=dc, G=G):
                self.dS(eng, self.y_sb[0:64, 0:S].rearrange("p (c h) -> p c h", h=64),
                        self.t_scr[2 * dc].rearrange("(a c) h -> a c h", c=G))
                self.dS(eng, self.y_sb[64:128, 0:S].rearrange("p (c h) -> p c h", h=64),
                        self.t_scr[2 * dc + 1].rearrange("(a c) h -> a c h", c=G))
            self.body("sync", fy)
            self.body("vector", lambda eng, dc=dc: eng.scalar_tensor_tensor(
                out=self.x[:, dc, :], in0=self.y_sb[:, 0:S],
                scalar=self.wsum_c[:, dc:dc + 1], in1=self.x[:, dc, :],
                op0=ALU.mult, op1=ALU.add))

    def emit_quant(self, in_aps, out_ap, W, twist, waits=None):
        nb = W // 16
        q0 = self.qsc[:, 0, 0:W]
        extra_w = getattr(self, "_prep_quant_wait", None)
        self._prep_quant_wait = None
        if waits is None:
            waits = self.snap()
        fv1_waits = list(waits) + [(self.cV, self.cV.n), (self.cA, self.cA.n)]

        def fv1(eng):
            if extra_w is not None:
                eng.wait_ge(extra_w[0].sem, extra_w[1])
            if in_aps is not None:
                o = 0
                for ap in in_aps:
                    w = ap.shape[-1]
                    if twist:
                        ins = eng.transpose(out=self.qsc[:, 0, o:o + w], in_=ap)
                    else:
                        ins = eng.tensor_copy(out=self.qsc[:, 0, o:o + w], in_=ap)
                    o += w
                    self.ve(eng, ins)
            ins = eng.tensor_reduce(out=self.qsf[:, 0, 0:nb],
                                    in_=q0.rearrange("p (g s) -> p g s", s=16),
                                    axis=AX.X, op=ALU.max, apply_absolute_value=True)
            self.vq(eng, ins)
            return eng.tensor_scalar(out=self.qsf[:, 1, 0:nb], in0=self.qsf[:, 0, 0:nb],
                                     scalar1=1.0 / 6.0, scalar2=1e-12, op0=ALU.mult, op1=ALU.max)
        self.body("vector", fv1, waits=fv1_waits)
        cV1 = self.cV.n

        def fa1(eng):
            ins = eng.activation(out=self.qsf[:, 2, 0:nb], in_=self.qsf[:, 1, 0:nb],
                                 func=AF.Ln, bias=self.consts[:, 1:2], scale=1.0)
            self.ae(eng, ins)
            eng.activation(out=self.qsf[:, 3, 0:nb], in_=self.qsf[:, 2, 0:nb],
                           func=AF.Exp, bias=self.consts[:, 1:2], scale=-1.0)
            eng.activation(out=self.qsc[:, 1, 0:W], in_=q0,
                           func=AF.Abs, bias=self.consts[:, 1:2], scale=1.0)
            return eng.activation(out=self.qsc[:, 3, 0:W], in_=q0,
                                  func=AF.Sign, bias=self.consts[:, 1:2], scale=1.0)
        self.body("scalar", fa1, waits=[(self.cV, cV1), (self.cA, self.cA.n)])
        cA1 = self.cA.n

        def fv2(eng):
            a = self.qsc[:, 1, 0:W]
            st = self.qsc[:, 2, 0:W]
            a3 = lambda t: t.rearrange("p (g s) -> p g s", s=16)
            rsc = self.qsf[:, 3, 0:nb].unsqueeze(2).broadcast_to((128, nb, 16))
            ins = eng.tensor_tensor(out=a3(a), in0=a3(a), in1=rsc, op=ALU.mult)
            self.vq(eng, ins)
            ins = eng.tensor_scalar(out=st, in0=a, scalar1=2.5, scalar2=None, op0=ALU.is_gt)
            self.vq(eng, ins)
            for th in (3.5, 5.0, 5.0):
                ins = eng.scalar_tensor_tensor(out=st, in0=a, scalar=th, in1=st,
                                               op0=ALU.is_gt, op1=ALU.add)
                self.vq(eng, ins)
            ins = eng.tensor_scalar(out=st, in0=st, scalar1=2.0, scalar2=None, op0=ALU.mult)
            self.vq(eng, ins)
            for th in (0.25, 0.75, 1.25, 1.75):
                ins = eng.scalar_tensor_tensor(out=st, in0=a, scalar=th, in1=st,
                                               op0=ALU.is_gt, op1=ALU.add)
                self.vq(eng, ins)
            ins = eng.tensor_scalar(out=self.qsf[:, 2, 0:nb], in0=self.qsf[:, 1, 0:nb],
                                    scalar1=0.5, scalar2=None, op0=ALU.mult)
            self.vq(eng, ins)
            hsc = self.qsf[:, 2, 0:nb].unsqueeze(2).broadcast_to((128, nb, 16))
            ins = eng.tensor_tensor(out=a3(st), in0=a3(st), in1=hsc, op=ALU.mult)
            self.vq(eng, ins)
            ins = eng.tensor_tensor(out=a, in0=st, in1=self.qsc[:, 3, 0:W], op=ALU.mult)
            self.ve(eng, ins)
            if twist:
                ins = eng.transpose(out=out_ap, in_=a)
                self.ve(eng, ins)
                return eng.nop()
            if out_ap is None:
                return eng.nop()
            return eng.tensor_copy(out=out_ap, in_=a)
        self.body("vector", fv2, waits=[(self.cA, cA1)])

    def ph_l1_ffn(self):
        NQT = self.NQT
        wgT = self.wgT_scr.rearrange("(c p) f -> p c f", p=128)
        wuT = self.wuT_scr.rearrange("(c p) f -> p c f", p=128)
        wdT = self.wdT_scr.rearrange("(c p) f -> p c f", p=128)
        xqp = lambda q, pp: self.wbufA[:, 4 * (q % 2) + pp, :]
        xqc = lambda q, dc: self.wbufA[:, 4 * (q % 2) + dc // 2,
                                       512 * (dc % 2):512 * (dc % 2) + 512]
        tD_prev = None
        for qt in range(NQT):
            qts = slice(qt * 512, (qt + 1) * 512)
            if qt == 0:
                for pp in range(NC8 // 2):
                    self.emit_quant([self.xn[:, 2 * pp, qts], self.xn[:, 2 * pp + 1, qts]],
                                    xqp(qt, pp), 1024, twist=True, waits=[])
            tT, tA, tV, tS = {}, {}, {}, {}
            snapA = self.snap()
            snapA_cV = self.cV.n
            for ic in range(NI):
                sl = 16 * (ic % 2)
                bg, bu = (0, 1) if ic % 2 == 0 else (2, 3)
                if ic >= 2:
                    wts = [(self.cT, tT[ic - 2])]
                elif tD_prev is not None:
                    wts = [(self.cT, tD_prev)]
                else:
                    wts = list(snapA)
                cw = self.cW[ic % 2]
                def fwg(eng, ic=ic, sl=sl, cw=cw):
                    cw.inc(eng.dma_start(out=self.wbufB[:, sl:sl + 8, :],
                                         in_=wgT[:, :, ic * 128:(ic + 1) * 128]), 16)
                    cw.inc(eng.dma_start(out=self.wbufB[:, sl + 8:sl + 16, :],
                                         in_=wuT[:, :, ic * 128:(ic + 1) * 128]), 16)
                self.body("sync", fwg, waits=wts)
                tS[ic] = cw.n
                wts = [(cw, tS[ic])]
                if ic >= 2:
                    wts.append((self.cV, tV[ic - 2]))
                else:
                    wts.append((self.cV, snapA_cV))
                def ft(eng, ic=ic, sl=sl, bg=bg, bu=bu, qt=qt):
                    for dc in range(NC8):
                        eng.matmul(self.bank(bg), self.wbufB[:, sl + dc, :], xqc(qt, dc),
                                   start=(dc == 0), stop=(dc == NC8 - 1))
                    last = None
                    for dc in range(NC8):
                        last = eng.matmul(self.bank(bu), self.wbufB[:, sl + 8 + dc, :], xqc(qt, dc),
                                          start=(dc == 0), stop=(dc == NC8 - 1))
                    return last
                self.body("tensor", ft, waits=wts)
                tT[ic] = self.cT.n
                wts = [(self.cT, tT[ic])]
                if ic >= 1:
                    wts.append((self.cV, tV[ic - 1]))
                self.body("scalar", lambda eng, bg=bg: eng.activation(
                    out=self.scrA[:], in_=self.bank(bg), func=AF.Silu,
                    bias=self.consts[:, 1:2], scale=1.0), waits=wts)
                tA[ic] = self.cA.n
                self.body("vector", lambda eng, ic=ic, bu=bu: eng.tensor_tensor(
                    out=self.hq[:, ic, :], in0=self.scrA[:], in1=self.bank(bu), op=ALU.mult),
                    waits=[(self.cA, tA[ic])])
                tV[ic] = self.cV.n
                if ic % 2 == 1:
                    pp = ic // 2
                    self.emit_quant([self.hq[:, 2 * pp:2 * pp + 2, :].rearrange("p a b -> p (a b)")],
                                    self.hq[:, 2 * pp:2 * pp + 2, :].rearrange("p a b -> p (a b)"),
                                    1024, twist=True, waits=[])
            assert NI % 2 == 0
            vq_last = self.cV.n
            if qt + 1 < NQT:
                nqts = slice((qt + 1) * 512, (qt + 2) * 512)
                for pp in range(NC8 // 2):
                    self.emit_quant([self.xn[:, 2 * pp, nqts], self.xn[:, 2 * pp + 1, nqts]],
                                    xqp(qt + 1, pp), 1024, twist=True,
                                    waits=[(self.cT, tT[NI - 1])])
            tD, vU, wD = {}, {}, {}
            for db in range(NC8):
                cw = self.cW[db % 2]
                wts = [(self.cT, tD[db - 1])] if db >= 1 else [(self.cT, tT[NI - 1])]
                def fds(eng, db=db, cw=cw):
                    cw.inc(eng.dma_start(out=self.wbufB[:, 0:NI, :],
                                         in_=wdT[:, :, db * 128:(db + 1) * 128]), 16)
                self.body("sync", fds, waits=wts)
                wD[db] = cw.n
                b = 2 + db % 2
                if db < 2:
                    wts = [(cw, wD[db]), (self.cV, vq_last)]
                    if db == 0:
                        wts.append((self.cA, tA[NI - 1]))
                else:
                    wts = [(cw, wD[db]), (self.cV, vU[db - 2])]
                def ft2(eng, b=b):
                    last = None
                    for ic in range(NI):
                        last = eng.matmul(self.bank(b), self.wbufB[:, ic, :], self.hq[:, ic, :],
                                          start=(ic == 0), stop=(ic == NI - 1))
                    return last
                self.body("tensor", ft2, waits=wts)
                tD[db] = self.cT.n
                self.body("vector", lambda eng, qt=qt, db=db, b=b: eng.tensor_tensor(
                    out=self.x[:, db, qt * 512:(qt + 1) * 512],
                    in0=self.x[:, db, qt * 512:(qt + 1) * 512],
                    in1=self.bank(b), op=ALU.add), waits=[(self.cT, tD[db])])
                vU[db] = self.cV.n
            tD_prev = tD[NC8 - 1]

    def ph_store_x(self, s):
        self.body("sync", lambda eng, s=s: self.dS(
            eng, self.out[s].rearrange("(c p) q -> p c q", p=128), self.x[:]) and None)

    def ph_final(self):
        snap = self.snap()
        eng = Rec(self.streams["sync"])
        for c, n in snap:
            if n > 0:
                eng.wait_ge(c.sem, n)


def build_masks():
    m = np.zeros((4, 128, 512), dtype=np.float32)
    for oi, o in enumerate((0, 128, 256, 384)):
        i = np.arange(128)[:, None]
        j = np.arange(512)[None, :]
        m[oi] = np.where(i + o <= j, 0.0, NEG)
    return m.astype(ml_dtypes.bfloat16)


def host_inputs(inputs, cfg, core):
    nseq = cfg["nseq"]
    x = np.asarray(inputs["L0_attn_x"], dtype=np.float32)
    xs = x[core * nseq:(core + 1) * nseq]
    x_t = np.ascontiguousarray(xs.transpose(0, 2, 1))
    f32 = lambda a: np.ascontiguousarray(np.asarray(a, dtype=np.float32))
    tr = lambda a: np.ascontiguousarray(np.asarray(a, dtype=np.float32).T)
    return {
        "x_t": x_t,
        "msk": build_masks(),
        "L0_wqT": tr(inputs["L0_attn_q_proj_weight"]),
        "L0_wkT": tr(inputs["L0_attn_k_proj_weight"]),
        "L0_wvT": tr(inputs["L0_attn_v_proj_weight"]),
        "L0_woT": tr(inputs["L0_attn_o_proj_weight"]),
        "L0_w1": f32(inputs["L0_ffn_w1"]),
        "L0_w2": f32(inputs["L0_ffn_w2"]),
        "L0_b1h": np.ascontiguousarray(f32(inputs["L0_ffn_b1"]).reshape(32, 128).T),
        "L0_b2h": np.ascontiguousarray(f32(inputs["L0_ffn_b2"]).reshape(8, 128).T),
        "L1_wqkv": f32(inputs["L1_attn_W_qkv"]),
        "L1_wout": f32(inputs["L1_attn_W_out"]),
        "L1_wg": f32(inputs["L1_ffn_w_gate"]),
        "L1_wu": f32(inputs["L1_ffn_w_up"]),
        "L1_wd": f32(inputs["L1_ffn_w_down"]),
    }


def kernel(**inputs):
    cfg = default_cfg()
    p = Prog(cfg)
    nc = p.build()
    in_maps = [host_inputs(inputs, cfg, core) for core in range(8)]
    res = run_bass_kernel_spmd(nc, in_maps, core_ids=list(range(8)))
    outs = []
    for core in range(8):
        o = np.asarray(res.results[core]["out_t"], dtype=np.float32)
        outs.append(o.transpose(0, 2, 1))
    return np.ascontiguousarray(np.concatenate(outs, axis=0))


# version 15
# speedup vs baseline: 1.0409x; 1.0106x over previous
import sys
sys.path.insert(0, "/opt/trn_rl_repo")
import contextlib
import numpy as np
import ml_dtypes
import concourse.bass as bass
import concourse.mybir as mybir
from concourse.bass_utils import run_bass_kernel_spmd

F32 = mybir.dt.float32
BF16 = mybir.dt.bfloat16
AF = mybir.ActivationFunctionType
ALU = mybir.AluOpType
AX = mybir.AxisListType

D = 1024
NH = 16
HD = 64
DFF0 = 4096
DFF1 = 2816
NC8 = D // 128
NI = DFF1 // 128
EXP_C = 16.0
NEG = -1e10


def default_cfg():
    return dict(S=2048, nseq=4, layers=(0, 1), dumps=(), skip_attn=False, skip_ffn=False)


class Ctr:
    def __init__(self, es, nc, name):
        self.sem = es.enter_context(nc.semaphore(name))
        self.n = 0

    def inc(self, ins, k=1):
        ins.then_inc(self.sem, k)
        self.n += k


class FakeIns:
    __slots__ = ("incs",)

    def __init__(self):
        self.incs = []

    def then_inc(self, sem, v):
        self.incs.append((sem, v))
        return self


class Rec:

    def __init__(self, stream):
        self._stream = stream

    def __getattr__(self, name):
        st = self._stream

        def call(*a, **k):
            fi = FakeIns()
            st.append((name, a, k, fi))
            return fi
        return call


class Prog:
    def __init__(self, cfg):
        self.cfg = cfg
        self.S = cfg["S"]
        self.NQT = self.S // 512
        self.NKC = self.S // 128
        self.nc = bass.Bass("TRN2", target_bir_lowering=False)
        self.es = contextlib.ExitStack()
        self.dump_names = []

    def sb(self, name, shape, dt):
        return self.es.enter_context(self.nc.sbuf_tensor(name, shape, dt))

    def bank(self, i, lo=0, hi=512):
        return self.psum[:, i * 512 + lo:i * 512 + hi]

    def build(self):
        cfg, S, nc = self.cfg, self.S, self.nc
        di = lambda n, s, dt=F32: nc.dram_tensor(n, list(s), dt, kind="ExternalInput").ap()
        dt_ = lambda n, s, dt: nc.dram_tensor(n, list(s), dt, kind="Internal").ap()
        self.x_in = di("x_t", (cfg["nseq"], D, S))
        self.out = nc.dram_tensor("out_t", [cfg["nseq"], D, S], F32, kind="ExternalOutput").ap()
        self.msk = di("msk", (4, 128, 512), BF16)
        self.wqT = di("L0_wqT", (D, D))
        self.wkT = di("L0_wkT", (D, D))
        self.wvT = di("L0_wvT", (D, D))
        self.woT = di("L0_woT", (D, D))
        self.w1n = di("L0_w1", (DFF0, D))
        self.w2n = di("L0_w2", (D, DFF0))
        self.b1h = di("L0_b1h", (128, DFF0 // 128))
        self.b2h = di("L0_b2h", (128, NC8))
        self.wqkv = di("L1_wqkv", (NH, D, 192))
        self.wout = di("L1_wout", (NH, HD, D))
        self.wg = di("L1_wg", (DFF1, D))
        self.wu = di("L1_wu", (DFF1, D))
        self.wd = di("L1_wd", (D, DFF1))

        self.v_scr = dt_("v_scr", (S, D), BF16)
        self.t_scr = dt_("t_scr", (NH, S, HD), BF16)
        self.bw1T_scr = dt_("bw1T_scr", (D, DFF0), BF16)
        self.bw2T_scr = dt_("bw2T_scr", (DFF0, D), BF16)
        self.wgT_scr = dt_("wgT_scr", (D, DFF1), BF16)
        self.wuT_scr = dt_("wuT_scr", (D, DFF1), BF16)
        self.wdT_scr = dt_("wdT_scr", (DFF1, D), BF16)
        self.row_scr = dt_("row_scr", (1024,), F32)

        self.x = self.sb("x", [128, NC8, S], F32)
        self.xn = self.sb("xn", [128, NC8, S], BF16)
        self.arena = self.sb("arena", [128, 20480], BF16)
        self.kpair = self.sb("kpair", [128, S], BF16)
        self.qsmall = self.sb("qsmall", [128, 512], BF16)
        self.vpair = self.sb("vpair", [128, self.NKC, 132], BF16)
        self.wbufA = self.sb("wbufA", [128, NC8, 1024], BF16)
        self.wbufB = self.sb("wbufB", [128, 32, 128], BF16)
        self.mskt = self.sb("mskt", [128, 4, 512], BF16)
        self.consts = self.sb("consts", [128, 4], F32)
        self.ones1 = self.sb("ones1", [128, 1], BF16)
        self.onesK = self.sb("onesK", [1, 128], BF16)
        self.eA = self.sb("eA", [1, 128], BF16)
        self.eB = self.sb("eB", [1, 128], BF16)
        self.rowb = self.sb("rowb", [1, 2 * max(S, 2048)], BF16)
        self.st_sb = self.sb("st_sb", [1, 2 * S], BF16)
        self.dAB = self.sb("dAB", [1, 1024], BF16)
        self.wsum_c = self.sb("wsum_c", [128, NC8], F32)
        self.scrA = self.sb("scrA", [128, 512], F32)
        self.scrB = self.sb("scrB", [128, 512], F32)
        self.qsc = self.sb("qsc", [128, 4, 1024], BF16)
        self.qsf = self.sb("qsf", [128, 4, 64], F32)
        self.h_stage = self.sb("h_stage", [128, 512], BF16)
        self.tstage = self.sb("tstage", [128, 2, 64], BF16)
        self.b1_sb = self.sb("b1_sb", [128, 32], F32)
        self.b2_sb = self.sb("b2_sb", [128, NC8], F32)
        self.psum = self.es.enter_context(self.nc.psum_tensor("psum", [128, 4096], F32))

        A = self.arena
        self.attn_fm = A[:, 0:NC8 * S].rearrange("p (c q) -> p c q", c=NC8)
        off = NC8 * S
        self.ptA_g = A[:, off:off + 2048]
        self.ptB_g = A[:, off + 2048:off + 4096]
        self.ptA_f = A[:, 0:self.NKC * 512].rearrange("p (k r) -> p k r", r=512)
        self.ptB_f = A[:, self.NKC * 512:2 * self.NKC * 512].rearrange("p (k r) -> p k r", r=512)
        self.h_l0 = A[:, 0:32 * 512].rearrange("p (k r) -> p k r", r=512)
        self.xq = A[:, 0:NC8 * 512].rearrange("p (k r) -> p k r", r=512)
        self.hq = A[:, NC8 * 512:(NC8 + NI) * 512].rearrange("p (k r) -> p k r", r=512)
        self.prep_bf = A[:, 0:4096]
        self.y_sb = self.qsc[:, 0:2, :].rearrange("p a b -> p (a b)")

        es = self.es
        nc_ = self.nc
        self.cT = Ctr(es, nc_, "cT")
        self.cV = Ctr(es, nc_, "cV")
        self.cA = Ctr(es, nc_, "cA")
        self.cG = Ctr(es, nc_, "cG")
        self.cS = Ctr(es, nc_, "cS")
        self.cW = [Ctr(es, nc_, "cW0"), Ctr(es, nc_, "cW1")]
        self.cWG = [Ctr(es, nc_, "cWG0"), Ctr(es, nc_, "cWG1")]
        self.all_ctrs = [self.cT, self.cV, self.cA, self.cG, self.cS] + self.cW + self.cWG

        self.streams = {k: [] for k in ("tensor", "vector", "scalar", "gpsimd", "sync")}
        self.emit_all()
        with es:
            with nc.Block() as blk:
                for kind in ("tensor", "vector", "scalar", "gpsimd", "sync"):
                    getattr(blk, kind)(self._replayer(kind))
        return nc

    def _replayer(self, kind):
        stream = self.streams[kind]

        def run(eng):
            seen = {}
            for name, a, k, fi in stream:
                if name == "wait_ge":
                    sem, v = a
                    key = id(sem)
                    if seen.get(key, -1) >= v:
                        continue
                    seen[key] = v
                    eng.wait_ge(sem, v)
                    continue
                ins = getattr(eng, name)(*a, **k)
                for sem, v in fi.incs:
                    ins.then_inc(sem, v)
        return run

    def snap(self):
        return [(c, c.n) for c in self.all_ctrs]

    def body(self, kind, fn, dma_inc=None, waits=None):
        ctr = {"tensor": self.cT, "vector": self.cV, "scalar": self.cA,
               "gpsimd": self.cG, "sync": self.cS}[kind]
        is_dma = kind in ("gpsimd", "sync")
        if waits is None:
            waits = self.snap()
        waits = [(c, n) for (c, n) in waits if n > 0]
        eng = Rec(self.streams[kind])
        for c, n in waits:
            eng.wait_ge(c.sem, n)
        last = fn(eng)
        if is_dma:
            assert last is None, "DMA bodies must inc per-DMA via dS/dG"
        else:
            ctr.inc(last, 1)

    def dS(self, eng, out, in_, transpose=False):
        if transpose:
            ins = eng.dma_start_transpose(out=out, in_=in_)
        else:
            ins = eng.dma_start(out=out, in_=in_)
        self.cS.inc(ins, 16)
        return ins

    def dG(self, eng, out, in_):
        ins = eng.dma_start(out=out, in_=in_)
        self.cG.inc(ins, 16)
        return ins

    QUANT_EDGES = False

    def ve(self, eng, ins):
        self.cV.inc(ins)
        eng.wait_ge(self.cV.sem, self.cV.n)

    def vq(self, eng, ins):
        if self.QUANT_EDGES:
            self.ve(eng, ins)

    def ae(self, eng, ins):
        self.cA.inc(ins)
        eng.wait_ge(self.cA.sem, self.cA.n)

    def dump(self, name, ap, shape, dt=F32):
        if name not in self.cfg["dumps"]:
            return
        o = self.nc.dram_tensor("dump_" + name, list(shape), dt, kind="ExternalOutput").ap()
        self.dump_names.append("dump_" + name)
        self.body("sync", lambda eng: self.dS(eng, o, ap) and None)

    def drain(self, gens, n):
        for _ in range(n):
            for g in list(gens):
                try:
                    next(g)
                    break
                except StopIteration:
                    gens.remove(g)

    def drain_all(self, gens):
        for g in list(gens):
            for _ in g:
                pass
            gens.remove(g)

    def gen_xq0(self):
        for pp in range(NC8 // 2):
            self.emit_quant([self.xn[:, 2 * pp, 0:512], self.xn[:, 2 * pp + 1, 0:512]],
                            self.wbufA[:, pp, :], 1024, twist=True, waits=[])
            yield
            yield

    def emit_all(self):
        cfg = self.cfg
        self.ph_init()
        if 1 in cfg["layers"]:
            self.ph_prep_wsum()
        self._g0 = [self.gen_prep_l0()] if 0 in cfg["layers"] else []
        self._g1 = [self.gen_prep_l1()] if 1 in cfg["layers"] else []
        for s in range(cfg["nseq"]):
            self.ph_load_x(s)
            for L in cfg["layers"]:
                self.ph_layernorm()
                self.dump(f"xn{L}_s{s}", self.xn[:].rearrange("p c q -> p (c q)"),
                          (128, NC8 * self.S), BF16)
                xq0_drained = False
                if not cfg.get("skip_attn"):
                    self.ph_project_v(L)
                    if s == 0:
                        dr = self._g0 if L == 0 else self._g1
                    elif L == 1 and not cfg.get("skip_ffn"):
                        dr = [self.gen_xq0()]
                        xq0_drained = True
                    else:
                        dr = None
                    self.ph_attention(L, drains=dr)
                if L == 0:
                    self.drain_all(self._g0)
                    if not cfg.get("skip_attn"):
                        self.dump(f"afm_s{s}", self.attn_fm[:].rearrange("p c q -> p (c q)"),
                                  (128, NC8 * self.S), BF16)
                        self.ph_l0_oproj()
                    if not cfg.get("skip_ffn"):
                        self.ph_l0_ffn()
                else:
                    self.drain_all(self._g1)
                    if not cfg.get("skip_attn"):
                        self.ph_l1_scatter()
                    if not cfg.get("skip_ffn"):
                        self.ph_l1_ffn(xq0_drained)
                self.dump(f"x{L}_s{s}", self.x[:].rearrange("p c q -> p (c q)"),
                          (128, NC8 * self.S), F32)
            self.ph_store_x(s)
        self.ph_final()

    def ph_init(self):
        def fv(eng):
            eng.memset(self.consts[:, 0:1], -EXP_C)
            eng.memset(self.consts[:, 1:2], 0.0)
            eng.memset(self.consts[:, 2:3], 1e-5)
            eng.memset(self.ones1[:], 1.0)
            eng.memset(self.onesK[:], 1.0)
            eng.memset(self.eA[:, 0:64], 1.0)
            eng.memset(self.eA[:, 64:128], 0.0)
            eng.memset(self.eB[:, 0:64], 0.0)
            eng.memset(self.eB[:, 64:128], 1.0)
            eng.memset(self.vpair[:, :, 64:65].rearrange("p a b -> p (a b)"), 1.0)
            return eng.memset(self.vpair[:, :, 129:130].rearrange("p a b -> p (a b)"), 1.0)
        self.body("vector", fv)

        def fs(eng):
            self.dS(eng, self.mskt[:], self.msk.rearrange("m p r -> p m r"))
            self.dS(eng, self.b1_sb[:], self.b1h)
            self.dS(eng, self.b2_sb[:], self.b2h)
        self.body("sync", fs)

    def gen_prep_l0(self):
        wflat = self.wbufA[:].rearrange("p c v -> p (c v)")
        pin = wflat[:, 0:2048].rearrange("p (s w) -> p s w", s=2)
        sg = wflat[:, 2048:6144].rearrange("p (s w) -> p s w", s=4)
        alf = self.qsf[:, 0, 0:4]
        snapP = self.snap()
        first = True
        for wn, scr, R, F in [(self.w1n, self.bw1T_scr, DFF0, D),
                              (self.w2n, self.bw2T_scr, D, DFF0)]:
            nchunk = R // 128
            ncb = F // 1024
            wv = wn.rearrange("(c p) f -> p c f", p=128)
            scrTv = scr.rearrange("(b p) f -> p b f", p=128)
            sPrev = getattr(self, "_g0_sprev", 0)
            for c in range(nchunk):
                for b in range(ncb):
                    cw = self.cWG[b % 2]
                    wts = list(snapP) if first else [(self.cA, self.cA.n), (self.cV, self.cV.n)]
                    first = False
                    def fdma(eng, c=c, b=b, cw=cw, wv=wv):
                        cw.inc(eng.dma_start(out=pin[:, b % 2, :],
                                             in_=wv[:, c, b * 1024:(b + 1) * 1024]), 16)
                    self.body("gpsimd", fdma, waits=wts)
                    wD = cw.n
                    def fred(eng, b=b, F=F):
                        ins = eng.tensor_reduce(out=alf[:, 2:3], in_=pin[:, b % 2, :],
                                                axis=AX.X, op=ALU.add, apply_absolute_value=True)
                        self.ve(eng, ins)
                        if b == 0:
                            return eng.tensor_scalar(out=alf[:, 0:1], in0=alf[:, 2:3],
                                                     scalar1=1.0 / F, scalar2=None, op0=ALU.mult)
                        return eng.scalar_tensor_tensor(out=alf[:, 0:1], in0=alf[:, 2:3],
                                                        scalar=1.0 / F, in1=alf[:, 0:1],
                                                        op0=ALU.mult, op1=ALU.add)
                    self.body("vector", fred, waits=[(cw, wD), (self.cV, self.cV.n)])
                    self.body("scalar", lambda eng, b=b: eng.activation(
                        out=sg[:, b, :], in_=pin[:, b % 2, :], func=AF.Sign,
                        bias=self.consts[:, 1:2], scale=1.0),
                        waits=[(cw, wD), (self.cA, self.cA.n), (self.cS, sPrev)])
                aSg = self.cA.n
                def fmul(eng, ncb=ncb):
                    last = None
                    for b in range(ncb):
                        last = eng.tensor_scalar(out=sg[:, b, :], in0=sg[:, b, :],
                                                 scalar1=alf[:, 0:1], scalar2=None, op0=ALU.mult)
                        self.ve(eng, last)
                    return eng.nop()
                self.body("vector", fmul, waits=[(self.cA, aSg), (self.cV, self.cV.n)])
                vMul = self.cV.n
                def ftr(eng, c=c, ncb=ncb, scrTv=scrTv):
                    for b in range(ncb):
                        for j in range(8):
                            self.dS(eng, self.qsc[:, 3, j * 128:(j + 1) * 128],
                                    sg[:, b, j * 128:(j + 1) * 128], transpose=True)
                        eng.wait_ge(self.cS.sem, self.cS.n)
                        self.dS(eng, scrTv[:, b * 8:b * 8 + 8, c * 128:(c + 1) * 128],
                                self.qsc[:, 3, :].rearrange("p (j i) -> p j i", i=128))
                        eng.wait_ge(self.cS.sem, self.cS.n)
                self.body("sync", ftr, waits=[(self.cV, vMul), (self.cS, self.cS.n)])
                sPrev = self.cS.n
                self._g0_sprev = sPrev
                yield

    def gen_prep_l1(self):
        pin = self.wbufA[:].rearrange("p c v -> p (c v)")[:, 4096:6144].rearrange("p (s w) -> p s w", s=2)
        for w, scrT, R, C in [(self.wg, self.wgT_scr, DFF1, D),
                              (self.wu, self.wuT_scr, DFF1, D),
                              (self.wd, self.wdT_scr, D, DFF1)]:
            nchunk = R // 128
            wv = w.rearrange("(c p) f -> p c f", p=128)
            scrTv = scrT.rearrange("(b p) f -> p b f", p=128)
            cbs = []
            o = 0
            while o < C:
                wdt = min(1024, C - o)
                cbs.append((o, wdt))
                o += wdt
            blocks = [(c, co, wdt) for c in range(nchunk) for (co, wdt) in cbs]
            snapP = self.snap()
            vQ = {}
            for i, (c, co, wdt) in enumerate(blocks):
                cw = self.cWG[i % 2]
                wts = list(snapP) if i < 2 else [(self.cV, vQ[i - 2])]
                self.body("gpsimd", lambda eng, c=c, co=co, wdt=wdt, i=i, cw=cw, wv=wv: cw.inc(
                    eng.dma_start(out=pin[:, i % 2, 0:wdt], in_=wv[:, c, co:co + wdt]), 16) and None,
                    waits=wts, dma_inc=0)
                wD = cw.n
                self._prep_quant_wait = (cw, wD)
                self.emit_quant([pin[:, i % 2, 0:wdt]], None, wdt, twist=False)
                vQ[i] = self.cV.n
                npc = wdt // 128
                def ftr(eng, npc=npc):
                    for j in range(npc):
                        self.dS(eng, self.qsc[:, 3, j * 128:(j + 1) * 128],
                                self.qsc[:, 1, j * 128:(j + 1) * 128], transpose=True)
                self.body("sync", ftr)
                self.body("sync", lambda eng, c=c, co=co, npc=npc, scrTv=scrTv: self.dS(
                    eng, scrTv[:, co // 128:co // 128 + npc, c * 128:(c + 1) * 128],
                    self.qsc[:, 3, 0:npc * 128].rearrange("p (j i) -> p j i", i=128)) and None)
                yield

    def ph_prep_wsum(self):
        wv = self.wout.rearrange("n h d -> (n h) d").rearrange("(c p) d -> p c d", p=128)
        for c in range(NC8):
            self.body("gpsimd", lambda eng, c=c, wv=wv: self.dG(
                eng, self.prep_bf[:, 0:1024], wv[:, c, :]) and None)
            def ft(eng, c=c):
                eng.matmul(self.bank(0)[0:1, :], self.ones1[:], self.prep_bf[:, 0:512],
                           start=(c == 0), stop=(c == NC8 - 1))
                return eng.matmul(self.bank(1)[0:1, :], self.ones1[:], self.prep_bf[:, 512:1024],
                                  start=(c == 0), stop=(c == NC8 - 1))
            self.body("tensor", ft)
        def fev(eng):
            eng.copy(out=self.scrA[0:1, :], in_=self.bank(0)[0:1, :])
            return eng.copy(out=self.scrB[0:1, :], in_=self.bank(1)[0:1, :])
        self.body("scalar", fev)
        def fst(eng):
            eng.dma_start(out=self.row_scr[0:512], in_=self.scrA[0:1, :].rearrange("a b -> (a b)"))
            eng.dma_start(out=self.row_scr[512:1024], in_=self.scrB[0:1, :].rearrange("a b -> (a b)"))
            return eng.dma_start(out=self.wsum_c[:], in_=self.row_scr.rearrange("(c p) -> p c", p=128))
        def fst(eng):
            self.dS(eng, self.row_scr[0:512].unsqueeze(0), self.scrA[0:1, :])
            self.dS(eng, self.row_scr[512:1024].unsqueeze(0), self.scrB[0:1, :])
            eng.wait_ge(self.cS.sem, self.cS.n)
            ins = eng.dma_start(out=self.wsum_c[:], in_=self.row_scr.rearrange("(c p) -> p c", p=128),
                                allow_slow_non_contiguous=True)
            self.cS.inc(ins, 16)
        self.body("sync", fst)

    def ph_load_x(self, s):
        self.body("sync", lambda eng, s=s: self.dS(
            eng, self.x[:], self.x_in[s].rearrange("(c p) q -> p c q", p=128)) and None)

    def ph_layernorm(self):
        S, NQT = self.S, self.NQT
        inv = 1.0 / D
        for qt in range(NQT):
            b0, b1 = (0, 1) if qt % 2 == 0 else (2, 3)
            for dc in range(NC8):
                o = 512 * (dc % 2)
                def fa(eng, qt=qt, dc=dc, o=o):
                    eng.copy(out=self.qsc[:, 0, o:o + 512], in_=self.x[:, dc, qt * 512:(qt + 1) * 512])
                    return eng.activation(out=self.qsc[:, 1, o:o + 512],
                                          in_=self.x[:, dc, qt * 512:(qt + 1) * 512],
                                          func=AF.Square, bias=self.consts[:, 1:2], scale=1.0)
                self.body("scalar", fa)
                def ft(eng, dc=dc, o=o, b0=b0, b1=b1):
                    eng.matmul(self.bank(b0)[0:1, :], self.ones1[:], self.qsc[:, 0, o:o + 512],
                               start=(dc == 0), stop=(dc == NC8 - 1))
                    return eng.matmul(self.bank(b1)[0:1, :], self.ones1[:], self.qsc[:, 1, o:o + 512],
                                      start=(dc == 0), stop=(dc == NC8 - 1))
                self.body("tensor", ft)
            def fe(eng, qt=qt, b0=b0, b1=b1):
                eng.mul(out=self.st_sb[0:1, qt * 512:(qt + 1) * 512],
                        in_=self.bank(b0)[0:1, :], mul=inv)
                return eng.mul(out=self.st_sb[0:1, S + qt * 512:S + (qt + 1) * 512],
                               in_=self.bank(b1)[0:1, :], mul=inv)
            self.body("scalar", fe)
        def fv(eng):
            ins = eng.tensor_tensor(out=self.rowb[0:1, S:2 * S], in0=self.st_sb[0:1, 0:S],
                                    in1=self.st_sb[0:1, 0:S], op=ALU.mult)
            self.ve(eng, ins)
            return eng.tensor_tensor(out=self.st_sb[0:1, S:2 * S], in0=self.st_sb[0:1, S:2 * S],
                                     in1=self.rowb[0:1, S:2 * S], op=ALU.subtract)
        self.body("vector", fv)
        def fs(eng):
            ins = eng.activation(out=self.st_sb[0:1, S:2 * S], in_=self.st_sb[0:1, S:2 * S],
                                 func=AF.Ln, bias=self.consts[0:1, 2:3], scale=1.0)
            self.ae(eng, ins)
            return eng.activation(out=self.rowb[0:1, 0:S], in_=self.st_sb[0:1, S:2 * S],
                                  func=AF.Exp, bias=self.consts[0:1, 1:2], scale=-0.5)
        self.body("scalar", fs)
        self.body("vector", lambda eng: eng.scalar_tensor_tensor(
            out=self.rowb[0:1, S:2 * S], in0=self.st_sb[0:1, 0:S], scalar=-1.0,
            in1=self.rowb[0:1, 0:S], op0=ALU.mult, op1=ALU.mult))
        for qt in range(NQT):
            b0, b1 = (0, 1) if qt % 2 == 0 else (2, 3)
            def fbc(eng, qt=qt, b0=b0, b1=b1):
                eng.matmul(self.bank(b0), self.onesK[:], self.rowb[0:1, qt * 512:(qt + 1) * 512],
                           start=True, stop=True)
                return eng.matmul(self.bank(b1), self.onesK[:],
                                  self.rowb[0:1, S + qt * 512:S + (qt + 1) * 512],
                                  start=True, stop=True)
            self.body("tensor", fbc)
            def fap(eng, qt=qt, b0=b0, b1=b1):
                last = None
                for dc in range(NC8):
                    ins = eng.tensor_tensor(out=self.scrA[:],
                                            in0=self.x[:, dc, qt * 512:(qt + 1) * 512],
                                            in1=self.bank(b0), op=ALU.mult)
                    self.ve(eng, ins)
                    last = eng.tensor_tensor(out=self.xn[:, dc, qt * 512:(qt + 1) * 512],
                                             in0=self.scrA[:], in1=self.bank(b1), op=ALU.add)
                    if dc < NC8 - 1:
                        self.ve(eng, last)
                return last
            self.body("vector", fap)

    def ph_project_v(self, L):
        S = self.S
        if L == 0:
            self.body("gpsimd", lambda eng: self.dG(
                eng, self.wbufA[:], self.wvT.rearrange("(c p) v -> p c v", p=128)) and None)
        else:
            def fwv(eng):
                for c in range(NC8):
                    self.dG(eng, self.wbufA[:, c, :].rearrange("p (n h) -> p n h", n=NH),
                            self.wqkv[:, :, 128:192].rearrange("n (c p) h -> p c n h", p=128)[:, c, :, :])
            self.body("gpsimd", fwv)
        snapV = self.snap()
        aE = {}
        sD = {}
        for qc in range(S // 128):
            b0, b1 = (0, 1) if qc % 2 == 0 else (2, 3)
            wts = list(snapV) if qc < 2 else [(self.cA, aE[qc - 2])]
            def ft(eng, qc=qc, b0=b0, b1=b1):
                last = None
                for half, b in ((0, b0), (1, b1)):
                    for dc in range(NC8):
                        last = eng.matmul(self.bank(b), self.xn[:, dc, qc * 128:(qc + 1) * 128],
                                          self.wbufA[:, dc, half * 512:(half + 1) * 512],
                                          start=(dc == 0), stop=(dc == NC8 - 1))
                return last
            self.body("tensor", ft, waits=wts)
            tT = self.cT.n
            o = 1024 * (qc % 2)
            wts = [(self.cT, tT)] + ([(self.cS, sD[qc - 2])] if qc >= 2 else [])
            def fe(eng, b0=b0, b1=b1, o=o):
                eng.copy(out=self.qsc[:, 0, :].rearrange("p a -> p a")[:, 0:0] if False else
                         self.qsc[:, 0 + 2 * (o // 1024), :][:, 0:512], in_=self.bank(b0))
                return eng.copy(out=self.qsc[:, 0 + 2 * (o // 1024), :][:, 512:1024], in_=self.bank(b1))
            self.body("scalar", fe, waits=wts)
            aE[qc] = self.cA.n
            self.body("sync", lambda eng, qc=qc, o=o: self.dS(
                eng, self.v_scr[qc * 128:(qc + 1) * 128, :],
                self.qsc[:, 0 + 2 * (o // 1024), :]) and None,
                waits=[(self.cA, aE[qc]), (self.cS, self.cS.n)])
            sD[qc] = self.cS.n

    def ph_attention(self, L, drains=None):
        S, NQT = self.S, self.NQT
        self._at = dict(fv=self.cV.n, fa2=self.cA.n)

        def fwl(eng, hp, ro):
            if L == 0:
                self.dG(eng, self.wbufB[:, ro:ro + 8, :],
                        self.wqT.rearrange("(c p) k -> p c k", p=128)[:, :, hp * 128:(hp + 1) * 128])
                self.dG(eng, self.wbufB[:, ro + 8:ro + 16, :],
                        self.wkT.rearrange("(c p) k -> p c k", p=128)[:, :, hp * 128:(hp + 1) * 128])
            else:
                for e in range(2):
                    self.dG(eng, self.wbufB[:, ro:ro + 8, e * 64:(e + 1) * 64],
                            self.wqkv[2 * hp + e, :, 0:64].rearrange("(c p) h -> p c h", p=128))
                    self.dG(eng, self.wbufB[:, ro + 8:ro + 16, e * 64:(e + 1) * 64],
                            self.wqkv[2 * hp + e, :, 64:128].rearrange("(c p) h -> p c h", p=128))
        cG_hp = {}
        for hp in range(NH // 2):
            ro = 16 * (hp % 2)
            if hp == 0:
                self.body("gpsimd", lambda eng: fwl(eng, 0, 0))
                cG_hp[0] = self.cG.n
            if hp + 1 < NH // 2:
                self.body("gpsimd", lambda eng, hp=hp: fwl(eng, hp + 1, 16 * ((hp + 1) % 2)))
                cG_hp[hp + 1] = self.cG.n
            def fvl(eng, hp=hp):
                self.dS(eng, self.vpair[:, :, 0:64],
                        self.v_scr[:, hp * 128:hp * 128 + 64].rearrange("(k p) v -> p k v", p=128))
                self.dS(eng, self.vpair[:, :, 65:129],
                        self.v_scr[:, hp * 128 + 64:hp * 128 + 128].rearrange("(k p) v -> p k v", p=128))
            self.body("sync", fvl)
            cG_h = cG_hp[hp]
            cS_h = self.cS.n
            snapH = self.snap()
            kev = {}
            for qt in range(NQT):
                bk = 4 if qt % 2 == 0 else 5
                if qt < 2:
                    wts = [(self.cG, cG_h), (self.cA, self._at["fa2"]), (self.cV, self._at["fv"])]
                else:
                    wts = [(self.cA, kev[qt - 2])]
                def ftk(eng, qt=qt, bk=bk, ro=ro):
                    last = None
                    for dc in range(NC8):
                        last = eng.matmul(self.bank(bk), self.wbufB[:, ro + 8 + dc, :],
                                          self.xn[:, dc, qt * 512:(qt + 1) * 512],
                                          start=(dc == 0), stop=(dc == NC8 - 1))
                    return last
                self.body("tensor", ftk, waits=wts)
                tK = self.cT.n
                self.body("scalar", lambda eng, qt=qt, bk=bk: eng.copy(
                    out=self.kpair[:, qt * 512:(qt + 1) * 512], in_=self.bank(bk)),
                    waits=[(self.cT, tK), (self.cA, self.cA.n)])
                kev[qt] = self.cA.n
            for qt in range(NQT):
                nkk = 4 * (qt + 1)
                def ftq(eng, qt=qt, ro=ro):
                    last = None
                    for dc in range(NC8):
                        last = eng.matmul(self.bank(6), self.wbufB[:, ro + dc, :],
                                          self.xn[:, dc, qt * 512:(qt + 1) * 512],
                                          start=(dc == 0), stop=(dc == NC8 - 1))
                    return last
                self.body("tensor", ftq,
                          waits=[(self.cG, cG_h), (self.cA, self._at["fa2"]), (self.cA, kev[qt])])
                tQ = self.cT.n
                self.body("scalar", lambda eng: eng.mul(out=self.qsmall[:], in_=self.bank(6), mul=0.125),
                          waits=[(self.cT, tQ), (self.cA, self.cA.n)])
                self._at["qev"] = self.cA.n
                self._at["cS_h"] = cS_h
                if L == 0:
                    self.attn_l0_tile(hp, qt, nkk)
                else:
                    self.attn_l1_tile(hp, qt, nkk)
                if drains:
                    self.drain(drains, 10 if L == 0 else 2)

    def _emit_sgroup(self, eng, g, qt, base=0):
        last = None
        for j in range(2):
            kc = 2 * g + j
            last = eng.matmul(self.bank(base + j), self.kpair[0:64, kc * 128:(kc + 1) * 128],
                              self.qsmall[0:64, :], start=True, stop=True)
            last = eng.matmul(self.bank(base + 2 + j), self.kpair[64:128, kc * 128:(kc + 1) * 128],
                              self.qsmall[64:128, :], start=True, stop=True)
        return last

    def _group_has_mask(self, g, qt):
        return any(0 <= (2 * g + j) * 128 - qt * 512 < 512 for j in range(2))

    def _emit_masks(self, eng, g, qt, base=0):
        last = None
        for j in range(2):
            kc = 2 * g + j
            o = kc * 128 - qt * 512
            if 0 <= o < 512:
                for hb in (base + j, base + 2 + j):
                    last = eng.tensor_tensor(out=self.bank(hb), in0=self.bank(hb),
                                             in1=self.mskt[:, o // 128, :], op=ALU.add)
        return last

    def attn_l0_tile(self, hp, qt, nkk):
        ng = nkk // 2
        snap0 = self.snap()
        tS = []
        base_V = self.cV.n
        base_A = self.cA.n
        hasm = [self._group_has_mask(g, qt) for g in range(ng)]
        masked = [g for g in range(ng) if hasm[g]]
        vM = {}
        for i, g in enumerate(masked):
            vM[g] = base_V + i + 1
        aE = [base_A + g + 1 for g in range(ng)]

        ftw = [(self.cA, self._at["qev"]), (self.cS, self._at["cS_h"]),
               (self.cV, self._at["fv"])]

        def emit_av(eng, g):
            last = None
            sl = 1024 * (g % 2)
            for j in range(2):
                kc = 2 * g + j
                first = (kc == 0)
                lastk = (kc == nkk - 1)
                eng.matmul(self.bank(4)[0:65, :], self.vpair[:, kc, 0:65],
                           self.ptA_g[:, sl + j * 512:sl + (j + 1) * 512],
                           start=first, stop=lastk, tile_position=(0, 0))
                last = eng.matmul(self.bank(5)[0:65, :], self.vpair[:, kc, 65:130],
                                  self.ptB_g[:, sl + j * 512:sl + (j + 1) * 512],
                                  start=first, stop=lastk, tile_position=(0, 0))
            return last

        def ft(eng):
            last = self._emit_sgroup(eng, 0, qt)
            self.cT.inc(last)
            tS.append(self.cT.n)
            for g in range(1, ng):
                eng.wait_ge(self.cA.sem, aE[g - 1])
                last = self._emit_sgroup(eng, g, qt)
                self.cT.inc(last)
                tS.append(self.cT.n)
                last = emit_av(eng, g - 1)
                if g < ng - 1:
                    self.cT.inc(last)
            eng.wait_ge(self.cA.sem, aE[ng - 1])
            return emit_av(eng, ng - 1)
        self.body("tensor", ft, waits=ftw)

        def fv(eng):
            last = None
            for g in masked:
                eng.wait_ge(self.cT.sem, tS[g])
                last = self._emit_masks(eng, g, qt)
                if g != masked[-1]:
                    self.cV.inc(last)
            return last if last is not None else eng.nop()
        self.body("vector", fv, waits=snap0)
        assert self.cV.n == base_V + max(len(masked), 1), (self.cV.n, base_V, masked)

        def fa(eng):
            last = None
            for g in range(ng):
                eng.wait_ge(self.cT.sem, tS[g])
                if hasm[g]:
                    eng.wait_ge(self.cV.sem, vM[g])
                sl = 1024 * (g % 2)
                eng.activation(out=self.ptA_g[:, sl:sl + 1024], in_=self.psum[:, 0:1024],
                               func=AF.Exp, bias=self.consts[:, 0:1], scale=1.0)
                last = eng.activation(out=self.ptB_g[:, sl:sl + 1024], in_=self.psum[:, 1024:2048],
                                      func=AF.Exp, bias=self.consts[:, 0:1], scale=1.0)
                if g < ng - 1:
                    self.cA.inc(last)
            return last
        self.body("scalar", fa, waits=snap0)
        assert self.cA.n == aE[-1], (self.cA.n, aE)

        def fa2(eng):
            eng.copy(out=self.dAB[0:1, 0:512], in_=self.bank(4)[64:65, :])
            return eng.copy(out=self.dAB[0:1, 512:1024], in_=self.bank(5)[64:65, :])
        self.body("scalar", fa2)
        self._at["fa2"] = self.cA.n
        def ft2(eng):
            eng.matmul(self.bank(7), self.eA[:], self.dAB[0:1, 0:512], start=True, stop=False)
            return eng.matmul(self.bank(7), self.eB[:], self.dAB[0:1, 512:1024], start=False, stop=True)
        self.body("tensor", ft2)
        def fa3(eng):
            ins = eng.activation(out=self.scrA[:], in_=self.bank(7), func=AF.Ln,
                                 bias=self.consts[:, 1:2], scale=1.0)
            self.ae(eng, ins)
            return eng.activation(out=self.scrB[:], in_=self.scrA[:], func=AF.Exp,
                                  bias=self.consts[:, 1:2], scale=-1.0)
        self.body("scalar", fa3)
        def fvE(eng):
            eng.tensor_tensor(
                out=self.attn_fm[0:64, hp, qt * 512:(qt + 1) * 512],
                in0=self.bank(4)[0:64, :], in1=self.scrB[0:64, :], op=ALU.mult)
            return eng.tensor_tensor(
                out=self.attn_fm[64:128, hp, qt * 512:(qt + 1) * 512],
                in0=self.bank(5)[0:64, :], in1=self.scrB[64:128, :], op=ALU.mult)
        self.body("vector", fvE)
        self._at["fv"] = self.cV.n

    def attn_l1_tile(self, hp, qt, nkk):
        ng = nkk // 2
        snap0 = self.snap()
        tS = []
        base_V = self.cV.n
        base_A = self.cA.n
        hasm = [self._group_has_mask(g, qt) for g in range(ng)]
        masked = [g for g in range(ng) if hasm[g]]
        vM = {}
        for i, g in enumerate(masked):
            vM[g] = base_V + i + 1
        aE = [base_A + g + 1 for g in range(ng)]
        bb = lambda g: 4 * (g % 2)

        ftw = [(self.cA, self._at["qev"]), (self.cS, self._at["cS_h"])]

        def ft(eng):
            last = None
            for g in range(ng):
                if g >= 2:
                    eng.wait_ge(self.cA.sem, aE[g - 2])
                last = self._emit_sgroup(eng, g, qt, base=bb(g))
                if g < ng - 1:
                    self.cT.inc(last)
                tS.append(self.cT.n + (1 if g == ng - 1 else 0))
            return last
        self.body("tensor", ft, waits=ftw)

        def fv(eng):
            last = None
            for g in masked:
                eng.wait_ge(self.cT.sem, tS[g])
                last = self._emit_masks(eng, g, qt, base=bb(g))
                if g != masked[-1]:
                    self.cV.inc(last)
            return last if last is not None else eng.nop()
        self.body("vector", fv, waits=snap0)

        def fa(eng):
            last = None
            for g in range(ng):
                eng.wait_ge(self.cT.sem, tS[g])
                if hasm[g]:
                    eng.wait_ge(self.cV.sem, vM[g])
                eng.activation(out=self.ptA_f[:, 2 * g:2 * g + 2, :].rearrange("p a b -> p (a b)"),
                               in_=self.psum[:, bb(g) * 512:(bb(g) + 2) * 512],
                               func=AF.Exp, bias=self.consts[:, 0:1], scale=1.0)
                last = eng.activation(out=self.ptB_f[:, 2 * g:2 * g + 2, :].rearrange("p a b -> p (a b)"),
                                      in_=self.psum[:, (bb(g) + 2) * 512:(bb(g) + 4) * 512],
                                      func=AF.Exp, bias=self.consts[:, 0:1], scale=1.0)
                if g < ng - 1:
                    self.cA.inc(last)
            return last
        self.body("scalar", fa, waits=snap0)

        expL = self.cA.n
        expL = self.cA.n
        for qb in range(4):
            wts = [(self.cA, expL), (self.cA, self._at["fa2"]),
                   (self.cV, self._at["fv"]), (self.cS, self._at["cS_h"])]
            def ftav(eng, qb=qb):
                last = None
                for pt, b, lo in ((self.ptA_f, 4, 0), (self.ptB_f, 5, 65)):
                    for kc in range(nkk):
                        last = eng.matmul(self.bank(b, 0, 65),
                                          pt[:, kc, qb * 128:(qb + 1) * 128],
                                          self.vpair[:, kc, lo:lo + 65],
                                          start=(kc == 0), stop=(kc == nkk - 1))
                return last
            self.body("tensor", ftav, waits=wts)
            def fa2(eng):
                ins = eng.activation(out=self.qsf[:, 0, 0:1], in_=self.bank(4, 64, 65),
                                     func=AF.Ln, bias=self.consts[:, 1:2], scale=1.0)
                self.ae(eng, ins)
                ins = eng.activation(out=self.qsf[:, 0, 1:2], in_=self.qsf[:, 0, 0:1],
                                     func=AF.Exp, bias=self.consts[:, 1:2], scale=-1.0)
                self.ae(eng, ins)
                ins = eng.activation(out=self.qsf[:, 0, 2:3], in_=self.bank(5, 64, 65),
                                     func=AF.Ln, bias=self.consts[:, 1:2], scale=1.0)
                self.ae(eng, ins)
                return eng.activation(out=self.qsf[:, 0, 3:4], in_=self.qsf[:, 0, 2:3],
                                      func=AF.Exp, bias=self.consts[:, 1:2], scale=-1.0)
            self.body("scalar", fa2)
            def fv2(eng):
                eng.tensor_scalar(out=self.tstage[:, 0, :], in0=self.bank(4, 0, 64),
                                  scalar1=self.qsf[:, 0, 1:2], scalar2=None, op0=ALU.mult)
                return eng.tensor_scalar(out=self.tstage[:, 1, :], in0=self.bank(5, 0, 64),
                                         scalar1=self.qsf[:, 0, 3:4], scalar2=None, op0=ALU.mult)
            self.body("vector", fv2)
            self._at["fv"] = self.cV.n
            self._at["fa2"] = self.cA.n
            sbase = qt * 512 + qb * 128
            self.body("sync", lambda eng, hp=hp, sbase=sbase: self.dS(
                eng, self.t_scr[2 * hp:2 * hp + 2, sbase:sbase + 128, :].rearrange("n (j p) h -> p j n h", p=128),
                self.tstage[:].unsqueeze(1)) and None)

    def ph_l0_oproj(self):
        NQT = self.NQT
        self.body("gpsimd", lambda eng: self.dG(
            eng, self.wbufA[:], self.woT.rearrange("(c p) d -> p c d", p=128)) and None)
        for qt in range(NQT):
            for db in range(NC8):
                b = (qt * NC8 + db) % 2
                def ft(eng, qt=qt, db=db, b=b):
                    last = None
                    for vc in range(NC8):
                        last = eng.matmul(self.bank(b), self.wbufA[:, vc, db * 128:(db + 1) * 128],
                                          self.attn_fm[:, vc, qt * 512:(qt + 1) * 512],
                                          start=(vc == 0), stop=(vc == NC8 - 1))
                    return last
                self.body("tensor", ft)
                self.body("vector", lambda eng, qt=qt, db=db, b=b: eng.tensor_tensor(
                    out=self.x[:, db, qt * 512:(qt + 1) * 512],
                    in0=self.x[:, db, qt * 512:(qt + 1) * 512],
                    in1=self.bank(b), op=ALU.add))

    def ph_l0_ffn(self):
        NQT = self.NQT
        bw1 = self.bw1T_scr.rearrange("(c p) f -> p c f", p=128)
        bw2 = self.bw2T_scr.rearrange("(c p) f -> p c f", p=128)
        w1v = self.wbufA[:].rearrange("p c v -> p (c v)")[:, 0:NC8 * 512].rearrange("p (c f) -> p c f", f=512)
        for qt in range(NQT):
            snapF = self.snap()
            tG = {}
            aG = {}
            tT1 = {}
            for hg in range(8):
                cw = self.cW[hg % 2]
                wts = list(snapF) if hg < 2 else [(self.cT, tT1[(hg - 2) * 4 + 3])]
                def fg(eng, hg=hg, cw=cw):
                    cw.inc(eng.dma_start(
                        out=self.wbufA[:, :, 512 * (hg % 2):512 * (hg % 2) + 512],
                        in_=bw1[:, :, hg * 512:(hg + 1) * 512]), 16)
                self.body("sync", fg, waits=wts)
                wG = cw.n
                for hj in range(4):
                    hc = hg * 4 + hj
                    b = hc % 2
                    wts = [(cw, wG)]
                    if hc >= 2:
                        wts.append((self.cA, aG[hc - 2]))
                    elif hg < 1:
                        wts += list(snapF)
                    def ft(eng, hj=hj, b=b, qt=qt, hg=hg):
                        last = None
                        for dc in range(NC8):
                            last = eng.matmul(
                                self.bank(b),
                                self.wbufA[:, dc, 512 * (hg % 2) + hj * 128:512 * (hg % 2) + (hj + 1) * 128],
                                self.xn[:, dc, qt * 512:(qt + 1) * 512],
                                start=(dc == 0), stop=(dc == NC8 - 1))
                        return last
                    self.body("tensor", ft, waits=wts)
                    tT1[hc] = self.cT.n
                    self.body("scalar", lambda eng, hc=hc, b=b: eng.activation(
                        out=self.h_l0[:, hc, :], in_=self.bank(b), func=AF.Gelu,
                        bias=self.b1_sb[:, hc:hc + 1], scale=1.0),
                        waits=[(self.cT, tT1[hc])] + ([(self.cA, self.cA.n)] if hc == 0 else []))
                    aG[hc] = self.cA.n
            tT2 = {}
            vU = {}
            aAll = self.cA.n
            for db in range(NC8):
                for half in range(2):
                    cw = self.cW[half]
                    wts = [(self.cT, tT2.get((db - 1, half), 0))] if db >= 1 else [(self.cT, self.cT.n)]
                    def fg2(eng, db=db, half=half, cw=cw):
                        cw.inc(eng.dma_start(
                            out=self.wbufB[:, 16 * half:16 * half + 16, :],
                            in_=bw2[:, 16 * half:16 * half + 16, db * 128:(db + 1) * 128]), 16)
                    self.body("sync", fg2, waits=wts)
                    tG[(db, half)] = cw.n
                b = 2 + db % 2
                wts = [(self.cW[0], tG[(db, 0)]), (self.cW[1], tG[(db, 1)]), (self.cA, aAll)]
                if db >= 2:
                    wts.append((self.cV, vU[db - 2]))
                def ft2(eng, db=db, b=b):
                    last = None
                    for hc in range(32):
                        last = eng.matmul(self.bank(b), self.wbufB[:, hc, :], self.h_l0[:, hc, :],
                                          start=(hc == 0), stop=(hc == 31))
                        if hc == 15:
                            pass
                    return last
                self.body("tensor", ft2, waits=wts)
                tT2[(db, 0)] = self.cT.n
                tT2[(db, 1)] = self.cT.n
                self.body("vector", lambda eng, qt=qt, db=db, b=b: eng.scalar_tensor_tensor(
                    out=self.x[:, db, qt * 512:(qt + 1) * 512], in0=self.bank(b),
                    scalar=self.b2_sb[:, db:db + 1], in1=self.x[:, db, qt * 512:(qt + 1) * 512],
                    op0=ALU.add, op1=ALU.add), waits=[(self.cT, tT2[(db, 0)]), (self.cV, self.cV.n)])
                vU[db] = self.cV.n

    def ph_l1_scatter(self):
        S = self.S
        for dc in range(NC8):
            G = S // 64
            def fy(eng, dc=dc, G=G):
                self.dS(eng, self.y_sb[0:64, 0:S].rearrange("p (c h) -> p c h", h=64),
                        self.t_scr[2 * dc].rearrange("(a c) h -> a c h", c=G))
                self.dS(eng, self.y_sb[64:128, 0:S].rearrange("p (c h) -> p c h", h=64),
                        self.t_scr[2 * dc + 1].rearrange("(a c) h -> a c h", c=G))
            self.body("sync", fy)
            self.body("vector", lambda eng, dc=dc: eng.scalar_tensor_tensor(
                out=self.x[:, dc, :], in0=self.y_sb[:, 0:S],
                scalar=self.wsum_c[:, dc:dc + 1], in1=self.x[:, dc, :],
                op0=ALU.mult, op1=ALU.add))

    def emit_quant(self, in_aps, out_ap, W, twist, waits=None):
        nb = W // 16
        q0 = self.qsc[:, 0, 0:W]
        extra_w = getattr(self, "_prep_quant_wait", None)
        self._prep_quant_wait = None
        if waits is None:
            waits = self.snap()
        fv1_waits = list(waits) + [(self.cV, self.cV.n), (self.cA, self.cA.n)]

        def fv1(eng):
            if extra_w is not None:
                eng.wait_ge(extra_w[0].sem, extra_w[1])
            if in_aps is not None:
                o = 0
                for ap in in_aps:
                    w = ap.shape[-1]
                    if twist:
                        ins = eng.transpose(out=self.qsc[:, 0, o:o + w], in_=ap)
                    else:
                        ins = eng.tensor_copy(out=self.qsc[:, 0, o:o + w], in_=ap)
                    o += w
                    self.ve(eng, ins)
            ins = eng.tensor_reduce(out=self.qsf[:, 0, 0:nb],
                                    in_=q0.rearrange("p (g s) -> p g s", s=16),
                                    axis=AX.X, op=ALU.max, apply_absolute_value=True)
            self.vq(eng, ins)
            return eng.tensor_scalar(out=self.qsf[:, 1, 0:nb], in0=self.qsf[:, 0, 0:nb],
                                     scalar1=1.0 / 6.0, scalar2=1e-12, op0=ALU.mult, op1=ALU.max)
        self.body("vector", fv1, waits=fv1_waits)
        cV1 = self.cV.n

        def fa1(eng):
            ins = eng.activation(out=self.qsf[:, 2, 0:nb], in_=self.qsf[:, 1, 0:nb],
                                 func=AF.Ln, bias=self.consts[:, 1:2], scale=1.0)
            self.ae(eng, ins)
            eng.activation(out=self.qsf[:, 3, 0:nb], in_=self.qsf[:, 2, 0:nb],
                           func=AF.Exp, bias=self.consts[:, 1:2], scale=-1.0)
            eng.activation(out=self.qsc[:, 1, 0:W], in_=q0,
                           func=AF.Abs, bias=self.consts[:, 1:2], scale=1.0)
            return eng.activation(out=self.qsc[:, 3, 0:W], in_=q0,
                                  func=AF.Sign, bias=self.consts[:, 1:2], scale=1.0)
        self.body("scalar", fa1, waits=[(self.cV, cV1), (self.cA, self.cA.n)])
        cA1 = self.cA.n

        def fv2(eng):
            a = self.qsc[:, 1, 0:W]
            st = self.qsc[:, 2, 0:W]
            a3 = lambda t: t.rearrange("p (g s) -> p g s", s=16)
            rsc = self.qsf[:, 3, 0:nb].unsqueeze(2).broadcast_to((128, nb, 16))
            ins = eng.tensor_tensor(out=a3(a), in0=a3(a), in1=rsc, op=ALU.mult)
            self.vq(eng, ins)
            ins = eng.tensor_scalar(out=st, in0=a, scalar1=2.5, scalar2=None, op0=ALU.is_gt)
            self.vq(eng, ins)
            for th in (3.5, 5.0, 5.0):
                ins = eng.scalar_tensor_tensor(out=st, in0=a, scalar=th, in1=st,
                                               op0=ALU.is_gt, op1=ALU.add)
                self.vq(eng, ins)
            ins = eng.tensor_scalar(out=st, in0=st, scalar1=2.0, scalar2=None, op0=ALU.mult)
            self.vq(eng, ins)
            for th in (0.25, 0.75, 1.25, 1.75):
                ins = eng.scalar_tensor_tensor(out=st, in0=a, scalar=th, in1=st,
                                               op0=ALU.is_gt, op1=ALU.add)
                self.vq(eng, ins)
            ins = eng.tensor_scalar(out=self.qsf[:, 2, 0:nb], in0=self.qsf[:, 1, 0:nb],
                                    scalar1=0.5, scalar2=None, op0=ALU.mult)
            self.vq(eng, ins)
            hsc = self.qsf[:, 2, 0:nb].unsqueeze(2).broadcast_to((128, nb, 16))
            ins = eng.tensor_tensor(out=a3(st), in0=a3(st), in1=hsc, op=ALU.mult)
            self.vq(eng, ins)
            ins = eng.tensor_tensor(out=a, in0=st, in1=self.qsc[:, 3, 0:W], op=ALU.mult)
            self.ve(eng, ins)
            if twist:
                ins = eng.transpose(out=out_ap, in_=a)
                self.ve(eng, ins)
                return eng.nop()
            if out_ap is None:
                return eng.nop()
            return eng.tensor_copy(out=out_ap, in_=a)
        self.body("vector", fv2, waits=[(self.cA, cA1)])

    def ph_l1_ffn(self, xq0_drained=False):
        NQT = self.NQT
        wgT = self.wgT_scr.rearrange("(c p) f -> p c f", p=128)
        wuT = self.wuT_scr.rearrange("(c p) f -> p c f", p=128)
        wdT = self.wdT_scr.rearrange("(c p) f -> p c f", p=128)
        xqp = lambda q, pp: self.wbufA[:, 4 * (q % 2) + pp, :]
        xqc = lambda q, dc: self.wbufA[:, 4 * (q % 2) + dc // 2,
                                       512 * (dc % 2):512 * (dc % 2) + 512]
        tD_prev = None
        for qt in range(NQT):
            qts = slice(qt * 512, (qt + 1) * 512)
            if qt == 0 and not xq0_drained:
                for pp in range(NC8 // 2):
                    self.emit_quant([self.xn[:, 2 * pp, qts], self.xn[:, 2 * pp + 1, qts]],
                                    xqp(qt, pp), 1024, twist=True, waits=[])
            tT, tA, tV, tS = {}, {}, {}, {}
            snapA = self.snap()
            snapA_cV = self.cV.n
            for ic in range(NI):
                sl = 16 * (ic % 2)
                bg, bu = (0, 1) if ic % 2 == 0 else (2, 3)
                if ic >= 2:
                    wts = [(self.cT, tT[ic - 2])]
                elif tD_prev is not None:
                    wts = [(self.cT, tD_prev)]
                else:
                    wts = list(snapA)
                cw = self.cW[ic % 2]
                def fwg(eng, ic=ic, sl=sl, cw=cw):
                    cw.inc(eng.dma_start(out=self.wbufB[:, sl:sl + 8, :],
                                         in_=wgT[:, :, ic * 128:(ic + 1) * 128]), 16)
                    cw.inc(eng.dma_start(out=self.wbufB[:, sl + 8:sl + 16, :],
                                         in_=wuT[:, :, ic * 128:(ic + 1) * 128]), 16)
                self.body("sync", fwg, waits=wts)
                tS[ic] = cw.n
                wts = [(cw, tS[ic])]
                if ic >= 2:
                    wts.append((self.cV, tV[ic - 2]))
                else:
                    wts.append((self.cV, snapA_cV))
                def ft(eng, ic=ic, sl=sl, bg=bg, bu=bu, qt=qt):
                    for dc in range(NC8):
                        eng.matmul(self.bank(bg), self.wbufB[:, sl + dc, :], xqc(qt, dc),
                                   start=(dc == 0), stop=(dc == NC8 - 1))
                    last = None
                    for dc in range(NC8):
                        last = eng.matmul(self.bank(bu), self.wbufB[:, sl + 8 + dc, :], xqc(qt, dc),
                                          start=(dc == 0), stop=(dc == NC8 - 1))
                    return last
                self.body("tensor", ft, waits=wts)
                tT[ic] = self.cT.n
                wts = [(self.cT, tT[ic])]
                if ic >= 1:
                    wts.append((self.cV, tV[ic - 1]))
                self.body("scalar", lambda eng, bg=bg: eng.activation(
                    out=self.scrA[:], in_=self.bank(bg), func=AF.Silu,
                    bias=self.consts[:, 1:2], scale=1.0), waits=wts)
                tA[ic] = self.cA.n
                self.body("vector", lambda eng, ic=ic, bu=bu: eng.tensor_tensor(
                    out=self.hq[:, ic, :], in0=self.scrA[:], in1=self.bank(bu), op=ALU.mult),
                    waits=[(self.cA, tA[ic])])
                tV[ic] = self.cV.n
                if ic % 2 == 1:
                    pp = ic // 2
                    self.emit_quant([self.hq[:, 2 * pp:2 * pp + 2, :].rearrange("p a b -> p (a b)")],
                                    self.hq[:, 2 * pp:2 * pp + 2, :].rearrange("p a b -> p (a b)"),
                                    1024, twist=True, waits=[])
            assert NI % 2 == 0
            vq_last = self.cV.n
            if qt + 1 < NQT:
                nqts = slice((qt + 1) * 512, (qt + 2) * 512)
                for pp in range(NC8 // 2):
                    self.emit_quant([self.xn[:, 2 * pp, nqts], self.xn[:, 2 * pp + 1, nqts]],
                                    xqp(qt + 1, pp), 1024, twist=True,
                                    waits=[(self.cT, tT[NI - 1])])
            tD, vU, wD = {}, {}, {}
            for db in range(NC8):
                cw = self.cW[db % 2]
                wts = [(self.cT, tD[db - 1])] if db >= 1 else [(self.cT, tT[NI - 1])]
                def fds(eng, db=db, cw=cw):
                    cw.inc(eng.dma_start(out=self.wbufB[:, 0:NI, :],
                                         in_=wdT[:, :, db * 128:(db + 1) * 128]), 16)
                self.body("sync", fds, waits=wts)
                wD[db] = cw.n
                b = 2 + db % 2
                if db < 2:
                    wts = [(cw, wD[db]), (self.cV, vq_last)]
                    if db == 0:
                        wts.append((self.cA, tA[NI - 1]))
                else:
                    wts = [(cw, wD[db]), (self.cV, vU[db - 2])]
                def ft2(eng, b=b):
                    last = None
                    for ic in range(NI):
                        last = eng.matmul(self.bank(b), self.wbufB[:, ic, :], self.hq[:, ic, :],
                                          start=(ic == 0), stop=(ic == NI - 1))
                    return last
                self.body("tensor", ft2, waits=wts)
                tD[db] = self.cT.n
                self.body("vector", lambda eng, qt=qt, db=db, b=b: eng.tensor_tensor(
                    out=self.x[:, db, qt * 512:(qt + 1) * 512],
                    in0=self.x[:, db, qt * 512:(qt + 1) * 512],
                    in1=self.bank(b), op=ALU.add), waits=[(self.cT, tD[db])])
                vU[db] = self.cV.n
            tD_prev = tD[NC8 - 1]

    def ph_store_x(self, s):
        self.body("sync", lambda eng, s=s: self.dS(
            eng, self.out[s].rearrange("(c p) q -> p c q", p=128), self.x[:]) and None)

    def ph_final(self):
        snap = self.snap()
        eng = Rec(self.streams["sync"])
        for c, n in snap:
            if n > 0:
                eng.wait_ge(c.sem, n)


def build_masks():
    m = np.zeros((4, 128, 512), dtype=np.float32)
    for oi, o in enumerate((0, 128, 256, 384)):
        i = np.arange(128)[:, None]
        j = np.arange(512)[None, :]
        m[oi] = np.where(i + o <= j, 0.0, NEG)
    return m.astype(ml_dtypes.bfloat16)


def host_inputs(inputs, cfg, core):
    nseq = cfg["nseq"]
    x = np.asarray(inputs["L0_attn_x"], dtype=np.float32)
    xs = x[core * nseq:(core + 1) * nseq]
    x_t = np.ascontiguousarray(xs.transpose(0, 2, 1))
    f32 = lambda a: np.ascontiguousarray(np.asarray(a, dtype=np.float32))
    tr = lambda a: np.ascontiguousarray(np.asarray(a, dtype=np.float32).T)
    return {
        "x_t": x_t,
        "msk": build_masks(),
        "L0_wqT": tr(inputs["L0_attn_q_proj_weight"]),
        "L0_wkT": tr(inputs["L0_attn_k_proj_weight"]),
        "L0_wvT": tr(inputs["L0_attn_v_proj_weight"]),
        "L0_woT": tr(inputs["L0_attn_o_proj_weight"]),
        "L0_w1": f32(inputs["L0_ffn_w1"]),
        "L0_w2": f32(inputs["L0_ffn_w2"]),
        "L0_b1h": np.ascontiguousarray(f32(inputs["L0_ffn_b1"]).reshape(32, 128).T),
        "L0_b2h": np.ascontiguousarray(f32(inputs["L0_ffn_b2"]).reshape(8, 128).T),
        "L1_wqkv": f32(inputs["L1_attn_W_qkv"]),
        "L1_wout": f32(inputs["L1_attn_W_out"]),
        "L1_wg": f32(inputs["L1_ffn_w_gate"]),
        "L1_wu": f32(inputs["L1_ffn_w_up"]),
        "L1_wd": f32(inputs["L1_ffn_w_down"]),
    }


def kernel(**inputs):
    cfg = default_cfg()
    p = Prog(cfg)
    nc = p.build()
    in_maps = [host_inputs(inputs, cfg, core) for core in range(8)]
    res = run_bass_kernel_spmd(nc, in_maps, core_ids=list(range(8)))
    outs = []
    for core in range(8):
        o = np.asarray(res.results[core]["out_t"], dtype=np.float32)
        outs.append(o.transpose(0, 2, 1))
    return np.ascontiguousarray(np.concatenate(outs, axis=0))
